# version 03
# baseline (speedup 1.0000x reference)
import numpy as np
import ml_dtypes
from contextlib import ExitStack

import concourse.bass as bass
import concourse.mybir as mybir
from concourse.bass_utils import run_bass_kernel_spmd

F32 = mybir.dt.float32
BF16 = mybir.dt.bfloat16
AF = mybir.ActivationFunctionType
ALU = mybir.AluOpType

T = 2048
CT = 32
H = 32
NCH = 16
THALF = 1024
NSTRIP1 = 21
MNEG_VAL = -1e5
KB = 1024

_DUMPS = ()
_PHASES = 99


class Prog:
    def __init__(self):
        self.q = {k: [] for k in ("te", "ve", "ac", "gp", "sy")}
        self.c = {}
        self.nbar = 0
        self.sems = None

    def inc(self, sem, amt=1):
        self.c[sem] = self.c.get(sem, 0) + amt
        return self.c[sem]

    def val(self, sem):
        return self.c.get(sem, 0)

    def add(self, eng, emit, waits=(), inc=None):
        waits = tuple((s, v) for (s, v) in waits if v > 0)

        def fn(e, emit=emit, waits=waits, inc=inc):
            for (s, v) in waits:
                e.wait_ge(self.sems[s], v)
            ins = emit(e)
            if inc is not None:
                s, amt = inc
                if amt is None:
                    ins.then_inc(self.sems[s])
                else:
                    ins.then_inc(self.sems[s], amt)

        fn.meta = (eng, waits, inc)
        self.q[eng].append(fn)

    def barrier(self):
        self.nbar += 1
        tgt = 5 * self.nbar
        for eng in ("te", "ve", "ac", "gp", "sy"):
            def fn(e, tgt=tgt):
                e.sem_inc(self.sems["sB"], 1)
                e.wait_ge(self.sems["sB"], tgt)
            fn.meta = (eng, (("sB", tgt),), ("sB", 1))
            fn.is_barrier = True
            self.q[eng].append(fn)


def build_nc(dumps=(), phases=99):
    nc = bass.Bass("TRN2", target_bir_lowering=False)
    p = Prog()

    xF = nc.dram_tensor("xF", [4096, T], F32, kind="ExternalInput").ap()
    xT = nc.dram_tensor("xT", [T, 4096], F32, kind="ExternalInput").ap()
    wtiles = nc.dram_tensor("wtiles", [NSTRIP1, 128, CT, 512], F32, kind="ExternalInput").ap()
    wotiles = nc.dram_tensor("wotiles", [16, 128, CT, 256], F32, kind="ExternalInput").ap()
    kerntiles = nc.dram_tensor("kerntiles", [16, 128, CT, 256], F32, kind="ExternalInput").ap()
    loraAT = nc.dram_tensor("loraAT", [16, CT, 128, 32], F32, kind="ExternalInput").ap()
    loraBT = nc.dram_tensor("loraBT", [16, 32, 4096], F32, kind="ExternalInput").ap()
    lorasAT = nc.dram_tensor("lorasAT", [CT, 128, 32], F32, kind="ExternalInput").ap()
    lorasBT = nc.dram_tensor("lorasBT", [32, 4096], F32, kind="ExternalInput").ap()
    cosrep_d = nc.dram_tensor("cosrep", [T, 512], BF16, kind="ExternalInput").ap()
    sinSrep_d = nc.dram_tensor("sinSrep", [T, 512], BF16, kind="ExternalInput").ap()
    identf_d = nc.dram_tensor("identf", [128, 128], F32, kind="ExternalInput").ap()
    mnegf_d = nc.dram_tensor("mnegf", [128, 128], F32, kind="ExternalInput").ap()
    utf_d = nc.dram_tensor("utf", [128, 128], F32, kind="ExternalInput").ap()
    dtbias_d = nc.dram_tensor("dtbias_row", [1, 512], F32, kind="ExternalInput").ap()
    alog_d = nc.dram_tensor("alog_row", [1, 32], F32, kind="ExternalInput").ap()
    ffnbias_d = nc.dram_tensor("ffnbias_row", [1, 4096], F32, kind="ExternalInput").ap()
    gate_d = nc.dram_tensor("gate_row", [1, 16], F32, kind="ExternalInput").ap()
    gnorm_d = nc.dram_tensor("gnorm_col", [128, 1], F32, kind="ExternalInput").ap()
    amask_d = nc.dram_tensor("amask", [128, 3], F32, kind="ExternalInput").ap()
    bmask_d = nc.dram_tensor("bmask_bc", [128, 96], F32, kind="ExternalInput").ap()

    y_out = nc.dram_tensor("y_out", [T, 4096], F32, kind="ExternalOutput").ap()

    q_sp = nc.dram_tensor("q_sp", [T, 4096], BF16, kind="Internal").ap()
    k_sp = nc.dram_tensor("k_sp", [T, 1024], BF16, kind="Internal").ap()
    v_sp = nc.dram_tensor("v_sp", [T, 1024], BF16, kind="Internal").ap()
    g_sp = nc.dram_tensor("g_sp", [T, 4096], BF16, kind="Internal").ap()
    y_sp = nc.dram_tensor("y_sp", [T, 4096], BF16, kind="Internal").ap()
    x1_sp = nc.dram_tensor("x1_sp", [T, 4096], BF16, kind="Internal").ap()
    gtab_sp = nc.dram_tensor("gtab_sp", [8, 128, 512], F32, kind="Internal").ap()
    lndbg = nc.dram_tensor("lndbg", [16, 128, 8], F32, kind="Internal").ap()
    zdbg = nc.dram_tensor("zdbg", [128, 4096], F32, kind="Internal").ap()
    prevs_sp = nc.dram_tensor("prevs_sp", [H, T, 128], BF16, kind="Internal").ap()
    agin = nc.dram_tensor("agin", [4224, 128], F32, kind="Internal").ap()
    agout = nc.dram_tensor("agout", [16896, 128], F32, kind="Internal").ap()
    scr = dict(q_sp=q_sp, k_sp=k_sp, v_sp=v_sp, g_sp=g_sp, y_sp=y_sp,
               x1_sp=x1_sp, gtab_sp=gtab_sp, agin=agin, agout=agout,
               lndbg=lndbg, zdbg=zdbg)
    dbg = {}
    for name in dumps:
        t = scr[name]
        dbg[name] = nc.dram_tensor("dbg_" + name, list(t.shape), t.dtype,
                                   kind="ExternalOutput").ap()

    identf = nc.alloc_sbuf_tensor("identf_b", [128, 128], F32)
    mnegf = nc.alloc_sbuf_tensor("mnegf_b", [128, 128], F32)
    utf = nc.alloc_sbuf_tensor("utf_b", [128, 128], F32)
    onesf = nc.alloc_sbuf_tensor("onesf", [128, 128], F32)
    onesrow = nc.alloc_sbuf_tensor("onesrow", [1, 128], F32)
    onesrow_b = nc.alloc_sbuf_tensor("onesrow_b", [1, 128], BF16)
    dtbias = nc.alloc_sbuf_tensor("dtbias", [1, 512], F32)
    dtbias_bf = nc.alloc_sbuf_tensor("dtbias_bf", [1, 512], BF16)
    alog = nc.alloc_sbuf_tensor("alog", [1, 32], F32)
    eA = nc.alloc_sbuf_tensor("eA", [1, 32], F32)
    gater = nc.alloc_sbuf_tensor("gater", [1, 16], F32)
    gnorm = nc.alloc_sbuf_tensor("gnorm", [128, 1], F32)
    amask = nc.alloc_sbuf_tensor("amask_b", [128, 3], F32)
    bmask = nc.alloc_sbuf_tensor("bmask_b", [128, 96], F32)
    negA = nc.alloc_sbuf_tensor("negA", [128, 32], F32)
    gateb = nc.alloc_sbuf_tensor("gateb", [128, 16], F32)
    dt_t = nc.alloc_sbuf_tensor("dt_t", [128, 512], F32)
    Dtot = nc.alloc_sbuf_tensor("Dtot", [128, 32], F32)
    Dg = nc.alloc_sbuf_tensor("Dg", [128, 32], F32)
    Dp = nc.alloc_sbuf_tensor("Dp", [128, 32], F32)
    ssqh = nc.alloc_sbuf_tensor("ssqh", [128, NCH], F32)
    rrmsh = nc.alloc_sbuf_tensor("rrmsh", [128, NCH], F32)
    lnsc = nc.alloc_sbuf_tensor("lnsc", [128, 8], F32)
    epsb = nc.alloc_sbuf_tensor("epsb", [128, 1], F32)
    rt1 = [nc.alloc_sbuf_tensor(f"rt1_{i}", [128, 512], BF16) for i in range(2)]
    rt2 = [nc.alloc_sbuf_tensor(f"rt2_{i}", [128, 512], BF16) for i in range(2)]
    rt3 = [nc.alloc_sbuf_tensor(f"rt3_{i}", [128, 512], BF16) for i in range(2)]
    dtm = nc.alloc_sbuf_tensor("dtm", [128, 32], F32)
    dtl = nc.alloc_sbuf_tensor("dtl", [128, 32], F32)
    Fh = [nc.alloc_sbuf_tensor(f"Fh{i}", [128, 128], F32) for i in range(2)]

    xcb = [nc.alloc_sbuf_tensor(f"xcb{i}", [128, 128], BF16) for i in range(2)]

    sqscr = nc.alloc_sbuf_tensor("sqscr", [128, 128], F32)

    tscr = nc.alloc_sbuf_tensor("tscr", [128, 128], F32)

    ARENA = 188 * KB
    arena = nc.alloc_sbuf_tensor("arena", [128, ARENA], mybir.dt.uint8)
    abase = nc.lookup_mloc(arena).addr

    def at(name, shape, dtype, off_kb):
        return nc.alloc_sbuf_tensor_at(name, list(shape), dtype,
                                       offset=abase + int(off_kb * KB))

    XFH = at("XFH", [128, CT * THALF], BF16, 0)
    wstrip = [at(f"wstrip{i}", [128, CT * 512], BF16, 64 + 32 * i) for i in range(2)]
    hstage = [at(f"hstage{i}", [128, 8 * 512], BF16, 128 + 8 * i) for i in range(2)]
    cosb = at("cosb", [128, NCH * 512], BF16, 144)
    sinb = at("sinb", [128, NCH * 512], BF16, 160)
    g_t = at("g_t", [128, 512], F32, 0)
    G_t = at("G_t", [128, 512], F32, 2)
    negG_t = at("negG_t", [128, 512], F32, 4)
    expG_t = at("expG_t", [128, 512], F32, 6)
    Gl_t = at("Gl_t", [128, 512], F32, 8)
    Saf_t = at("Saf_t", [128, 512], F32, 10)
    ed_t = at("ed_t", [128, 512], F32, 12)
    ecd_t = at("ecd_t", [128, 512], F32, 14)
    kdg_t = at("kdg_t", [128, 512], F32, 16)
    tmp_t = at("tmp_t", [128, 512], F32, 18)
    Dcum_t = at("Dcum_t", [128, 512], F32, 16)
    Sin = at("Sin", [128, H * 128], F32, 20)
    Fj = at("Fj", [128, H * 128], F32, 36)
    Yacc = at("Yacc", [128, NCH * 128], F32, 52)
    ypacc = at("ypacc", [128, NCH * 128], BF16, 60)
    qTh = [at(f"qTh{i}", [128, T], BF16, 64 + 4 * i) for i in range(2)]
    kTh = [at(f"kTh{i}", [128, T], BF16, 72 + 4 * i) for i in range(2)]
    ktok = [at(f"ktok{i}", [128, T], BF16, 80 + 4 * i) for i in range(2)]
    vtok = [at(f"vtok{i}", [128, T], BF16, 88 + 4 * i) for i in range(2)]
    gth = [at(f"gth{i}", [128, T], BF16, 96 + 4 * i) for i in range(2)]
    prevacc = [at(f"prevacc{i}", [128, NCH * 128], BF16, 104 + 4 * i) for i in range(4)]
    prevldr = [at(f"prevldr{i}", [128, NCH * 128], BF16, 120 + 4 * i) for i in range(2)]
    pmb = [at(f"pmb{i}", [128, 128], BF16, 132 + 0.25 * i) for i in range(2)]
    Sloc = [at(f"Sloc{i}", [128, 128], F32, 132.5 + 0.5 * i) for i in range(4)]
    xc1 = [at(f"xc1_{i}", [128, 128], BF16, 134.5 + 0.25 * i) for i in range(4)]
    kd1 = [at(f"kd1_{i}", [128, 128], BF16, 135.5 + 0.25 * i) for i in range(4)]
    diagp = [at(f"diagp{i}", [128, 128], F32, 128 + 0.5 * i) for i in range(2)]
    decb = [at(f"decb{i}", [128, 128], F32, 129 + 0.5 * i) for i in range(2)]
    sdtb = [at(f"sdtb{i}", [128, 128], BF16, 130 + 0.25 * i) for i in range(2)]
    zbp = [at(f"zbp{i}", [128, 128], F32, 130.5 + 0.5 * i) for i in range(2)]
    sgb = [at(f"sgb{i}", [128, 128], BF16, 131.5 + 0.25 * i) for i in range(2)]
    yT = at("yT", [128, H * THALF], BF16, 0)
    stg = at("stg", [128, 8 * 4096], BF16, 64)
    nstrip = [at(f"nstrip{i}", [128, CT * 256], BF16, 128 + 16 * i) for i in range(2)]
    zbuf = at("zbuf", [128, 4096], F32, 0)
    sqb = at("sqb", [128, 4096], F32, 16)
    xch = [at(f"xch{i}", [128, 4096], F32, 32 + 16 * i) for i in range(2)]
    x1ch = [at(f"x1ch{i}", [128, 4096], BF16, 32 + 8 * i) for i in range(2)]
    AeT32 = at("AeT32", [128, CT * 32], F32, 64)
    BeT = at("BeT", [32, 4096], F32, 68)
    bstg = at("bstg", [32, 4096], F32, 84)
    AxTb = at("AxTb", [32, THALF], BF16, 160)
    AxsTb = at("AxsTb", [64, THALF], BF16, 162)
    BeTb = at("BeTb", [32, 4096], BF16, 164)
    BsTb = at("BsTb", [64, 4096], BF16, 172)
    lstg = [at(f"lstg{i}", [128, 16 * 32], F32, 180 + 2 * i) for i in range(2)]
    AeT = at("AeT", [128, CT * 32], BF16, 184)
    AsT = at("AsT", [128, CT * 32], BF16, 186)

    bank = [nc.alloc_psum_tensor(f"bank{i}", [128, 512], F32) for i in range(8)]

    es = ExitStack()
    semnames = ["sW", "sX", "sSP", "sPE", "sAC", "sDV", "sCC", "sB", "sOUT"]
    p.sems = {n: es.enter_context(nc.semaphore(n)) for n in semnames}

    def ap3(t, a):
        return t.ap().rearrange("p (a b) -> p a b", a=a)

    for (dst, src) in [(identf, identf_d), (mnegf, mnegf_d), (utf, utf_d),
                       (dtbias, dtbias_d), (alog, alog_d), (gater, gate_d),
                       (gnorm, gnorm_d), (amask, amask_d), (bmask, bmask_d)]:
        p.inc("sX", 16)
        p.add("sy", lambda e, d=dst, s=src: e.dma_start(out=d[:, :], in_=s), (), ("sX", 16))
    p.inc("sX", 16)
    p.add("sy", lambda e: e.dma_start(out=ap3(cosb, NCH),
                                      in_=cosrep_d.rearrange("(n p) c -> p n c", p=128)),
          (), ("sX", 16))
    p.inc("sX", 16)
    p.add("sy", lambda e: e.dma_start(out=ap3(sinb, NCH),
                                      in_=sinSrep_d.rearrange("(n p) c -> p n c", p=128)),
          (), ("sX", 16))
    x0 = p.val("sX")
    p.inc("sDV")
    p.add("ve", lambda e: e.memset(onesf[:, :], 1.0), (), ("sDV", 1))
    p.inc("sDV")
    p.add("ve", lambda e: e.memset(epsb[:, :], 1e-5), (), ("sDV", 1))
    v0 = p.inc("sDV")
    p.add("ve", lambda e: e.memset(onesrow[:, :], 1.0), (), ("sDV", 1))
    a0 = p.inc("sAC")
    p.add("ac", lambda e: e.activation(eA[:, :], alog[:, :], AF.Exp), [("sX", x0)], ("sAC", 1))
    p.add("ac", lambda e: e.copy(onesrow_b[:, :], onesrow[:, :]), [("sDV", v0)])
    a1 = p.inc("sAC")
    p.add("ac", lambda e: e.copy(dtbias_bf[:, :], dtbias[:, :]), [("sX", x0)], ("sAC", 1))
    t0 = p.inc("sPE")
    p.add("te", lambda e: e.matmul(bank[2][:, 0:32], onesrow[0:1, :], eA[0:1, :],
                                   start=True, stop=True),
          [("sAC", a0), ("sDV", v0)], ("sPE", 1))
    t1 = p.inc("sPE")
    p.add("te", lambda e: e.matmul(bank[3][:, 0:16], onesrow[0:1, :], gater[0:1, :],
                                   start=True, stop=True),
          [("sX", x0), ("sDV", v0)], ("sPE", 1))
    p.add("ve", lambda e: e.tensor_scalar_mul(negA[:, :], bank[2][:, 0:32], -1.0),
          [("sPE", t0)])
    p.add("ac", lambda e: e.copy(gateb[:, :], bank[3][:, 0:16]), [("sPE", t1)])
    p.barrier()

    def strip_dest(s):
        if s < 8:
            return ("q", q_sp, s * 512, True)
        if s < 10:
            return ("k", k_sp, (s - 8) * 512, True)
        if s < 12:
            return ("v", v_sp, (s - 10) * 512, False)
        if s < 20:
            return ("g", g_sp, (s - 12) * 512, False)
        return ("dt", None, 0, False)

    XFHv = ap3(XFH, CT)
    pe_tile = {}
    last_rope_dv = [0, 0]
    last_dt_dv = [0]
    evac_done = {}
    hs_written = {}
    spill_done = {}
    pe_half = {}
    wload = {}
    xload = {}
    kglob = 0
    for half in range(2):
        w = [("sPE", pe_half.get(half - 1, 0))]
        xload[half] = p.inc("sW", 16)
        p.add("gp", lambda e, hf=half: e.dma_start(
            out=XFHv,
            in_=xF[:, hf * THALF:(hf + 1) * THALF].rearrange("(ct p) t -> p ct t", p=128)),
            w, ("sW", 16))
        for s in range(NSTRIP1):
            k = kglob
            kglob += 1
            par = k % 2
            w = []
            if k >= 2:
                pk = k - 2
                ph, ps = divmod(pk, NSTRIP1)
                w.append(("sPE", pe_tile[(ph, ps, 7)]))
            wload[k] = p.inc("sW", 16)
            p.add("gp", lambda e, s=s, par=par: e.dma_start(
                out=ap3(wstrip[par], CT), in_=wtiles[s]), w, ("sW", 16))
            kind, dest, dcol, isrope = strip_dest(s)
            for tt in range(8):
                n = half * 8 + tt
                bk = (2 + tt % 2) if kind == "dt" else (k * 8 + tt) % 2
                w = [("sW", wload[k]), ("sW", xload[half])]
                if kind == "dt":
                    prevtile = (half, s, tt - 2) if tt >= 2 else \
                        ((half - 1, s, tt + 6) if half >= 1 else None)
                else:
                    gtix = k * 8 + tt
                    if gtix >= 2:
                        pg = gtix - 2
                        pk2, ptt = divmod(pg, 8)
                        ph2, ps2 = divmod(pk2, NSTRIP1)
                        prevtile = (ph2, ps2, ptt)
                        while prevtile is not None and strip_dest(prevtile[1])[0] == "dt":
                            pg -= 1
                            if pg < 0:
                                prevtile = None
                                break
                            pk2, ptt = divmod(pg, 8)
                            ph2, ps2 = divmod(pk2, NSTRIP1)
                            prevtile = (ph2, ps2, ptt)
                    else:
                        prevtile = None
                if prevtile is not None and prevtile in evac_done:
                    w += evac_done[prevtile]

                def mk_pe(par=par, tt=tt, bk=bk, isdt=(kind == "dt")):
                    def f(e):
                        wv = ap3(wstrip[par], CT)
                        ins = None
                        for ct in range(CT):
                            ins = e.matmul(bank[bk][:, 0:512],
                                           XFHv[:, ct, tt * 128:(tt + 1) * 128],
                                           wv[:, ct, :],
                                           start=(ct == 0),
                                           stop=(ct == CT - 1 and not isdt))
                        if isdt:
                            ins = e.matmul(bank[bk][:, 0:512], onesrow_b[0:1, :],
                                           dtbias_bf[0:1, :], start=False, stop=True)
                        return ins
                    return f
                pe_tile[(half, s, tt)] = p.inc("sPE")
                p.add("te", mk_pe(), w, ("sPE", 1))

                if kind == "dt":
                    vv = p.inc("sDV")
                    p.add("ve", lambda e, bk=bk: e.tensor_scalar_max(
                        dtm[:, :], bank[bk][:, 0:32], 0.0),
                        [("sPE", pe_tile[(half, s, tt)]), ("sDV", last_dt_dv[0])],
                        ("sDV", 1))
                    dt_max_dv = vv
                    aa = p.inc("sAC")
                    p.add("ac", lambda e, bk=bk: e.activation(dtl[:, :], bank[bk][:, 0:32], AF.Abs),
                          [("sPE", pe_tile[(half, s, tt)]), ("sDV", last_dt_dv[0])],
                          ("sAC", 1))
                    evac_done[(half, s, tt)] = [("sDV", vv), ("sAC", aa)]
                    p.add("ac", lambda e: e.activation(dtl[:, :], dtl[:, :], AF.Exp, scale=-1.0))
                    aa2 = p.inc("sAC")
                    p.add("ac", lambda e: e.activation(dtl[:, :], dtl[:, :], AF.Ln, bias=1.0),
                          (), ("sAC", 1))
                    last_dt_dv[0] = p.inc("sDV")
                    p.add("ve", lambda e, n=n: e.tensor_add(
                        dt_t[:, n * 32:(n + 1) * 32], dtm[:, :], dtl[:, :]),
                        [("sAC", aa2), ("sDV", dt_max_dv)], ("sDV", 1))
                    continue

                wA = [("sPE", pe_tile[(half, s, tt)])]
                if (k - 2) in spill_done:
                    wA.append(("sSP", spill_done[k - 2]))
                if isrope:
                    rp_ = (k * 8 + tt) % 2
                    if last_rope_dv[rp_] > 0:
                        wA.append(("sDV", last_rope_dv[rp_]))
                    aa = p.inc("sAC")
                    p.add("ac", lambda e, bk=bk, rp_=rp_: e.copy(rt1[rp_][:, :], bank[bk][:, 0:512]),
                          wA, ("sAC", 1))
                    evac_done[(half, s, tt)] = [("sAC", aa)]
                    r1 = rt1[rp_].ap().rearrange("p (h s d) -> p h s d", h=4, s=2)
                    r2 = rt2[rp_].ap().rearrange("p (h s d) -> p h s d", h=4, s=2)
                    p.add("ve", lambda e, n=n, rp_=rp_: e.tensor_mul(
                        rt3[rp_][:, :], rt1[rp_][:, :], ap3(cosb, NCH)[:, n, :]),
                        [("sAC", aa), ("sX", x0)])
                    p.add("ve", lambda e, n=n, r1=r1, r2=r2: e.tensor_mul(
                        r2[:, :, 0, :], r1[:, :, 1, :],
                        ap3(sinb, NCH)[:, n, :].rearrange("p (h s d) -> p h s d", h=4, s=2)[:, :, 0, :]))
                    vhi = p.inc("sDV")
                    p.add("ve", lambda e, n=n, r1=r1, r2=r2: e.tensor_mul(
                        r2[:, :, 1, :], r1[:, :, 0, :],
                        ap3(sinb, NCH)[:, n, :].rearrange("p (h s d) -> p h s d", h=4, s=2)[:, :, 1, :]),
                        (), ("sDV", 1))
                    vv = p.inc("sDV")
                    p.add("ve", lambda e, par=par, tt=tt, rp_=rp_: e.tensor_add(
                        hstage[par][:, tt * 512:(tt + 1) * 512], rt3[rp_][:, :], rt2[rp_][:, :]),
                        [("sDV", vhi)], ("sDV", 1))
                    last_rope_dv[rp_] = vv
                    hs_written[(half, s)] = ("sDV", vv)
                else:
                    aa = p.inc("sAC")
                    p.add("ac", lambda e, bk=bk, par=par, tt=tt: e.copy(
                        hstage[par][:, tt * 512:(tt + 1) * 512], bank[bk][:, 0:512]),
                        wA, ("sAC", 1))
                    evac_done[(half, s, tt)] = [("sAC", aa)]
                    hs_written[(half, s)] = ("sAC", aa)
            if kind != "dt":
                spill_done[k] = p.inc("sSP", 16)
                p.add("sy", lambda e, par=par, dest=dest, dcol=dcol, hf=half: e.dma_start(
                    out=dest[hf * THALF:(hf + 1) * THALF, dcol:dcol + 512]
                        .rearrange("(tt p) o -> p tt o", p=128),
                    in_=ap3(hstage[par], 8)),
                    [hs_written[(half, s)]], ("sSP", 16))
        pe_half[half] = p.val("sPE")
    p.barrier()
    if phases <= 1:
        return _finish(nc, p, dbg, scr, dumps)

    vg = None
    for n in range(NCH):
        inc = ("sDV", 1) if n == NCH - 1 else None
        p.add("ve", lambda e, n=n: e.tensor_mul(g_t[:, n * 32:(n + 1) * 32],
                                                dt_t[:, n * 32:(n + 1) * 32], negA[:, :]),
              (), inc)
    vg = p.inc("sDV")
    for n in range(NCH):
        p.add("te", lambda e, n=n: e.matmul(bank[4][:, n * 32:(n + 1) * 32], utf[:, :],
                                            g_t[:, n * 32:(n + 1) * 32], start=True, stop=True),
              [("sDV", vg)] if n == 0 else ())
        p.add("te", lambda e, n=n: e.matmul(bank[5][:, n * 32:(n + 1) * 32], onesf[:, :],
                                            g_t[:, n * 32:(n + 1) * 32], start=True, stop=True),
              (), ("sPE", 1) if n == NCH - 1 else None)
        if n == NCH - 1:
            pass
    tGall = p.inc("sPE")
    aG = p.inc("sAC")
    p.add("ac", lambda e: e.copy(G_t[:, :], bank[4][:, :]), [("sPE", tGall)], ("sAC", 1))
    aGl = p.inc("sAC")
    p.add("ac", lambda e: e.copy(Gl_t[:, :], bank[5][:, :]), (), ("sAC", 1))
    vsf = p.inc("sDV")
    p.add("ve", lambda e: e.memset(Saf_t[:, 15 * 32:16 * 32], 0.0), [("sAC", aGl)], ("sDV", 1))
    for n in range(14, -1, -1):
        wv = [("sDV", vsf)]
        vsf = p.inc("sDV")
        p.add("ve", lambda e, n=n: e.tensor_add(Saf_t[:, n * 32:(n + 1) * 32],
                                                Saf_t[:, (n + 1) * 32:(n + 2) * 32],
                                                Gl_t[:, (n + 1) * 32:(n + 2) * 32]),
              wv, ("sDV", 1))
    p.add("ve", lambda e: e.tensor_sub(tmp_t[:, :], Gl_t[:, :], G_t[:, :]), [("sAC", aG)])
    vS = p.inc("sDV")
    p.add("ve", lambda e: e.tensor_scalar_mul(negG_t[:, :], G_t[:, :], -1.0), (), ("sDV", 1))
    p.add("ac", lambda e: e.activation(ed_t[:, :], tmp_t[:, :], AF.Exp), [("sDV", vS)])
    p.add("ac", lambda e: e.activation(ecd_t[:, :], Gl_t[:, :], AF.Exp))
    aE = p.inc("sAC")
    p.add("ac", lambda e: e.activation(expG_t[:, :], G_t[:, :], AF.Exp), (), ("sAC", 1))

    vpf = p.inc("sDV")
    p.add("ve", lambda e: e.memset(Dcum_t[:, 0:32], 0.0), (), ("sDV", 1))
    for n in range(1, NCH):
        wv = [("sDV", vpf)]
        vpf = p.inc("sDV")
        p.add("ve", lambda e, n=n: e.tensor_add(Dcum_t[:, n * 32:(n + 1) * 32],
                                                Dcum_t[:, (n - 1) * 32:n * 32],
                                                Gl_t[:, (n - 1) * 32:n * 32]),
              wv, ("sDV", 1))
    aDc = p.inc("sAC")
    p.add("ac", lambda e: e.activation(Dcum_t[:, :], Dcum_t[:, :], AF.Exp),
          [("sDV", vpf)], ("sAC", 1))
    vD = p.inc("sDV")
    p.add("ve", lambda e: e.tensor_add(Dg[:, :], Saf_t[:, 0:32], Gl_t[:, 0:32]),
          [("sDV", vsf)], ("sDV", 1))
    aD = p.inc("sAC")
    p.add("ac", lambda e: e.activation(Dtot[:, :], Dg[:, :], AF.Exp), [("sDV", vD)], ("sAC", 1))
    tabs = [dt_t, g_t, G_t, Gl_t, Saf_t, ed_t, ecd_t, Dcum_t]
    wtab = [("sAC", p.val("sAC")), ("sDV", p.val("sDV"))]
    for i, tb in enumerate(tabs):
        p.inc("sSP", 16)
        p.add("sy", lambda e, i=i, tb=tb: e.dma_start(out=gtab_sp[i], in_=tb[:, :]),
              wtab, ("sSP", 16))
    p.barrier()
    if phases <= 2:
        return _finish(nc, p, dbg, scr, dumps)

    evF = {}
    dmaF = {}
    spP = {}
    kvload = {}
    lastCMM = {}
    for grp in range(8):
        kv = grp
        kpar = grp % 2
        w = []
        if grp >= 2:
            w.append(("sPE", lastCMM[grp - 2]))
        kvload[grp] = p.inc("sX", 32)
        p.add("sy", lambda e, kv=kv, kpar=kpar: e.dma_start(
            out=ap3(ktok[kpar], NCH),
            in_=k_sp[:, kv * 128:(kv + 1) * 128].rearrange("(n p) d -> p n d", p=128)),
            w, ("sX", 16))
        p.add("sy", lambda e, kv=kv, kpar=kpar: e.dma_start(
            out=ap3(vtok[kpar], NCH),
            in_=v_sp[:, kv * 128:(kv + 1) * 128].rearrange("(n p) d -> p n d", p=128)),
            (), ("sX", 16))
        vz0 = {}
        for hm in range(4):
            w = []
            if grp >= 1 and (grp - 1, hm) in spP:
                w.append(("sSP", spP[(grp - 1, hm)]))
            if grp >= 1:
                w.append(("sAC", evF[4 * (grp - 1) + hm]))
            p.add("ve", lambda e, hm=hm: e.memset(Sloc[hm][:, :], 0.0), w)
            vz0[hm] = p.inc("sDV")
            p.add("ve", lambda e, hm=hm: e.memset(prevacc[hm][:, 0:128], 0.0), (), ("sDV", 1))
        vst1 = {}
        vcst = {}
        for n in range(NCH):
            for hm in range(4):
                h = 4 * grp + hm
                col = n * 32 + h
                w = [("sX", kvload[grp]), ("sDV", vz0[hm])]
                if n >= 2:
                    w.append(("sPE", lastCMM.get((grp, hm, n - 2), 0)))
                elif grp >= 1:
                    w.append(("sPE", lastCMM.get((grp - 1, hm, NCH - 2 + n), 0)))
                p.add("ve", lambda e, n=n, hm=hm, kpar=kpar, col=col: e.tensor_scalar_mul(
                    xc1[hm][:, :], ap3(vtok[kpar], NCH)[:, n, :], dt_t[:, col:col + 1]), w)
                vk = p.inc("sDV")
                p.add("ve", lambda e, n=n, hm=hm, kpar=kpar, col=col: e.tensor_scalar_mul(
                    kd1[hm][:, :], ap3(ktok[kpar], NCH)[:, n, :], ed_t[:, col:col + 1]),
                    (), ("sDV", 1))
                w = [("sDV", vk)]
                if (n - 1, hm) in vst1:
                    w.append(("sDV", vst1[(n - 1, hm)]))
                tCm = p.inc("sPE")
                lastCMM[(grp, hm, n)] = tCm
                p.add("te", lambda e, hm=hm: e.matmul(bank[4 + hm][:, 0:128],
                                                      kd1[hm][:, :], xc1[hm][:, :],
                                                      start=True, stop=True), w, ("sPE", 1))
                vst1[(n, hm)] = p.inc("sDV")
                p.add("ve", lambda e, hm=hm, col=col: e.scalar_tensor_tensor(
                    Sloc[hm][:, :], Sloc[hm][:, :], ecd_t[:, col:col + 1],
                    bank[4 + hm][:, 0:128], ALU.mult, ALU.add),
                    [("sPE", tCm)], ("sDV", 1))
                if n < NCH - 1:
                    vcst[(n, hm)] = p.inc("sDV")
                    p.add("ve", lambda e, hm=hm, n=n: e.tensor_copy(
                        prevacc[hm][:, (n + 1) * 128:(n + 2) * 128], Sloc[hm][:, :]),
                        [("sDV", vst1[(n, hm)])], ("sDV", 1))
        lastCMM[grp] = p.val("sPE")
        for hm in range(4):
            h = 4 * grp + hm
            w = [("sDV", vst1[(NCH - 1, hm)])]
            if h >= 2:
                w.append(("sSP", dmaF.get(h - 2, 0)))
            evF[h] = p.inc("sAC")
            p.add("ac", lambda e, hm=hm: e.copy(Fh[hm % 2][:, :], Sloc[hm][:, :]),
                  w if hm < 2 else [("sDV", vst1[(NCH - 1, hm)]), ("sAC", evF[h - 2] if h >= 2 else 0), ("sSP", dmaF.get(h - 2, 0))],
                  ("sAC", 1))
            dmaF[h] = p.inc("sSP", 16)
            p.add("sy", lambda e, h=h, hm=hm: e.dma_start(
                out=agin[h * 128:(h + 1) * 128, :], in_=Fh[hm % 2][:, :]),
                [("sAC", evF[h])], ("sSP", 16))
            spP[(grp, hm)] = p.inc("sSP", 16)
            p.add("sy", lambda e, h=h, hm=hm: e.dma_start(
                out=prevs_sp[h].rearrange("(n p) d -> p n d", p=128),
                in_=ap3(prevacc[hm], NCH)),
                [("sDV", vcst[(NCH - 2, hm)]), ("sDV", vz0[hm])], ("sSP", 16))
    p.inc("sSP", 16)
    p.add("sy", lambda e: e.dma_start(out=agin[4096:4224, 0:32], in_=Dtot[:, :]),
          [("sAC", aD)], ("sSP", 16))
    p.add("gp", lambda e: e.collective_compute(
        "AllGather", ALU.bypass, replica_groups=[[0, 1, 2, 3], [4, 5, 6, 7]],
        ins=[agin.opt()], outs=[agout.opt()]),
        [("sSP", p.val("sSP"))], ("sCC", None))
    if phases <= 2.9:
        p.add("gp", lambda e: e.wait_ge(p.sems["sCC"], 1))
        p.barrier()
        return _finish(nc, p, dbg, scr, dumps)
    p.inc("sDV")
    p.add("ve", lambda e: e.memset(Sin[:, :], 0.0), (), ("sDV", 1))
    for j in range(3):
        w = [("sCC", 1), ("sDV", p.val("sDV"))]
        xj = p.inc("sX", 32)
        p.add("sy", lambda e, j=j: e.dma_start(
            out=ap3(Fj, H),
            in_=agout[j * 4224:j * 4224 + 4096, :].rearrange("(h p) e -> p h e", p=128)),
            w, ("sX", 16))
        p.add("sy", lambda e, j=j: e.dma_start(
            out=Dg[:, :], in_=agout[j * 4224 + 4096:j * 4224 + 4224, 0:32]), (), ("sX", 16))
        p.add("ve", lambda e, j=j: e.scalar_tensor_tensor(
            Dp[:, :], Dg[:, :], amask[:, j:j + 1], bmask[:, j * 32:(j + 1) * 32],
            ALU.mult, ALU.add), [("sX", xj)])
        for h in range(H):
            vts = p.inc("sDV")
            p.add("ve", lambda e, h=h: e.tensor_scalar_mul(
                tscr[:, :], Sin[:, h * 128:(h + 1) * 128], Dp[:, h:h + 1]), (), ("sDV", 1))
            p.add("ve", lambda e, h=h, j=j: e.scalar_tensor_tensor(
                Sin[:, h * 128:(h + 1) * 128], ap3(Fj, H)[:, h, :], amask[:, j:j + 1],
                tscr[:, :], ALU.mult, ALU.add), [("sDV", vts)],
                ("sDV", 1) if h == H - 1 else None)
        p.inc("sDV")
    p.barrier()
    if phases <= 3:
        return _finish(nc, p, dbg, scr, dumps)

    bS = [3, 0]; bR = [4, 2]; bY = [5, 1]; bZ = [6, 7]
    ld = {}
    lastuse = {}
    spillY = {}
    prev_vyp15 = [0]
    last_SDt = {0: 0, 1: 0}
    last_exp = {0: 0, 1: 0}
    last_Y = {0: 0, 1: 0}
    last_yc = {0: 0, 1: 0}
    last_zb = {0: 0, 1: 0}
    last_rowb = {0: 0, 1: 0}
    last_xcmm = {0: 0, 1: 0}
    last_ZMM = {0: 0, 1: 0}
    last_yp = {0: 0, 1: 0}
    for h in range(H):
        kv = h // 4
        par = h % 2
        kpar = kv % 2
        w = [lastuse[("q", h - 2)]] if ("q", h - 2) in lastuse else []
        ld[("q", h)] = p.inc("sX", 16)
        p.add("sy", lambda e, h=h, par=par: e.dma_start_transpose(
            out=qTh[par][:, :], in_=q_sp[:, h * 128:(h + 1) * 128]), w, ("sX", 16))
        w = [lastuse[("g", h - 2)]] if ("g", h - 2) in lastuse else []
        ld[("g", h)] = p.inc("sX", 16)
        p.add("sy", lambda e, h=h, par=par: e.dma_start(
            out=ap3(gth[par], NCH),
            in_=g_sp[:, h * 128:(h + 1) * 128].rearrange("(n p) d -> p n d", p=128)),
            w, ("sX", 16))
        w = [lastuse[("pv", h - 2)]] if ("pv", h - 2) in lastuse else []
        ld[("pv", h)] = p.inc("sX", 16)
        p.add("sy", lambda e, h=h, par=par: e.dma_start(
            out=ap3(prevldr[par], NCH),
            in_=prevs_sp[h].rearrange("(n p) d -> p n d", p=128)), w, ("sX", 16))
        if h % 4 == 0:
            w = [lastuse[("kv", kv - 2)]] if ("kv", kv - 2) in lastuse else []
            ld[("kT", kv)] = p.inc("sX", 16)
            p.add("sy", lambda e, kv=kv, kpar=kpar: e.dma_start_transpose(
                out=kTh[kpar][:, :], in_=k_sp[:, kv * 128:(kv + 1) * 128]), w, ("sX", 16))
            ld[("vtok", kv)] = p.inc("sX", 16)
            p.add("sy", lambda e, kv=kv, kpar=kpar: e.dma_start(
                out=ap3(vtok[kpar], NCH),
                in_=v_sp[:, kv * 128:(kv + 1) * 128].rearrange("(n p) d -> p n d", p=128)),
                (), ("sX", 16))
        vSDt = {}; vxc = {}; vzb = {}; vyc = {}; vpm = {}
        tS = {}; tR = {}; tY = {}; tZ = {}; aexp = {}
        for n in range(NCH):
            npar = n % 2
            col = n * 32 + h
            w = [("sX", ld[("pv", h)])]
            if last_ZMM[npar] > 0:
                w.append(("sPE", last_ZMM[npar]))
            vpm[n] = p.inc("sDV")
            p.add("ve", lambda e, n=n, h=h, par=par, npar=npar, col=col: e.scalar_tensor_tensor(
                pmb[npar][:, :], Sin[:, h * 128:(h + 1) * 128], Dcum_t[:, col:col + 1],
                ap3(prevldr[par], NCH)[:, n, :], ALU.mult, ALU.add), w, ("sDV", 1))
            w = [("sX", ld[("q", h)]), ("sX", ld[("kT", kv)])]
            if last_SDt[npar] > 0:
                w.append(("sDV", last_SDt[npar]))
            tS[n] = p.inc("sPE")
            p.add("te", lambda e, n=n, par=par, kpar=kpar, npar=npar: e.matmul(
                bank[bS[npar]][:, 0:128], kTh[kpar][:, n * 128:(n + 1) * 128],
                qTh[par][:, n * 128:(n + 1) * 128], start=True, stop=True), w, ("sPE", 1))
            w = []
            if last_rowb[npar] > 0:
                w.append(("sPE", last_rowb[npar]))
            vdg = p.inc("sDV")
            p.add("ve", lambda e, col=col, npar=npar: e.tensor_scalar_mul(
                diagp[npar][:, :], identf[:, :], G_t[:, col:col + 1]), w, ("sDV", 1))
            w = [("sDV", vdg)]
            if last_exp[npar] > 0:
                w.append(("sAC", last_exp[npar]))

            def mk_rowb(npar=npar):
                def f(e):
                    e.matmul(bank[bR[npar]][:, 0:128], onesf[:, :], diagp[npar][:, :],
                             start=True, stop=False)
                    return e.matmul(bank[bR[npar]][:, 0:128], identf[:, :], mnegf[:, :],
                                    start=False, stop=True)
                return f
            tR[n] = last_rowb[npar] = p.inc("sPE")
            p.add("te", mk_rowb(), w, ("sPE", 1))
            w = [("sPE", tR[n])]
            if last_SDt[npar] > 0:
                w.append(("sDV", last_SDt[npar]))
            aexp[n] = last_exp[npar] = p.inc("sAC")
            p.add("ac", lambda e, col=col, npar=npar: e.activation(
                decb[npar][:, :], bank[bR[npar]][:, 0:128], AF.Exp,
                bias=negG_t[:, col:col + 1]), w, ("sAC", 1))
            w = [("sPE", tS[n]), ("sAC", aexp[n])]
            if last_Y[npar] > 0:
                w.append(("sPE", last_Y[npar]))
            vSDt[n] = last_SDt[npar] = p.inc("sDV")
            p.add("ve", lambda e, npar=npar: e.tensor_mul(
                sdtb[npar][:, :], bank[bS[npar]][:, 0:128], decb[npar][:, :]), w, ("sDV", 1))
            w = [("sX", ld[("vtok", kv)])]
            if last_xcmm[npar] > 0:
                w.append(("sPE", last_xcmm[npar]))
            vxc[n] = p.inc("sDV")
            p.add("ve", lambda e, n=n, kpar=kpar, npar=npar, col=col: e.tensor_scalar_mul(
                xcb[npar][:, :], ap3(vtok[kpar], NCH)[:, n, :], dt_t[:, col:col + 1]),
                w, ("sDV", 1))
            w = [("sDV", vSDt[n]), ("sDV", vxc[n])]
            if last_yc[npar] > 0:
                w.append(("sDV", last_yc[npar]))
            tY[n] = last_Y[npar] = p.inc("sPE")
            p.add("te", lambda e, npar=npar: e.matmul(bank[bY[npar]][:, 0:128],
                                                      sdtb[npar][:, :], xcb[npar][:, :],
                                                      start=True, stop=True), w, ("sPE", 1))
            last_xcmm[npar] = tY[n]
            w = [("sDV", vpm[n]), ("sX", ld[("q", h)])]
            if last_zb[npar] > 0:
                w.append(("sDV", last_zb[npar]))
            tZ[n] = last_ZMM[npar] = p.inc("sPE")
            p.add("te", lambda e, n=n, par=par, npar=npar: e.matmul(
                bank[bZ[npar]][:, 0:128], qTh[par][:, n * 128:(n + 1) * 128],
                pmb[npar][:, :], start=True, stop=True), w, ("sPE", 1))
            w = [("sPE", tZ[n])]
            if last_yc[npar] > 0:
                w.append(("sDV", last_yc[npar]))
            vzb[n] = last_zb[npar] = p.inc("sDV")
            p.add("ve", lambda e, col=col, npar=npar: e.tensor_scalar_mul(
                zbp[npar][:, :], bank[bZ[npar]][:, 0:128], expG_t[:, col:col + 1]),
                w, ("sDV", 1))
            vyc[n] = last_yc[npar] = p.inc("sDV")
            p.add("ve", lambda e, n=n, npar=npar: e.tensor_add(
                Yacc[:, n * 128:(n + 1) * 128], zbp[npar][:, :], bank[bY[npar]][:, 0:128]),
                [("sPE", tY[n]), ("sDV", vzb[n])], ("sDV", 1))
            p.inc("sAC")
            p.add("ac", lambda e, n=n: e.activation(sqscr[:, :], Yacc[:, n * 128:(n + 1) * 128],
                                                    AF.Square, accum_out=ssqh[:, n:n + 1]),
                  [("sDV", vyc[n])], ("sAC", 1))
        lastuse[("q", h)] = ("sPE", max(tS[NCH - 1], tZ[NCH - 1]))
        lastuse[("kv", kv)] = ("sPE", tS[NCH - 1])
        lastuse[("pv", h)] = ("sDV", vpm[NCH - 1])
        lastuse[("g", h)] = None
        arr = p.inc("sAC", 2)
        wrr = [("sDV", prev_vyp15[0])] if prev_vyp15[0] > 0 else []
        p.add("ac", lambda e: e.activation(rrmsh[:, :], ssqh[:, :], AF.Ln,
                                           bias=epsb[:, 0:1], scale=1.0 / 128), wrr)
        p.add("ac", lambda e: e.activation(rrmsh[:, :], rrmsh[:, :], AF.Exp, scale=-0.5),
              (), ("sAC", 2))
        vyp = {}
        for n in range(NCH):
            npar = n % 2
            w = [("sX", ld[("g", h)]), ("sAC", arr)]
            if n == 0 and (h - 1) in spillY:
                w.append(("sSP", spillY[h - 1]))
            if last_yp[npar] > 0:
                w.append(("sDV", last_yp[npar]))
            asg = p.inc("sAC")
            p.add("ac", lambda e, n=n, par=par, npar=npar: e.activation(
                sgb[npar][:, :], ap3(gth[par], NCH)[:, n, :], AF.Silu), w, ("sAC", 1))
            p.add("ve", lambda e, n=n: e.tensor_scalar_mul(
                tscr[:, :], Yacc[:, n * 128:(n + 1) * 128], rrmsh[:, n:n + 1]),
                [("sAC", asg)])
            vyp[n] = last_yp[npar] = p.inc("sDV")
            p.add("ve", lambda e, n=n, npar=npar: e.tensor_mul(
                ypacc[:, n * 128:(n + 1) * 128], tscr[:, :], sgb[npar][:, :]), (), ("sDV", 1))
        prev_vyp15[0] = vyp[NCH - 1]
        lastuse[("g", h)] = ("sAC", p.val("sAC"))
        spillY[h] = p.inc("sSP", 16)
        p.add("sy", lambda e, h=h: e.dma_start(
            out=y_sp[:, h * 128:(h + 1) * 128].rearrange("(n p) d -> p n d", p=128),
            in_=ap3(ypacc, NCH)), [("sDV", vyp[NCH - 1])], ("sSP", 16))
    p.barrier()
    if phases <= 4:
        return _finish(nc, p, dbg, scr, dumps)

    yTv = ap3(yT, H)
    stgv = ap3(stg, 8)

    epi_state = {"aa": 0, "outdma": 0}

    def ln_epilogue(hf, tt, tpar, strips_done, residual_load, out_emit, out_inc):
        lx = p.inc("sX", 16)
        wl = [strips_done]
        if tt >= 2:
            wl.append(("sDV", p.val("sDV")))
        p.add("sy", residual_load, wl, ("sX", 16))
        wz = [("sX", lx), strips_done]
        if epi_state["aa"] > 0:
            wz.append(("sAC", epi_state["aa"]))
        vz = p.inc("sDV")
        p.add("ve", lambda e, tt=tt, tpar=tpar: e.tensor_add(
            zbuf[:, :], (xch if out_inc == "sSP" else x1ch)[tpar][:, :], stgv[:, tt, :]),
            wz, ("sDV", 1))
        aa = p.inc("sAC", 2)
        epi_state["aa"] = aa
        wacc = [("sDV", vz)]
        if out_inc == "sOUT" and epi_state["outdma"] > 0:
            wacc.append(("sOUT", epi_state["outdma"]))
        p.add("ac", lambda e: e.activation(sqb[:, :], zbuf[:, :], AF.Copy,
                                           accum_out=lnsc[:, 0:1]), wacc)
        p.add("ac", lambda e: e.activation(sqb[:, :], zbuf[:, :], AF.Square,
                                           accum_out=lnsc[:, 1:2]), (), ("sAC", 2))
        vmean = p.inc("sDV")
        p.add("ve", lambda e: e.tensor_scalar_mul(lnsc[:, 2:3], lnsc[:, 0:1], 1.0 / 4096),
              [("sAC", aa)], ("sDV", 1))
        am = p.inc("sAC", 2)
        p.add("ac", lambda e: e.activation(lnsc[:, 3:4], lnsc[:, 0:1], AF.Square,
                                           scale=1.0 / 4096))
        p.add("ac", lambda e: e.activation(lnsc[:, 4:5], lnsc[:, 1:2], AF.Copy,
                                           scale=1.0 / 4096), (), ("sAC", 2))
        vv = p.inc("sDV")
        p.add("ve", lambda e: e.tensor_sub(lnsc[:, 5:6], lnsc[:, 4:5], lnsc[:, 3:4]),
              [("sAC", am)], ("sDV", 1))
        ar = p.inc("sAC", 2)
        p.add("ac", lambda e: e.activation(lnsc[:, 6:7], lnsc[:, 5:6], AF.Ln, bias=epsb[:, 0:1]),
              [("sDV", vv)])
        p.add("ac", lambda e: e.activation(lnsc[:, 7:8], lnsc[:, 6:7], AF.Exp, scale=-0.5),
              (), ("sAC", 2))
        vo = p.inc("sDV")
        p.add("ve", out_emit, [("sAC", ar), ("sDV", vmean)], ("sDV", 1))
        nv = p.inc(out_inc, 16)
        if out_inc == "sOUT":
            epi_state["outdma"] = nv
        return vo

    pe_prev = None
    epi_done = None
    for hf in range(2):
        w = []
        if epi_done is not None:
            w.append(epi_done)
        for ht in range(H):
            p.inc("sX", 16)
            p.add("sy", lambda e, hf=hf, ht=ht: e.dma_start_transpose(
                out=yTv[:, ht, :],
                in_=y_sp[hf * THALF:(hf + 1) * THALF, ht * 128:(ht + 1) * 128]),
                w if ht == 0 else (), ("sX", 16))
        ldy = p.val("sX")
        wl = {}
        scl = {}
        pe_t = {}
        ev_t = {}
        for s in range(16):
            par = s % 2
            w = []
            if s >= 2:
                w.append(("sPE", pe_t[(s - 2, 7)]))
            elif pe_prev is not None:
                w.append(("sPE", pe_prev[(14 + s, 7)]))
            wl[s] = p.inc("sW", 16)
            p.add("gp", lambda e, s=s, par=par: e.dma_start(
                out=ap3(nstrip[par], CT), in_=wotiles[s]), w, ("sW", 16))
            scl[s] = p.inc("sDV")
            p.add("ve", lambda e, par=par: e.tensor_scalar_mul(
                nstrip[par][:, :], nstrip[par][:, :], gnorm[:, 0:1]),
                [("sW", wl[s])], ("sDV", 1))
            for tt in range(8):
                bk = (s * 8 + tt) % 2
                w = [("sDV", scl[s]), ("sX", ldy)]
                prev = (s, tt - 2) if tt >= 2 else ((s - 1, tt + 6) if s >= 1 else None)
                if prev in ev_t:
                    w.append(("sAC", ev_t[prev]))

                def mkp(par=par, tt=tt, bk=bk):
                    def f(e):
                        sv = ap3(nstrip[par], CT)
                        ins = None
                        for ht in range(H):
                            ins = e.matmul(bank[bk][:, 0:256],
                                           yTv[:, ht, tt * 128:(tt + 1) * 128],
                                           sv[:, ht, :], start=(ht == 0), stop=(ht == H - 1))
                        return ins
                    return f
                pe_t[(s, tt)] = p.inc("sPE")
                p.add("te", mkp(), w, ("sPE", 1))
                ev_t[(s, tt)] = p.inc("sAC")
                p.add("ac", lambda e, s=s, tt=tt, bk=bk: e.copy(
                    stgv[:, tt, s * 256:(s + 1) * 256], bank[bk][:, 0:256]),
                    [("sPE", pe_t[(s, tt)])], ("sAC", 1))
        pe_prev = pe_t
        strips_done = ("sAC", p.val("sAC"))
        for tt in range(8):
            tpar = tt % 2
            vo = ln_epilogue(
                hf, tt, tpar, strips_done,
                residual_load=lambda e, hf=hf, tt=tt, tpar=tpar: e.dma_start(
                    out=xch[tpar][:, :],
                    in_=xT[hf * THALF + tt * 128:hf * THALF + (tt + 1) * 128, :]),
                out_emit=lambda e, tt=tt: e.tensor_scalar(
                    stgv[:, tt, :], zbuf[:, :], lnsc[:, 2:3], lnsc[:, 7:8],
                    ALU.subtract, ALU.mult),
                out_inc="sSP")
            p.add("sy", lambda e, hf=hf, tt=tt: e.dma_start(
                out=x1_sp[hf * THALF + tt * 128:hf * THALF + (tt + 1) * 128, :],
                in_=stgv[:, tt, :]), [("sDV", vo)], ("sSP", 16))
        epi_done = ("sSP", p.val("sSP"))
    p.barrier()
    if phases <= 5:
        return _finish(nc, p, dbg, scr, dumps)

    x1Fv = ap3(yT, CT)
    vAct = {}
    for ct in range(CT):
        par = ct % 2
        w = []
        if ct >= 2:
            w.append(("sDV", vAct[ct - 2]))
        lx = p.inc("sX", 16)
        p.add("sy", lambda e, ct=ct, par=par: e.dma_start(
            out=ap3(lstg[par], 16),
            in_=loraAT[:, ct, :, :].rearrange("n p r -> p n r")), w, ("sX", 16))
        vs = p.inc("sDV")
        p.add("ve", lambda e, ct=ct, par=par: e.tensor_scalar_mul(
            AeT32[:, ct * 32:(ct + 1) * 32], ap3(lstg[par], 16)[:, 0, :], gateb[:, 0:1]),
            [("sX", lx)], ("sDV", 1))
        for nn in range(1, 16):
            wv = [("sDV", vs)]
            vs = p.inc("sDV")
            p.add("ve", lambda e, ct=ct, par=par, nn=nn: e.scalar_tensor_tensor(
                AeT32[:, ct * 32:(ct + 1) * 32], ap3(lstg[par], 16)[:, nn, :],
                gateb[:, nn:nn + 1], AeT32[:, ct * 32:(ct + 1) * 32], ALU.mult, ALU.add),
                wv, ("sDV", 1))
        vAct[ct] = p.inc("sDV")
        p.add("ve", lambda e, ct=ct: e.tensor_copy(AeT[:, ct * 32:(ct + 1) * 32],
                                                   AeT32[:, ct * 32:(ct + 1) * 32]),
              [("sDV", vs)], ("sDV", 1))
    vb = 0
    for nn in range(16):
        lx = p.inc("sX", 16)
        w = [("sDV", vb)] if vb else []
        p.add("sy", lambda e, nn=nn: e.dma_start(out=bstg[:, :], in_=loraBT[nn]),
              w, ("sX", 16))
        vbprev = vb
        vb = p.inc("sDV")
        if nn == 0:
            p.add("ve", lambda e: e.tensor_scalar_mul(BeT[:, :], bstg[:, :], gateb[0:32, 0:1]),
                  [("sX", lx)], ("sDV", 1))
        else:
            p.add("ve", lambda e, nn=nn: e.scalar_tensor_tensor(
                BeT[:, :], bstg[:, :], gateb[0:32, nn:nn + 1], BeT[:, :], ALU.mult, ALU.add),
                [("sX", lx), ("sDV", vbprev)], ("sDV", 1))
    vbe = p.inc("sDV")
    p.add("ve", lambda e: e.tensor_copy(BeTb[:, :], BeT[:, :]), [("sDV", vb)], ("sDV", 1))
    lsh = p.inc("sW", 32)
    p.add("gp", lambda e: e.dma_start(out=ap3(AsT, CT),
                                      in_=lorasAT.rearrange("ct p r -> p ct r")),
          (), ("sW", 16))
    p.add("gp", lambda e: e.dma_start(out=BsTb[0:32, :], in_=lorasBT), (), ("sW", 16))
    vz1 = p.inc("sDV")
    p.add("ve", lambda e: e.memset(BsTb[32:64, :], 0.0), (), ("sDV", 1))
    lfb = p.inc("sW", 16)
    p.add("gp", lambda e: e.dma_start(out=BsTb[32:33, :], in_=ffnbias_d),
          [("sDV", vz1)], ("sW", 16))
    pre_dv = p.val("sDV")

    pe_prev = None
    epi_done = None
    for hf in range(2):
        w = [("sDV", pre_dv)]
        if epi_done is not None:
            w.append(epi_done)
        for ctb in range(CT):
            p.inc("sX", 16)
            p.add("sy", lambda e, hf=hf, ctb=ctb: e.dma_start_transpose(
                out=x1Fv[:, ctb, :],
                in_=x1_sp[hf * THALF:(hf + 1) * THALF, ctb * 128:(ctb + 1) * 128]),
                w if ctb == 0 else (), ("sX", 16))
        ldx = p.val("sX")
        aax = {}
        for blk in range(2):
            w = [("sX", ldx)]
            if blk == 0:
                w += [("sDV", vAct[CT - 1]), ("sW", lsh)]
            else:
                w.append(("sAC", aax[0]))

            def mkax(blk=blk):
                def f(e):
                    for ct in range(CT):
                        e.matmul(bank[2][0:32, 0:512], AeT[:, ct * 32:(ct + 1) * 32],
                                 x1Fv[:, ct, blk * 512:(blk + 1) * 512],
                                 start=(ct == 0), stop=(ct == CT - 1))
                    ins = None
                    for ct in range(CT):
                        ins = e.matmul(bank[3][0:32, 0:512], AsT[:, ct * 32:(ct + 1) * 32],
                                       x1Fv[:, ct, blk * 512:(blk + 1) * 512],
                                       start=(ct == 0), stop=(ct == CT - 1))
                    return ins
                return f
            tax = p.inc("sPE")
            p.add("te", mkax(), w, ("sPE", 1))
            aax[blk] = p.inc("sAC", 2)
            p.add("ac", lambda e, blk=blk: e.copy(AxTb[:, blk * 512:(blk + 1) * 512],
                                                  bank[2][0:32, 0:512]), [("sPE", tax)])
            p.add("ac", lambda e, blk=blk: e.copy(AxsTb[0:32, blk * 512:(blk + 1) * 512],
                                                  bank[3][0:32, 0:512]), (), ("sAC", 2))
        p.inc("sDV")
        p.add("ve", lambda e: e.memset(AxsTb[32:64, :], 0.0), [("sAC", aax[1])], ("sDV", 1))
        vones = p.inc("sDV")
        p.add("ve", lambda e: e.memset(AxsTb[32:33, :], 1.0), (), ("sDV", 1))
        wl = {}
        pe_t = {}
        ev_t = {}
        for s in range(16):
            par = s % 2
            w = []
            if s >= 2:
                w.append(("sPE", pe_t[(s - 2, 7)]))
            elif pe_prev is not None:
                w.append(("sPE", pe_prev[(14 + s, 7)]))
            wl[s] = p.inc("sW", 16)
            p.add("gp", lambda e, s=s, par=par: e.dma_start(
                out=ap3(nstrip[par], CT), in_=kerntiles[s]), w, ("sW", 16))
            for tt in range(8):
                bk = (s * 8 + tt) % 2
                w = [("sW", wl[s]), ("sX", ldx), ("sDV", vones), ("sDV", vbe)]
                prev = (s, tt - 2) if tt >= 2 else ((s - 1, tt + 6) if s >= 1 else None)
                if prev in ev_t:
                    w.append(("sAC", ev_t[prev]))

                def mkp(par=par, tt=tt, bk=bk, s=s):
                    def f(e):
                        sv = ap3(nstrip[par], CT)
                        for ct in range(CT):
                            e.matmul(bank[bk][:, 0:256], x1Fv[:, ct, tt * 128:(tt + 1) * 128],
                                     sv[:, ct, :], start=(ct == 0), stop=False)
                        e.matmul(bank[bk][:, 0:256], AxTb[0:32, tt * 128:(tt + 1) * 128],
                                 BeTb[0:32, s * 256:(s + 1) * 256], start=False, stop=False)
                        return e.matmul(bank[bk][:, 0:256],
                                        AxsTb[0:64, tt * 128:(tt + 1) * 128],
                                        BsTb[0:64, s * 256:(s + 1) * 256],
                                        start=False, stop=True)
                    return f
                pe_t[(s, tt)] = p.inc("sPE")
                p.add("te", mkp(), w, ("sPE", 1))
                ev_t[(s, tt)] = p.inc("sAC")
                p.add("ac", lambda e, s=s, tt=tt, bk=bk: e.copy(
                    stgv[:, tt, s * 256:(s + 1) * 256], bank[bk][:, 0:256]),
                    [("sPE", pe_t[(s, tt)])], ("sAC", 1))
        pe_prev = pe_t
        strips_done = ("sAC", p.val("sAC"))
        for tt in range(8):
            tpar = tt % 2
            vo = ln_epilogue(
                hf, tt, tpar, strips_done,
                residual_load=lambda e, hf=hf, tt=tt, tpar=tpar: e.dma_start(
                    out=x1ch[tpar][:, :],
                    in_=x1_sp[hf * THALF + tt * 128:hf * THALF + (tt + 1) * 128, :]),
                out_emit=lambda e: e.tensor_scalar(
                    sqb[:, :], zbuf[:, :], lnsc[:, 2:3], lnsc[:, 7:8],
                    ALU.subtract, ALU.mult),
                out_inc="sOUT")
            p.add("gp", lambda e, hf=hf, tt=tt: e.dma_start(
                out=y_out[hf * THALF + tt * 128:hf * THALF + (tt + 1) * 128, :],
                in_=sqb[:, :]), [("sDV", vo)], ("sOUT", 16))
        epi_done = ("sOUT", p.val("sOUT"))

    return _finish(nc, p, dbg, scr, dumps)


def _finish(nc, p, dbg, scr, dumps):
    for name in dumps:
        wv = [("sSP", p.val("sSP")), ("sDV", p.val("sDV")), ("sAC", p.val("sAC")),
              ("sPE", p.val("sPE")), ("sOUT", p.val("sOUT"))]
        p.inc("sOUT", 16)
        p.add("gp", lambda e, name=name: e.dma_start(out=dbg[name], in_=scr[name]),
              wv, ("sOUT", 16))
    tgt = p.val("sOUT")
    if tgt > 0:
        def fin(e, tgt=tgt):
            e.wait_ge(p.sems["sOUT"], tgt)
        fin.meta = ("gp", (("sOUT", tgt),), None)
        p.q["gp"].append(fin)

    with nc.Block() as block:
        block.tensor(lambda e: [f(e) for f in p.q["te"]])
        block.vector(lambda e: [f(e) for f in p.q["ve"]])
        block.scalar(lambda e: [f(e) for f in p.q["ac"]])
        block.gpsimd(lambda e: [f(e) for f in p.q["gp"]])
        block.sync(lambda e: [f(e) for f in p.q["sy"]])
    return nc



def _host_inputs(inputs, core):
    x = np.asarray(inputs["L0_attn_hidden_states"], dtype=np.float32).reshape(16384, 4096)
    sl = slice(core * T, (core + 1) * T)
    xTc = np.ascontiguousarray(x[sl])
    xFc = np.ascontiguousarray(xTc.T)

    Wq = np.asarray(inputs["L0_attn_Wq"], np.float32)
    Wk = np.asarray(inputs["L0_attn_Wk"], np.float32)
    Wv = np.asarray(inputs["L0_attn_Wv"], np.float32)
    Wg = np.asarray(inputs["L0_attn_Wg"], np.float32)
    Win = np.asarray(inputs["L0_attn_Win"], np.float32)
    Wcat = np.concatenate([Wq, Wk, Wv, Wg, Win], axis=0)
    WcatT = np.zeros((4096, NSTRIP1 * 512), np.float32)
    WcatT[:, :10272] = Wcat.T
    wtiles = np.ascontiguousarray(
        WcatT.reshape(CT, 128, NSTRIP1, 512).transpose(2, 1, 0, 3))

    WoT = np.ascontiguousarray(np.asarray(inputs["L0_attn_Wo"], np.float32).T)
    wotiles = np.ascontiguousarray(WoT.reshape(CT, 128, 16, 256).transpose(2, 1, 0, 3))
    kern = np.asarray(inputs["L0_ffn_kernel"], np.float32)
    kerntiles = np.ascontiguousarray(kern.reshape(CT, 128, 16, 256).transpose(2, 1, 0, 3))

    loraAT = np.ascontiguousarray(
        np.asarray(inputs["L0_ffn_lora_A"], np.float32).transpose(0, 2, 1)
        .reshape(16, CT, 128, 32))
    loraBT = np.ascontiguousarray(
        np.asarray(inputs["L0_ffn_lora_B"], np.float32).transpose(0, 2, 1))
    lorasAT = np.ascontiguousarray(
        np.asarray(inputs["L0_ffn_lora_A_shared"], np.float32).T.reshape(CT, 128, 32))
    lorasBT = np.ascontiguousarray(np.asarray(inputs["L0_ffn_lora_B_shared"], np.float32).T)

    pos = (core % 4) * T + np.arange(T, dtype=np.float64)
    invf = (1.0 / 10000.0) ** (np.arange(0, 128, 2, dtype=np.float64) / 128.0)
    ang = pos[:, None] * invf[None, :]
    cos128 = np.concatenate([np.cos(ang), np.cos(ang)], axis=1).astype(np.float32)
    sin64 = np.sin(ang).astype(np.float32)
    sinS = np.concatenate([-sin64, sin64], axis=1)
    cosrep = np.ascontiguousarray(np.tile(cos128, (1, 4))).astype(ml_dtypes.bfloat16)
    sinSrep = np.ascontiguousarray(np.tile(sinS, (1, 4))).astype(ml_dtypes.bfloat16)

    identf = np.eye(128, dtype=np.float32)
    mnegf = np.tril(np.full((128, 128), MNEG_VAL, np.float32), -1)
    utf = np.triu(np.ones((128, 128), np.float32))

    dtbias_row = np.zeros((1, 512), np.float32)
    dtbias_row[0, :32] = np.asarray(inputs["L0_attn_dt_bias"], np.float32)
    alog_row = np.asarray(inputs["L0_attn_A_log_bias"], np.float32).reshape(1, 32)
    ffnbias_row = np.asarray(inputs["L0_ffn_bias"], np.float32).reshape(1, 4096)
    gate_row = np.asarray(inputs["L0_ffn_gate"], np.float32)[core].reshape(1, 16)
    gnorm_col = np.asarray(inputs["L0_attn_gnorm_w"], np.float32).reshape(128, 1)

    posn = core % 4
    am = np.zeros((128, 3), np.float32)
    bm = np.zeros((128, 96), np.float32)
    for j in range(3):
        a = 1.0 if j < posn else 0.0
        am[:, j] = a
        bm[:, j * 32:(j + 1) * 32] = 1.0 - a

    return dict(xF=xFc, xT=xTc, wtiles=wtiles, wotiles=wotiles, kerntiles=kerntiles,
                loraAT=loraAT, loraBT=loraBT, lorasAT=lorasAT, lorasBT=lorasBT,
                cosrep=cosrep, sinSrep=sinSrep, identf=identf, mnegf=mnegf, utf=utf,
                dtbias_row=dtbias_row, alog_row=alog_row, ffnbias_row=ffnbias_row,
                gate_row=gate_row, gnorm_col=gnorm_col, amask=am, bmask_bc=bm)


_RESULTS = None


def kernel(**inputs):
    global _RESULTS
    nc = build_nc(dumps=_DUMPS, phases=_PHASES)
    in_maps = [_host_inputs(inputs, c) for c in range(8)]
    res = run_bass_kernel_spmd(nc, in_maps, core_ids=list(range(8)))
    _RESULTS = res
    out = np.concatenate([res.results[c]["y_out"] for c in range(8)], axis=0)
    return out.reshape(2, 8192, 4096)


# version 04
# speedup vs baseline: 1.0091x; 1.0091x over previous
import numpy as np
import ml_dtypes
from contextlib import ExitStack

import concourse.bass as bass
import concourse.mybir as mybir
from concourse.bass_utils import run_bass_kernel_spmd

F32 = mybir.dt.float32
BF16 = mybir.dt.bfloat16
AF = mybir.ActivationFunctionType
ALU = mybir.AluOpType

T = 2048
CT = 32
H = 32
NCH = 16
THALF = 1024
NSTRIP1 = 21
MNEG_VAL = -1e5
KB = 1024

_DUMPS = ()
_PHASES = 99


class Prog:
    def __init__(self):
        self.q = {k: [] for k in ("te", "ve", "ac", "gp", "sy")}
        self.c = {}
        self.nbar = 0
        self.sems = None

    def inc(self, sem, amt=1):
        self.c[sem] = self.c.get(sem, 0) + amt
        return self.c[sem]

    def val(self, sem):
        return self.c.get(sem, 0)

    def add(self, eng, emit, waits=(), inc=None):
        waits = tuple((s, v) for (s, v) in waits if v > 0)

        def fn(e, emit=emit, waits=waits, inc=inc):
            for (s, v) in waits:
                e.wait_ge(self.sems[s], v)
            ins = emit(e)
            if inc is not None:
                s, amt = inc
                if amt is None:
                    ins.then_inc(self.sems[s])
                else:
                    ins.then_inc(self.sems[s], amt)

        fn.meta = (eng, waits, inc)
        self.q[eng].append(fn)

    def barrier(self):
        self.nbar += 1
        tgt = 5 * self.nbar
        for eng in ("te", "ve", "ac", "gp", "sy"):
            def fn(e, tgt=tgt):
                e.sem_inc(self.sems["sB"], 1)
                e.wait_ge(self.sems["sB"], tgt)
            fn.meta = (eng, (("sB", tgt),), ("sB", 1))
            fn.is_barrier = True
            self.q[eng].append(fn)


def build_nc(dumps=(), phases=99):
    nc = bass.Bass("TRN2", target_bir_lowering=False)
    p = Prog()

    xF = nc.dram_tensor("xF", [4096, T], F32, kind="ExternalInput").ap()
    xT = nc.dram_tensor("xT", [T, 4096], F32, kind="ExternalInput").ap()
    wtiles = nc.dram_tensor("wtiles", [NSTRIP1, 128, CT, 512], F32, kind="ExternalInput").ap()
    wotiles = nc.dram_tensor("wotiles", [16, 128, CT, 256], F32, kind="ExternalInput").ap()
    kerntiles = nc.dram_tensor("kerntiles", [16, 128, CT, 256], F32, kind="ExternalInput").ap()
    loraAT = nc.dram_tensor("loraAT", [16, CT, 128, 32], F32, kind="ExternalInput").ap()
    loraBT = nc.dram_tensor("loraBT", [16, 32, 4096], F32, kind="ExternalInput").ap()
    lorasAT = nc.dram_tensor("lorasAT", [CT, 128, 32], F32, kind="ExternalInput").ap()
    lorasBT = nc.dram_tensor("lorasBT", [32, 4096], F32, kind="ExternalInput").ap()
    cosrep_d = nc.dram_tensor("cosrep", [T, 512], BF16, kind="ExternalInput").ap()
    sinSrep_d = nc.dram_tensor("sinSrep", [T, 512], BF16, kind="ExternalInput").ap()
    identf_d = nc.dram_tensor("identf", [128, 128], F32, kind="ExternalInput").ap()
    mnegf_d = nc.dram_tensor("mnegf", [128, 128], F32, kind="ExternalInput").ap()
    utf_d = nc.dram_tensor("utf", [128, 128], F32, kind="ExternalInput").ap()
    dtbias_d = nc.dram_tensor("dtbias_row", [1, 512], F32, kind="ExternalInput").ap()
    alog_d = nc.dram_tensor("alog_row", [1, 32], F32, kind="ExternalInput").ap()
    ffnbias_d = nc.dram_tensor("ffnbias_row", [1, 4096], F32, kind="ExternalInput").ap()
    gate_d = nc.dram_tensor("gate_row", [1, 16], F32, kind="ExternalInput").ap()
    gnorm_d = nc.dram_tensor("gnorm_col", [128, 1], F32, kind="ExternalInput").ap()
    amask_d = nc.dram_tensor("amask", [128, 3], F32, kind="ExternalInput").ap()
    bmask_d = nc.dram_tensor("bmask_bc", [128, 96], F32, kind="ExternalInput").ap()

    y_out = nc.dram_tensor("y_out", [T, 4096], F32, kind="ExternalOutput").ap()

    q_sp = nc.dram_tensor("q_sp", [T, 4096], BF16, kind="Internal").ap()
    k_sp = nc.dram_tensor("k_sp", [T, 1024], BF16, kind="Internal").ap()
    v_sp = nc.dram_tensor("v_sp", [T, 1024], BF16, kind="Internal").ap()
    g_sp = nc.dram_tensor("g_sp", [T, 4096], BF16, kind="Internal").ap()
    y_sp = nc.dram_tensor("y_sp", [T, 4096], BF16, kind="Internal").ap()
    x1_sp = nc.dram_tensor("x1_sp", [T, 4096], BF16, kind="Internal").ap()
    gtab_sp = nc.dram_tensor("gtab_sp", [8, 128, 512], F32, kind="Internal").ap()
    lndbg = nc.dram_tensor("lndbg", [16, 128, 8], F32, kind="Internal").ap()
    zdbg = nc.dram_tensor("zdbg", [128, 4096], F32, kind="Internal").ap()
    prevs_sp = nc.dram_tensor("prevs_sp", [H, T, 128], BF16, kind="Internal").ap()
    agin = nc.dram_tensor("agin", [4224, 128], F32, kind="Internal").ap()
    agout = nc.dram_tensor("agout", [16896, 128], F32, kind="Internal").ap()
    scr = dict(q_sp=q_sp, k_sp=k_sp, v_sp=v_sp, g_sp=g_sp, y_sp=y_sp,
               x1_sp=x1_sp, gtab_sp=gtab_sp, agin=agin, agout=agout,
               lndbg=lndbg, zdbg=zdbg)
    dbg = {}
    for name in dumps:
        t = scr[name]
        dbg[name] = nc.dram_tensor("dbg_" + name, list(t.shape), t.dtype,
                                   kind="ExternalOutput").ap()

    identf = nc.alloc_sbuf_tensor("identf_b", [128, 128], F32)
    mnegf = nc.alloc_sbuf_tensor("mnegf_b", [128, 128], F32)
    utf = nc.alloc_sbuf_tensor("utf_b", [128, 128], F32)
    onesf = nc.alloc_sbuf_tensor("onesf", [128, 128], F32)
    onesrow = nc.alloc_sbuf_tensor("onesrow", [1, 128], F32)
    onesrow_b = nc.alloc_sbuf_tensor("onesrow_b", [1, 128], BF16)
    dtbias = nc.alloc_sbuf_tensor("dtbias", [1, 512], F32)
    dtbias_bf = nc.alloc_sbuf_tensor("dtbias_bf", [1, 512], BF16)
    alog = nc.alloc_sbuf_tensor("alog", [1, 32], F32)
    eA = nc.alloc_sbuf_tensor("eA", [1, 32], F32)
    gater = nc.alloc_sbuf_tensor("gater", [1, 16], F32)
    gnorm = nc.alloc_sbuf_tensor("gnorm", [128, 1], F32)
    amask = nc.alloc_sbuf_tensor("amask_b", [128, 3], F32)
    bmask = nc.alloc_sbuf_tensor("bmask_b", [128, 96], F32)
    negA = nc.alloc_sbuf_tensor("negA", [128, 32], F32)
    gateb = nc.alloc_sbuf_tensor("gateb", [128, 16], F32)
    dt_t = nc.alloc_sbuf_tensor("dt_t", [128, 512], F32)
    Dtot = nc.alloc_sbuf_tensor("Dtot", [128, 32], F32)
    Dg = nc.alloc_sbuf_tensor("Dg", [128, 32], F32)
    Dp = nc.alloc_sbuf_tensor("Dp", [128, 32], F32)
    ssqh = nc.alloc_sbuf_tensor("ssqh", [128, NCH], F32)
    rrmsh = nc.alloc_sbuf_tensor("rrmsh", [128, NCH], F32)
    lnsc = nc.alloc_sbuf_tensor("lnsc", [128, 8], F32)
    epsb = nc.alloc_sbuf_tensor("epsb", [128, 1], F32)
    rt1 = [nc.alloc_sbuf_tensor(f"rt1_{i}", [128, 512], BF16) for i in range(2)]
    rt2 = [nc.alloc_sbuf_tensor(f"rt2_{i}", [128, 512], BF16) for i in range(2)]
    rt3 = [nc.alloc_sbuf_tensor(f"rt3_{i}", [128, 512], BF16) for i in range(2)]
    dtm = nc.alloc_sbuf_tensor("dtm", [128, 32], F32)
    dtl = nc.alloc_sbuf_tensor("dtl", [128, 32], F32)
    Fh = [nc.alloc_sbuf_tensor(f"Fh{i}", [128, 128], F32) for i in range(2)]


    sqscr = nc.alloc_sbuf_tensor("sqscr", [128, 128], F32)

    tscr = nc.alloc_sbuf_tensor("tscr", [128, 128], F32)

    ARENA = 188 * KB
    arena = nc.alloc_sbuf_tensor("arena", [128, ARENA], mybir.dt.uint8)
    abase = nc.lookup_mloc(arena).addr

    def at(name, shape, dtype, off_kb):
        return nc.alloc_sbuf_tensor_at(name, list(shape), dtype,
                                       offset=abase + int(off_kb * KB))

    XFH = at("XFH", [128, CT * THALF], BF16, 0)
    wstrip = [at(f"wstrip{i}", [128, CT * 512], BF16, 64 + 32 * i) for i in range(2)]
    hstage = [at(f"hstage{i}", [128, 8 * 512], BF16, 128 + 8 * i) for i in range(2)]
    cosb = at("cosb", [128, NCH * 512], BF16, 144)
    sinb = at("sinb", [128, NCH * 512], BF16, 160)
    g_t = at("g_t", [128, 512], F32, 0)
    G_t = at("G_t", [128, 512], F32, 2)
    negG_t = at("negG_t", [128, 512], F32, 4)
    expG_t = at("expG_t", [128, 512], F32, 6)
    Gl_t = at("Gl_t", [128, 512], F32, 8)
    Saf_t = at("Saf_t", [128, 512], F32, 10)
    ed_t = at("ed_t", [128, 512], F32, 12)
    ecd_t = at("ecd_t", [128, 512], F32, 14)
    kdg_t = at("kdg_t", [128, 512], F32, 16)
    tmp_t = at("tmp_t", [128, 512], F32, 18)
    Dcum_t = at("Dcum_t", [128, 512], F32, 16)
    Sin = at("Sin", [128, H * 128], F32, 20)
    Fj = at("Fj", [128, H * 128], F32, 36)
    Yacc = at("Yacc", [128, NCH * 128], F32, 52)
    ypacc = at("ypacc", [128, NCH * 128], BF16, 60)
    qTh = [at(f"qTh{i}", [128, T], BF16, 64 + 4 * i) for i in range(2)]
    kTh = [at(f"kTh{i}", [128, T], BF16, 72 + 4 * i) for i in range(2)]
    ktok = [at(f"ktok{i}", [128, T], BF16, 80 + 4 * i) for i in range(2)]
    vtok = [at(f"vtok{i}", [128, T], BF16, 88 + 4 * i) for i in range(2)]
    gth = [at(f"gth{i}", [128, T], BF16, 96 + 4 * i) for i in range(2)]
    prevacc = [at(f"prevacc{i}", [128, NCH * 128], BF16, 104 + 4 * i) for i in range(8)]
    prevldr = [at(f"prevldr{i}", [128, NCH * 128], BF16, 136 + 4 * i) for i in range(2)]
    Sloc = [at(f"Sloc{i}", [128, 128], F32, 144 + 0.5 * i) for i in range(8)]
    kd1 = [at(f"kd1_{i}", [128, 128], BF16, 148 + 0.25 * i) for i in range(8)]
    pmb = [at(f"pmb{i}", [128, 128], BF16, 154 + 0.25 * i) for i in range(2)]
    diagp = [at(f"diagp{i}", [128, 128], F32, 150 + 0.5 * i) for i in range(2)]
    decb = [at(f"decb{i}", [128, 128], F32, 151 + 0.5 * i) for i in range(2)]
    sdtb = [at(f"sdtb{i}", [128, 128], BF16, 152 + 0.25 * i) for i in range(2)]
    zbp = [at(f"zbp{i}", [128, 128], F32, 152.5 + 0.5 * i) for i in range(2)]
    sgb = [at(f"sgb{i}", [128, 128], BF16, 153.5 + 0.25 * i) for i in range(2)]
    yT = at("yT", [128, H * THALF], BF16, 0)
    stg = at("stg", [128, 8 * 4096], BF16, 64)
    nstrip = [at(f"nstrip{i}", [128, CT * 256], BF16, 128 + 16 * i) for i in range(2)]
    zbuf = at("zbuf", [128, 4096], F32, 0)
    sqb = at("sqb", [128, 4096], F32, 16)
    xch = [at(f"xch{i}", [128, 4096], F32, 32 + 16 * i) for i in range(2)]
    x1ch = [at(f"x1ch{i}", [128, 4096], BF16, 32 + 8 * i) for i in range(2)]
    AeT32 = at("AeT32", [128, CT * 32], F32, 64)
    BeT = at("BeT", [32, 4096], F32, 68)
    bstg = at("bstg", [32, 4096], F32, 84)
    AxTb = at("AxTb", [32, THALF], BF16, 160)
    AxsTb = at("AxsTb", [64, THALF], BF16, 162)
    BeTb = at("BeTb", [32, 4096], BF16, 164)
    BsTb = at("BsTb", [64, 4096], BF16, 172)
    lstg = [at(f"lstg{i}", [128, 16 * 32], F32, 180 + 2 * i) for i in range(2)]
    AeT = at("AeT", [128, CT * 32], BF16, 184)
    AsT = at("AsT", [128, CT * 32], BF16, 186)

    bank = [nc.alloc_psum_tensor(f"bank{i}", [128, 512], F32) for i in range(8)]

    es = ExitStack()
    semnames = ["sW", "sX", "sSP", "sPE", "sAC", "sDV", "sCC", "sB", "sOUT"]
    p.sems = {n: es.enter_context(nc.semaphore(n)) for n in semnames}

    def ap3(t, a):
        return t.ap().rearrange("p (a b) -> p a b", a=a)

    for (dst, src) in [(identf, identf_d), (mnegf, mnegf_d), (utf, utf_d),
                       (dtbias, dtbias_d), (alog, alog_d), (gater, gate_d),
                       (gnorm, gnorm_d), (amask, amask_d), (bmask, bmask_d)]:
        p.inc("sX", 16)
        p.add("sy", lambda e, d=dst, s=src: e.dma_start(out=d[:, :], in_=s), (), ("sX", 16))
    p.inc("sX", 16)
    p.add("sy", lambda e: e.dma_start(out=ap3(cosb, NCH),
                                      in_=cosrep_d.rearrange("(n p) c -> p n c", p=128)),
          (), ("sX", 16))
    p.inc("sX", 16)
    p.add("sy", lambda e: e.dma_start(out=ap3(sinb, NCH),
                                      in_=sinSrep_d.rearrange("(n p) c -> p n c", p=128)),
          (), ("sX", 16))
    x0 = p.val("sX")
    p.inc("sDV")
    p.add("ve", lambda e: e.memset(onesf[:, :], 1.0), (), ("sDV", 1))
    p.inc("sDV")
    p.add("ve", lambda e: e.memset(epsb[:, :], 1e-5), (), ("sDV", 1))
    v0 = p.inc("sDV")
    p.add("ve", lambda e: e.memset(onesrow[:, :], 1.0), (), ("sDV", 1))
    a0 = p.inc("sAC")
    p.add("ac", lambda e: e.activation(eA[:, :], alog[:, :], AF.Exp), [("sX", x0)], ("sAC", 1))
    p.add("ac", lambda e: e.copy(onesrow_b[:, :], onesrow[:, :]), [("sDV", v0)])
    a1 = p.inc("sAC")
    p.add("ac", lambda e: e.copy(dtbias_bf[:, :], dtbias[:, :]), [("sX", x0)], ("sAC", 1))
    t0 = p.inc("sPE")
    p.add("te", lambda e: e.matmul(bank[2][:, 0:32], onesrow[0:1, :], eA[0:1, :],
                                   start=True, stop=True),
          [("sAC", a0), ("sDV", v0)], ("sPE", 1))
    t1 = p.inc("sPE")
    p.add("te", lambda e: e.matmul(bank[3][:, 0:16], onesrow[0:1, :], gater[0:1, :],
                                   start=True, stop=True),
          [("sX", x0), ("sDV", v0)], ("sPE", 1))
    p.add("ve", lambda e: e.tensor_scalar_mul(negA[:, :], bank[2][:, 0:32], -1.0),
          [("sPE", t0)])
    p.add("ac", lambda e: e.copy(gateb[:, :], bank[3][:, 0:16]), [("sPE", t1)])
    p.barrier()

    def strip_dest(s):
        if s < 8:
            return ("q", q_sp, s * 512, True)
        if s < 10:
            return ("k", k_sp, (s - 8) * 512, True)
        if s < 12:
            return ("v", v_sp, (s - 10) * 512, False)
        if s < 20:
            return ("g", g_sp, (s - 12) * 512, False)
        return ("dt", None, 0, False)

    XFHv = ap3(XFH, CT)
    pe_tile = {}
    last_rope_dv = [0, 0]
    last_dt_dv = [0]
    evac_done = {}
    hs_written = {}
    spill_done = {}
    pe_half = {}
    wload = {}
    xload = {}
    kglob = 0
    for half in range(2):
        w = [("sPE", pe_half.get(half - 1, 0))]
        xload[half] = p.inc("sW", 16)
        p.add("gp", lambda e, hf=half: e.dma_start(
            out=XFHv,
            in_=xF[:, hf * THALF:(hf + 1) * THALF].rearrange("(ct p) t -> p ct t", p=128)),
            w, ("sW", 16))
        for s in range(NSTRIP1):
            k = kglob
            kglob += 1
            par = k % 2
            w = []
            if k >= 2:
                pk = k - 2
                ph, ps = divmod(pk, NSTRIP1)
                w.append(("sPE", pe_tile[(ph, ps, 7)]))
            wload[k] = p.inc("sW", 16)
            p.add("gp", lambda e, s=s, par=par: e.dma_start(
                out=ap3(wstrip[par], CT), in_=wtiles[s]), w, ("sW", 16))
            kind, dest, dcol, isrope = strip_dest(s)
            for tt in range(8):
                n = half * 8 + tt
                bk = (2 + tt % 2) if kind == "dt" else (k * 8 + tt) % 2
                w = [("sW", wload[k]), ("sW", xload[half])]
                if kind == "dt":
                    prevtile = (half, s, tt - 2) if tt >= 2 else \
                        ((half - 1, s, tt + 6) if half >= 1 else None)
                else:
                    gtix = k * 8 + tt
                    if gtix >= 2:
                        pg = gtix - 2
                        pk2, ptt = divmod(pg, 8)
                        ph2, ps2 = divmod(pk2, NSTRIP1)
                        prevtile = (ph2, ps2, ptt)
                        while prevtile is not None and strip_dest(prevtile[1])[0] == "dt":
                            pg -= 1
                            if pg < 0:
                                prevtile = None
                                break
                            pk2, ptt = divmod(pg, 8)
                            ph2, ps2 = divmod(pk2, NSTRIP1)
                            prevtile = (ph2, ps2, ptt)
                    else:
                        prevtile = None
                if prevtile is not None and prevtile in evac_done:
                    w += evac_done[prevtile]

                def mk_pe(par=par, tt=tt, bk=bk, isdt=(kind == "dt")):
                    def f(e):
                        wv = ap3(wstrip[par], CT)
                        ins = None
                        for ct in range(CT):
                            ins = e.matmul(bank[bk][:, 0:512],
                                           XFHv[:, ct, tt * 128:(tt + 1) * 128],
                                           wv[:, ct, :],
                                           start=(ct == 0),
                                           stop=(ct == CT - 1 and not isdt))
                        if isdt:
                            ins = e.matmul(bank[bk][:, 0:512], onesrow_b[0:1, :],
                                           dtbias_bf[0:1, :], start=False, stop=True)
                        return ins
                    return f
                pe_tile[(half, s, tt)] = p.inc("sPE")
                p.add("te", mk_pe(), w, ("sPE", 1))

                if kind == "dt":
                    vv = p.inc("sDV")
                    p.add("ve", lambda e, bk=bk: e.tensor_scalar_max(
                        dtm[:, :], bank[bk][:, 0:32], 0.0),
                        [("sPE", pe_tile[(half, s, tt)]), ("sDV", last_dt_dv[0])],
                        ("sDV", 1))
                    dt_max_dv = vv
                    aa = p.inc("sAC")
                    p.add("ac", lambda e, bk=bk: e.activation(dtl[:, :], bank[bk][:, 0:32], AF.Abs),
                          [("sPE", pe_tile[(half, s, tt)]), ("sDV", last_dt_dv[0])],
                          ("sAC", 1))
                    evac_done[(half, s, tt)] = [("sDV", vv), ("sAC", aa)]
                    p.add("ac", lambda e: e.activation(dtl[:, :], dtl[:, :], AF.Exp, scale=-1.0))
                    aa2 = p.inc("sAC")
                    p.add("ac", lambda e: e.activation(dtl[:, :], dtl[:, :], AF.Ln, bias=1.0),
                          (), ("sAC", 1))
                    last_dt_dv[0] = p.inc("sDV")
                    p.add("ve", lambda e, n=n: e.tensor_add(
                        dt_t[:, n * 32:(n + 1) * 32], dtm[:, :], dtl[:, :]),
                        [("sAC", aa2), ("sDV", dt_max_dv)], ("sDV", 1))
                    continue

                wA = [("sPE", pe_tile[(half, s, tt)])]
                if (k - 2) in spill_done:
                    wA.append(("sSP", spill_done[k - 2]))
                if isrope:
                    rp_ = (k * 8 + tt) % 2
                    if last_rope_dv[rp_] > 0:
                        wA.append(("sDV", last_rope_dv[rp_]))
                    aa = p.inc("sAC")
                    p.add("ac", lambda e, bk=bk, rp_=rp_: e.copy(rt1[rp_][:, :], bank[bk][:, 0:512]),
                          wA, ("sAC", 1))
                    evac_done[(half, s, tt)] = [("sAC", aa)]
                    r1 = rt1[rp_].ap().rearrange("p (h s d) -> p h s d", h=4, s=2)
                    r2 = rt2[rp_].ap().rearrange("p (h s d) -> p h s d", h=4, s=2)
                    p.add("ve", lambda e, n=n, rp_=rp_: e.tensor_mul(
                        rt3[rp_][:, :], rt1[rp_][:, :], ap3(cosb, NCH)[:, n, :]),
                        [("sAC", aa), ("sX", x0)])
                    p.add("ve", lambda e, n=n, r1=r1, r2=r2: e.tensor_mul(
                        r2[:, :, 0, :], r1[:, :, 1, :],
                        ap3(sinb, NCH)[:, n, :].rearrange("p (h s d) -> p h s d", h=4, s=2)[:, :, 0, :]))
                    vhi = p.inc("sDV")
                    p.add("ve", lambda e, n=n, r1=r1, r2=r2: e.tensor_mul(
                        r2[:, :, 1, :], r1[:, :, 0, :],
                        ap3(sinb, NCH)[:, n, :].rearrange("p (h s d) -> p h s d", h=4, s=2)[:, :, 1, :]),
                        (), ("sDV", 1))
                    vv = p.inc("sDV")
                    p.add("ve", lambda e, par=par, tt=tt, rp_=rp_: e.tensor_add(
                        hstage[par][:, tt * 512:(tt + 1) * 512], rt3[rp_][:, :], rt2[rp_][:, :]),
                        [("sDV", vhi)], ("sDV", 1))
                    last_rope_dv[rp_] = vv
                    hs_written[(half, s)] = ("sDV", vv)
                else:
                    aa = p.inc("sAC")
                    p.add("ac", lambda e, bk=bk, par=par, tt=tt: e.copy(
                        hstage[par][:, tt * 512:(tt + 1) * 512], bank[bk][:, 0:512]),
                        wA, ("sAC", 1))
                    evac_done[(half, s, tt)] = [("sAC", aa)]
                    hs_written[(half, s)] = ("sAC", aa)
            if kind != "dt":
                spill_done[k] = p.inc("sSP", 16)
                p.add("sy", lambda e, par=par, dest=dest, dcol=dcol, hf=half: e.dma_start(
                    out=dest[hf * THALF:(hf + 1) * THALF, dcol:dcol + 512]
                        .rearrange("(tt p) o -> p tt o", p=128),
                    in_=ap3(hstage[par], 8)),
                    [hs_written[(half, s)]], ("sSP", 16))
        pe_half[half] = p.val("sPE")
    p.barrier()
    if phases <= 1:
        return _finish(nc, p, dbg, scr, dumps)

    vg = None
    for n in range(NCH):
        inc = ("sDV", 1) if n == NCH - 1 else None
        p.add("ve", lambda e, n=n: e.tensor_mul(g_t[:, n * 32:(n + 1) * 32],
                                                dt_t[:, n * 32:(n + 1) * 32], negA[:, :]),
              (), inc)
    vg = p.inc("sDV")
    for n in range(NCH):
        p.add("te", lambda e, n=n: e.matmul(bank[4][:, n * 32:(n + 1) * 32], utf[:, :],
                                            g_t[:, n * 32:(n + 1) * 32], start=True, stop=True),
              [("sDV", vg)] if n == 0 else ())
        p.add("te", lambda e, n=n: e.matmul(bank[5][:, n * 32:(n + 1) * 32], onesf[:, :],
                                            g_t[:, n * 32:(n + 1) * 32], start=True, stop=True),
              (), ("sPE", 1) if n == NCH - 1 else None)
        if n == NCH - 1:
            pass
    tGall = p.inc("sPE")
    aG = p.inc("sAC")
    p.add("ac", lambda e: e.copy(G_t[:, :], bank[4][:, :]), [("sPE", tGall)], ("sAC", 1))
    aGl = p.inc("sAC")
    p.add("ac", lambda e: e.copy(Gl_t[:, :], bank[5][:, :]), (), ("sAC", 1))
    vsf = p.inc("sDV")
    p.add("ve", lambda e: e.memset(Saf_t[:, 15 * 32:16 * 32], 0.0), [("sAC", aGl)], ("sDV", 1))
    for n in range(14, -1, -1):
        wv = [("sDV", vsf)]
        vsf = p.inc("sDV")
        p.add("ve", lambda e, n=n: e.tensor_add(Saf_t[:, n * 32:(n + 1) * 32],
                                                Saf_t[:, (n + 1) * 32:(n + 2) * 32],
                                                Gl_t[:, (n + 1) * 32:(n + 2) * 32]),
              wv, ("sDV", 1))
    p.add("ve", lambda e: e.tensor_sub(tmp_t[:, :], Gl_t[:, :], G_t[:, :]), [("sAC", aG)])
    vS = p.inc("sDV")
    p.add("ve", lambda e: e.tensor_scalar_mul(negG_t[:, :], G_t[:, :], -1.0), (), ("sDV", 1))
    p.add("ac", lambda e: e.activation(ed_t[:, :], tmp_t[:, :], AF.Exp), [("sDV", vS)])
    p.add("ac", lambda e: e.activation(ecd_t[:, :], Gl_t[:, :], AF.Exp))
    aE = p.inc("sAC")
    p.add("ac", lambda e: e.activation(expG_t[:, :], G_t[:, :], AF.Exp), (), ("sAC", 1))

    vpf = p.inc("sDV")
    p.add("ve", lambda e: e.memset(Dcum_t[:, 0:32], 0.0), (), ("sDV", 1))
    for n in range(1, NCH):
        wv = [("sDV", vpf)]
        vpf = p.inc("sDV")
        p.add("ve", lambda e, n=n: e.tensor_add(Dcum_t[:, n * 32:(n + 1) * 32],
                                                Dcum_t[:, (n - 1) * 32:n * 32],
                                                Gl_t[:, (n - 1) * 32:n * 32]),
              wv, ("sDV", 1))
    aDc = p.inc("sAC")
    p.add("ac", lambda e: e.activation(Dcum_t[:, :], Dcum_t[:, :], AF.Exp),
          [("sDV", vpf)], ("sAC", 1))
    vD = p.inc("sDV")
    p.add("ve", lambda e: e.tensor_add(Dg[:, :], Saf_t[:, 0:32], Gl_t[:, 0:32]),
          [("sDV", vsf)], ("sDV", 1))
    aD = p.inc("sAC")
    p.add("ac", lambda e: e.activation(Dtot[:, :], Dg[:, :], AF.Exp), [("sDV", vD)], ("sAC", 1))
    aLn = p.inc("sAC")
    p.add("ac", lambda e: e.activation(g_t[:, :], dt_t[:, :], AF.Ln),
          [("sDV", p.val("sDV"))], ("sAC", 1))
    vNG = p.inc("sDV")
    p.add("ve", lambda e: e.scalar_tensor_tensor(negG_t[:, :], g_t[:, :], 1.0,
                                                 negG_t[:, :], ALU.mult, ALU.add),
          [("sAC", aLn)], ("sDV", 1))
    vED = p.inc("sDV")
    p.add("ve", lambda e: e.tensor_mul(tmp_t[:, :], ed_t[:, :], dt_t[:, :]),
          [("sDV", vNG)], ("sDV", 1))
    tabs = [dt_t, g_t, G_t, Gl_t, Saf_t, ed_t, ecd_t, Dcum_t]
    wtab = [("sAC", p.val("sAC")), ("sDV", p.val("sDV"))]
    for i, tb in enumerate(tabs):
        p.inc("sSP", 16)
        p.add("sy", lambda e, i=i, tb=tb: e.dma_start(out=gtab_sp[i], in_=tb[:, :]),
              wtab, ("sSP", 16))
    p.barrier()
    if phases <= 2:
        return _finish(nc, p, dbg, scr, dumps)

    edt_t = tmp_t
    evF = {}
    dmaF = {}
    spP = {}
    lastC_per = {}
    tCm = {}
    for grp in range(4):
        w = []
        if grp >= 1:
            w.append(("sPE", lastC_per[(grp - 1, "all")]))
        kvl = p.inc("sX", 64)
        for kk in range(2):
            kv = 2 * grp + kk
            p.add("sy", lambda e, kv=kv, kk=kk: e.dma_start(
                out=ap3(ktok[kk], NCH),
                in_=k_sp[:, kv * 128:(kv + 1) * 128].rearrange("(n p) d -> p n d", p=128)),
                w if kk == 0 else (), ("sX", 16))
            p.add("sy", lambda e, kv=kv, kk=kk: e.dma_start(
                out=ap3(vtok[kk], NCH),
                in_=v_sp[:, kv * 128:(kv + 1) * 128].rearrange("(n p) d -> p n d", p=128)),
                (), ("sX", 16))
        vz0 = {}
        for hm in range(8):
            h = 8 * grp + hm
            w = []
            if (grp - 1, hm) in spP:
                w.append(("sSP", spP[(grp - 1, hm)]))
            if grp >= 1:
                w.append(("sAC", evF[8 * (grp - 1) + hm]))
            p.add("ve", lambda e, hm=hm: e.memset(Sloc[hm][:, :], 0.0), w)
            vz0[hm] = p.inc("sDV")
            p.add("ve", lambda e, hm=hm: e.memset(prevacc[hm][:, 0:128], 0.0), (), ("sDV", 1))
        vst1 = {}
        vcst = {}
        for n in range(NCH):
            for hm in range(8):
                h = 8 * grp + hm
                kk = hm // 4
                col = n * 32 + h
                w = [("sX", kvl), ("sDV", vz0[hm])]
                if (grp, hm, n - 2) in tCm:
                    w.append(("sPE", tCm[(grp, hm, n - 2)]))
                elif grp >= 1 and n < 2:
                    w.append(("sPE", tCm[(grp - 1, hm, NCH - 2 + n)]))
                vk = p.inc("sDV")
                p.add("ve", lambda e, n=n, hm=hm, kk=kk, col=col: e.tensor_scalar_mul(
                    kd1[hm][:, :], ap3(ktok[kk], NCH)[:, n, :], edt_t[:, col:col + 1]),
                    w, ("sDV", 1))
                w = [("sDV", vk)]
                if (n - 1, hm) in vst1:
                    w.append(("sDV", vst1[(n - 1, hm)]))
                tCm[(grp, hm, n)] = p.inc("sPE")
                p.add("te", lambda e, hm=hm, kk=kk, n=n: e.matmul(
                    bank[hm][:, 0:128], kd1[hm][:, :], ap3(vtok[kk], NCH)[:, n, :],
                    start=True, stop=True), w, ("sPE", 1))
                vst1[(n, hm)] = p.inc("sDV")
                p.add("ve", lambda e, hm=hm, col=col: e.scalar_tensor_tensor(
                    Sloc[hm][:, :], Sloc[hm][:, :], ecd_t[:, col:col + 1],
                    bank[hm][:, 0:128], ALU.mult, ALU.add),
                    [("sPE", tCm[(grp, hm, n)])], ("sDV", 1))
                if n < NCH - 1:
                    vcst[(n, hm)] = p.inc("sDV")
                    p.add("ve", lambda e, hm=hm, n=n: e.tensor_copy(
                        prevacc[hm][:, (n + 1) * 128:(n + 2) * 128], Sloc[hm][:, :]),
                        [("sDV", vst1[(n, hm)])], ("sDV", 1))
        lastC_per[(grp, "all")] = p.val("sPE")
        for hm in range(8):
            h = 8 * grp + hm
            w = [("sDV", vst1[(NCH - 1, hm)])]
            if h >= 2:
                w.append(("sSP", dmaF.get(h - 2, 0)))
                w.append(("sAC", evF.get(h - 2, 0)))
            evF[h] = p.inc("sAC")
            p.add("ac", lambda e, hm=hm: e.copy(Fh[hm % 2][:, :], Sloc[hm][:, :]),
                  w, ("sAC", 1))
            dmaF[h] = p.inc("sSP", 16)
            p.add("sy", lambda e, h=h, hm=hm: e.dma_start(
                out=agin[h * 128:(h + 1) * 128, :], in_=Fh[hm % 2][:, :]),
                [("sAC", evF[h])], ("sSP", 16))
            spP[(grp, hm)] = p.inc("sSP", 16)
            p.add("sy", lambda e, h=h, hm=hm: e.dma_start(
                out=prevs_sp[h].rearrange("(n p) d -> p n d", p=128),
                in_=ap3(prevacc[hm], NCH)),
                [("sDV", vcst[(NCH - 2, hm)])], ("sSP", 16))
    p.inc("sSP", 16)
    p.add("sy", lambda e: e.dma_start(out=agin[4096:4224, 0:32], in_=Dtot[:, :]),
          [("sAC", aD)], ("sSP", 16))
    p.add("gp", lambda e: e.collective_compute(
        "AllGather", ALU.bypass, replica_groups=[[0, 1, 2, 3], [4, 5, 6, 7]],
        ins=[agin.opt()], outs=[agout.opt()]),
        [("sSP", p.val("sSP"))], ("sCC", None))
    if phases <= 2.9:
        p.add("gp", lambda e: e.wait_ge(p.sems["sCC"], 1))
        p.barrier()
        return _finish(nc, p, dbg, scr, dumps)
    p.inc("sDV")
    p.add("ve", lambda e: e.memset(Sin[:, :], 0.0), (), ("sDV", 1))
    for j in range(3):
        w = [("sCC", 1), ("sDV", p.val("sDV"))]
        xj = p.inc("sX", 32)
        p.add("sy", lambda e, j=j: e.dma_start(
            out=ap3(Fj, H),
            in_=agout[j * 4224:j * 4224 + 4096, :].rearrange("(h p) e -> p h e", p=128)),
            w, ("sX", 16))
        p.add("sy", lambda e, j=j: e.dma_start(
            out=Dg[:, :], in_=agout[j * 4224 + 4096:j * 4224 + 4224, 0:32]), (), ("sX", 16))
        p.add("ve", lambda e, j=j: e.scalar_tensor_tensor(
            Dp[:, :], Dg[:, :], amask[:, j:j + 1], bmask[:, j * 32:(j + 1) * 32],
            ALU.mult, ALU.add), [("sX", xj)])
        for h in range(H):
            vts = p.inc("sDV")
            p.add("ve", lambda e, h=h: e.tensor_scalar_mul(
                tscr[:, :], Sin[:, h * 128:(h + 1) * 128], Dp[:, h:h + 1]), (), ("sDV", 1))
            p.add("ve", lambda e, h=h, j=j: e.scalar_tensor_tensor(
                Sin[:, h * 128:(h + 1) * 128], ap3(Fj, H)[:, h, :], amask[:, j:j + 1],
                tscr[:, :], ALU.mult, ALU.add), [("sDV", vts)],
                ("sDV", 1) if h == H - 1 else None)
        p.inc("sDV")
    p.barrier()
    if phases <= 3:
        return _finish(nc, p, dbg, scr, dumps)

    bS = [3, 0]; bR = [4, 2]; bY = [5, 1]; bZ = [6, 7]
    ld = {}
    lastuse = {}
    spillY = {}
    prev_vyp15 = [0]
    last_SDt = {0: 0, 1: 0}
    last_exp = {0: 0, 1: 0}
    last_Y = {0: 0, 1: 0}
    last_yc = {0: 0, 1: 0}
    last_zb = {0: 0, 1: 0}
    last_rowb = {0: 0, 1: 0}
    last_ZMM = {0: 0, 1: 0}
    last_yp = {0: 0, 1: 0}
    for h in range(H):
        kv = h // 4
        par = h % 2
        kpar = kv % 2
        w = [lastuse[("q", h - 2)]] if ("q", h - 2) in lastuse else []
        ld[("q", h)] = p.inc("sX", 16)
        p.add("sy", lambda e, h=h, par=par: e.dma_start_transpose(
            out=qTh[par][:, :], in_=q_sp[:, h * 128:(h + 1) * 128]), w, ("sX", 16))
        w = [lastuse[("g", h - 2)]] if ("g", h - 2) in lastuse else []
        ld[("g", h)] = p.inc("sX", 16)
        p.add("sy", lambda e, h=h, par=par: e.dma_start(
            out=ap3(gth[par], NCH),
            in_=g_sp[:, h * 128:(h + 1) * 128].rearrange("(n p) d -> p n d", p=128)),
            w, ("sX", 16))
        w = [lastuse[("pv", h - 2)]] if ("pv", h - 2) in lastuse else []
        ld[("pv", h)] = p.inc("sX", 16)
        p.add("sy", lambda e, h=h, par=par: e.dma_start(
            out=ap3(prevldr[par], NCH),
            in_=prevs_sp[h].rearrange("(n p) d -> p n d", p=128)), w, ("sX", 16))
        if h % 4 == 0:
            w = [lastuse[("kv", kv - 2)]] if ("kv", kv - 2) in lastuse else []
            ld[("kT", kv)] = p.inc("sX", 16)
            p.add("sy", lambda e, kv=kv, kpar=kpar: e.dma_start_transpose(
                out=kTh[kpar][:, :], in_=k_sp[:, kv * 128:(kv + 1) * 128]), w, ("sX", 16))
            ld[("vtok", kv)] = p.inc("sX", 16)
            p.add("sy", lambda e, kv=kv, kpar=kpar: e.dma_start(
                out=ap3(vtok[kpar], NCH),
                in_=v_sp[:, kv * 128:(kv + 1) * 128].rearrange("(n p) d -> p n d", p=128)),
                (), ("sX", 16))
        vSDt = {}; vzb = {}; vyc = {}; vpm = {}
        tS = {}; tR = {}; tY = {}; tZ = {}; aexp = {}
        for n in range(NCH):
            npar = n % 2
            col = n * 32 + h
            w = [("sX", ld[("pv", h)])]
            if last_ZMM[npar] > 0:
                w.append(("sPE", last_ZMM[npar]))
            vpm[n] = p.inc("sDV")
            p.add("ve", lambda e, n=n, h=h, par=par, npar=npar, col=col: e.scalar_tensor_tensor(
                pmb[npar][:, :], Sin[:, h * 128:(h + 1) * 128], Dcum_t[:, col:col + 1],
                ap3(prevldr[par], NCH)[:, n, :], ALU.mult, ALU.add), w, ("sDV", 1))
            w = [("sX", ld[("q", h)]), ("sX", ld[("kT", kv)])]
            if last_SDt[npar] > 0:
                w.append(("sDV", last_SDt[npar]))
            tS[n] = p.inc("sPE")
            p.add("te", lambda e, n=n, par=par, kpar=kpar, npar=npar: e.matmul(
                bank[bS[npar]][:, 0:128], kTh[kpar][:, n * 128:(n + 1) * 128],
                qTh[par][:, n * 128:(n + 1) * 128], start=True, stop=True), w, ("sPE", 1))
            w = []
            if last_rowb[npar] > 0:
                w.append(("sPE", last_rowb[npar]))
            vdg = p.inc("sDV")
            p.add("ve", lambda e, col=col, npar=npar: e.tensor_scalar_mul(
                diagp[npar][:, :], identf[:, :], G_t[:, col:col + 1]), w, ("sDV", 1))
            w = [("sDV", vdg)]
            if last_exp[npar] > 0:
                w.append(("sAC", last_exp[npar]))

            def mk_rowb(npar=npar):
                def f(e):
                    e.matmul(bank[bR[npar]][:, 0:128], onesf[:, :], diagp[npar][:, :],
                             start=True, stop=False)
                    return e.matmul(bank[bR[npar]][:, 0:128], identf[:, :], mnegf[:, :],
                                    start=False, stop=True)
                return f
            tR[n] = last_rowb[npar] = p.inc("sPE")
            p.add("te", mk_rowb(), w, ("sPE", 1))
            w = [("sPE", tR[n])]
            if last_SDt[npar] > 0:
                w.append(("sDV", last_SDt[npar]))
            aexp[n] = last_exp[npar] = p.inc("sAC")
            p.add("ac", lambda e, col=col, npar=npar: e.activation(
                decb[npar][:, :], bank[bR[npar]][:, 0:128], AF.Exp,
                bias=negG_t[:, col:col + 1]), w, ("sAC", 1))
            w = [("sPE", tS[n]), ("sAC", aexp[n])]
            if last_Y[npar] > 0:
                w.append(("sPE", last_Y[npar]))
            vSDt[n] = last_SDt[npar] = p.inc("sDV")
            p.add("ve", lambda e, npar=npar: e.tensor_mul(
                sdtb[npar][:, :], bank[bS[npar]][:, 0:128], decb[npar][:, :]), w, ("sDV", 1))
            w = [("sDV", vSDt[n]), ("sX", ld[("vtok", kv)])]
            if last_yc[npar] > 0:
                w.append(("sDV", last_yc[npar]))
            tY[n] = last_Y[npar] = p.inc("sPE")
            p.add("te", lambda e, n=n, kpar=kpar, npar=npar: e.matmul(
                bank[bY[npar]][:, 0:128], sdtb[npar][:, :],
                ap3(vtok[kpar], NCH)[:, n, :], start=True, stop=True), w, ("sPE", 1))
            w = [("sDV", vpm[n]), ("sX", ld[("q", h)])]
            if last_zb[npar] > 0:
                w.append(("sDV", last_zb[npar]))
            tZ[n] = last_ZMM[npar] = p.inc("sPE")
            p.add("te", lambda e, n=n, par=par, npar=npar: e.matmul(
                bank[bZ[npar]][:, 0:128], qTh[par][:, n * 128:(n + 1) * 128],
                pmb[npar][:, :], start=True, stop=True), w, ("sPE", 1))
            w = [("sPE", tZ[n])]
            if last_yc[npar] > 0:
                w.append(("sDV", last_yc[npar]))
            vzb[n] = last_zb[npar] = p.inc("sDV")
            p.add("ve", lambda e, col=col, npar=npar: e.tensor_scalar_mul(
                zbp[npar][:, :], bank[bZ[npar]][:, 0:128], expG_t[:, col:col + 1]),
                w, ("sDV", 1))
            vyc[n] = last_yc[npar] = p.inc("sDV")
            p.add("ve", lambda e, n=n, npar=npar: e.tensor_add(
                Yacc[:, n * 128:(n + 1) * 128], zbp[npar][:, :], bank[bY[npar]][:, 0:128]),
                [("sPE", tY[n]), ("sDV", vzb[n])], ("sDV", 1))
            p.inc("sAC")
            p.add("ac", lambda e, n=n: e.activation(sqscr[:, :], Yacc[:, n * 128:(n + 1) * 128],
                                                    AF.Square, accum_out=ssqh[:, n:n + 1]),
                  [("sDV", vyc[n])], ("sAC", 1))
        lastuse[("q", h)] = ("sPE", max(tS[NCH - 1], tZ[NCH - 1]))
        lastuse[("kv", kv)] = ("sPE", max(tS[NCH - 1], tY[NCH - 1]))
        lastuse[("pv", h)] = ("sDV", vpm[NCH - 1])
        lastuse[("g", h)] = None
        arr = p.inc("sAC", 2)
        wrr = [("sDV", prev_vyp15[0])] if prev_vyp15[0] > 0 else []
        p.add("ac", lambda e: e.activation(rrmsh[:, :], ssqh[:, :], AF.Ln,
                                           bias=epsb[:, 0:1], scale=1.0 / 128), wrr)
        p.add("ac", lambda e: e.activation(rrmsh[:, :], rrmsh[:, :], AF.Exp, scale=-0.5),
              (), ("sAC", 2))
        vyp = {}
        for n in range(NCH):
            npar = n % 2
            w = [("sX", ld[("g", h)]), ("sAC", arr)]
            if n == 0 and (h - 1) in spillY:
                w.append(("sSP", spillY[h - 1]))
            if last_yp[npar] > 0:
                w.append(("sDV", last_yp[npar]))
            asg = p.inc("sAC")
            p.add("ac", lambda e, n=n, par=par, npar=npar: e.activation(
                sgb[npar][:, :], ap3(gth[par], NCH)[:, n, :], AF.Silu), w, ("sAC", 1))
            p.add("ve", lambda e, n=n: e.tensor_scalar_mul(
                tscr[:, :], Yacc[:, n * 128:(n + 1) * 128], rrmsh[:, n:n + 1]),
                [("sAC", asg)])
            vyp[n] = last_yp[npar] = p.inc("sDV")
            p.add("ve", lambda e, n=n, npar=npar: e.tensor_mul(
                ypacc[:, n * 128:(n + 1) * 128], tscr[:, :], sgb[npar][:, :]), (), ("sDV", 1))
        prev_vyp15[0] = vyp[NCH - 1]
        lastuse[("g", h)] = ("sAC", p.val("sAC"))
        spillY[h] = p.inc("sSP", 16)
        p.add("sy", lambda e, h=h: e.dma_start(
            out=y_sp[:, h * 128:(h + 1) * 128].rearrange("(n p) d -> p n d", p=128),
            in_=ap3(ypacc, NCH)), [("sDV", vyp[NCH - 1])], ("sSP", 16))
    p.barrier()
    if phases <= 4:
        return _finish(nc, p, dbg, scr, dumps)

    yTv = ap3(yT, H)
    stgv = ap3(stg, 8)

    epi_state = {"aa": 0, "outdma": 0}

    def ln_epilogue(hf, tt, tpar, strips_done, residual_load, out_emit, out_inc):
        lx = p.inc("sX", 16)
        wl = [strips_done]
        if tt >= 2:
            wl.append(("sDV", p.val("sDV")))
        p.add("sy", residual_load, wl, ("sX", 16))
        wz = [("sX", lx), strips_done]
        if epi_state["aa"] > 0:
            wz.append(("sAC", epi_state["aa"]))
        vz = p.inc("sDV")
        p.add("ve", lambda e, tt=tt, tpar=tpar: e.tensor_add(
            zbuf[:, :], (xch if out_inc == "sSP" else x1ch)[tpar][:, :], stgv[:, tt, :]),
            wz, ("sDV", 1))
        aa = p.inc("sAC", 2)
        epi_state["aa"] = aa
        wacc = [("sDV", vz)]
        if out_inc == "sOUT" and epi_state["outdma"] > 0:
            wacc.append(("sOUT", epi_state["outdma"]))
        p.add("ac", lambda e: e.activation(sqb[:, :], zbuf[:, :], AF.Copy,
                                           accum_out=lnsc[:, 0:1]), wacc)
        p.add("ac", lambda e: e.activation(sqb[:, :], zbuf[:, :], AF.Square,
                                           accum_out=lnsc[:, 1:2]), (), ("sAC", 2))
        vmean = p.inc("sDV")
        p.add("ve", lambda e: e.tensor_scalar_mul(lnsc[:, 2:3], lnsc[:, 0:1], 1.0 / 4096),
              [("sAC", aa)], ("sDV", 1))
        am = p.inc("sAC", 2)
        p.add("ac", lambda e: e.activation(lnsc[:, 3:4], lnsc[:, 0:1], AF.Square,
                                           scale=1.0 / 4096))
        p.add("ac", lambda e: e.activation(lnsc[:, 4:5], lnsc[:, 1:2], AF.Copy,
                                           scale=1.0 / 4096), (), ("sAC", 2))
        vv = p.inc("sDV")
        p.add("ve", lambda e: e.tensor_sub(lnsc[:, 5:6], lnsc[:, 4:5], lnsc[:, 3:4]),
              [("sAC", am)], ("sDV", 1))
        ar = p.inc("sAC", 2)
        p.add("ac", lambda e: e.activation(lnsc[:, 6:7], lnsc[:, 5:6], AF.Ln, bias=epsb[:, 0:1]),
              [("sDV", vv)])
        p.add("ac", lambda e: e.activation(lnsc[:, 7:8], lnsc[:, 6:7], AF.Exp, scale=-0.5),
              (), ("sAC", 2))
        vo = p.inc("sDV")
        p.add("ve", out_emit, [("sAC", ar), ("sDV", vmean)], ("sDV", 1))
        nv = p.inc(out_inc, 16)
        if out_inc == "sOUT":
            epi_state["outdma"] = nv
        return vo

    pe_prev = None
    epi_done = None
    for hf in range(2):
        w = []
        if epi_done is not None:
            w.append(epi_done)
        for ht in range(H):
            p.inc("sX", 16)
            p.add("sy", lambda e, hf=hf, ht=ht: e.dma_start_transpose(
                out=yTv[:, ht, :],
                in_=y_sp[hf * THALF:(hf + 1) * THALF, ht * 128:(ht + 1) * 128]),
                w if ht == 0 else (), ("sX", 16))
        ldy = p.val("sX")
        wl = {}
        scl = {}
        pe_t = {}
        ev_t = {}
        for s in range(16):
            par = s % 2
            w = []
            if s >= 2:
                w.append(("sPE", pe_t[(s - 2, 7)]))
            elif pe_prev is not None:
                w.append(("sPE", pe_prev[(14 + s, 7)]))
            wl[s] = p.inc("sW", 16)
            p.add("gp", lambda e, s=s, par=par: e.dma_start(
                out=ap3(nstrip[par], CT), in_=wotiles[s]), w, ("sW", 16))
            scl[s] = p.inc("sDV")
            p.add("ve", lambda e, par=par: e.tensor_scalar_mul(
                nstrip[par][:, :], nstrip[par][:, :], gnorm[:, 0:1]),
                [("sW", wl[s])], ("sDV", 1))
            for tt in range(8):
                bk = (s * 8 + tt) % 2
                w = [("sDV", scl[s]), ("sX", ldy)]
                prev = (s, tt - 2) if tt >= 2 else ((s - 1, tt + 6) if s >= 1 else None)
                if prev in ev_t:
                    w.append(("sAC", ev_t[prev]))

                def mkp(par=par, tt=tt, bk=bk):
                    def f(e):
                        sv = ap3(nstrip[par], CT)
                        ins = None
                        for ht in range(H):
                            ins = e.matmul(bank[bk][:, 0:256],
                                           yTv[:, ht, tt * 128:(tt + 1) * 128],
                                           sv[:, ht, :], start=(ht == 0), stop=(ht == H - 1))
                        return ins
                    return f
                pe_t[(s, tt)] = p.inc("sPE")
                p.add("te", mkp(), w, ("sPE", 1))
                ev_t[(s, tt)] = p.inc("sAC")
                p.add("ac", lambda e, s=s, tt=tt, bk=bk: e.copy(
                    stgv[:, tt, s * 256:(s + 1) * 256], bank[bk][:, 0:256]),
                    [("sPE", pe_t[(s, tt)])], ("sAC", 1))
        pe_prev = pe_t
        strips_done = ("sAC", p.val("sAC"))
        for tt in range(8):
            tpar = tt % 2
            vo = ln_epilogue(
                hf, tt, tpar, strips_done,
                residual_load=lambda e, hf=hf, tt=tt, tpar=tpar: e.dma_start(
                    out=xch[tpar][:, :],
                    in_=xT[hf * THALF + tt * 128:hf * THALF + (tt + 1) * 128, :]),
                out_emit=lambda e, tt=tt: e.tensor_scalar(
                    stgv[:, tt, :], zbuf[:, :], lnsc[:, 2:3], lnsc[:, 7:8],
                    ALU.subtract, ALU.mult),
                out_inc="sSP")
            p.add("sy", lambda e, hf=hf, tt=tt: e.dma_start(
                out=x1_sp[hf * THALF + tt * 128:hf * THALF + (tt + 1) * 128, :],
                in_=stgv[:, tt, :]), [("sDV", vo)], ("sSP", 16))
        epi_done = ("sSP", p.val("sSP"))
    p.barrier()
    if phases <= 5:
        return _finish(nc, p, dbg, scr, dumps)

    x1Fv = ap3(yT, CT)
    vAct = {}
    for ct in range(CT):
        par = ct % 2
        w = []
        if ct >= 2:
            w.append(("sDV", vAct[ct - 2]))
        lx = p.inc("sX", 16)
        p.add("sy", lambda e, ct=ct, par=par: e.dma_start(
            out=ap3(lstg[par], 16),
            in_=loraAT[:, ct, :, :].rearrange("n p r -> p n r")), w, ("sX", 16))
        vs = p.inc("sDV")
        p.add("ve", lambda e, ct=ct, par=par: e.tensor_scalar_mul(
            AeT32[:, ct * 32:(ct + 1) * 32], ap3(lstg[par], 16)[:, 0, :], gateb[:, 0:1]),
            [("sX", lx)], ("sDV", 1))
        for nn in range(1, 16):
            wv = [("sDV", vs)]
            vs = p.inc("sDV")
            p.add("ve", lambda e, ct=ct, par=par, nn=nn: e.scalar_tensor_tensor(
                AeT32[:, ct * 32:(ct + 1) * 32], ap3(lstg[par], 16)[:, nn, :],
                gateb[:, nn:nn + 1], AeT32[:, ct * 32:(ct + 1) * 32], ALU.mult, ALU.add),
                wv, ("sDV", 1))
        vAct[ct] = p.inc("sDV")
        p.add("ve", lambda e, ct=ct: e.tensor_copy(AeT[:, ct * 32:(ct + 1) * 32],
                                                   AeT32[:, ct * 32:(ct + 1) * 32]),
              [("sDV", vs)], ("sDV", 1))
    vb = 0
    for nn in range(16):
        lx = p.inc("sX", 16)
        w = [("sDV", vb)] if vb else []
        p.add("sy", lambda e, nn=nn: e.dma_start(out=bstg[:, :], in_=loraBT[nn]),
              w, ("sX", 16))
        vbprev = vb
        vb = p.inc("sDV")
        if nn == 0:
            p.add("ve", lambda e: e.tensor_scalar_mul(BeT[:, :], bstg[:, :], gateb[0:32, 0:1]),
                  [("sX", lx)], ("sDV", 1))
        else:
            p.add("ve", lambda e, nn=nn: e.scalar_tensor_tensor(
                BeT[:, :], bstg[:, :], gateb[0:32, nn:nn + 1], BeT[:, :], ALU.mult, ALU.add),
                [("sX", lx), ("sDV", vbprev)], ("sDV", 1))
    vbe = p.inc("sDV")
    p.add("ve", lambda e: e.tensor_copy(BeTb[:, :], BeT[:, :]), [("sDV", vb)], ("sDV", 1))
    lsh = p.inc("sW", 32)
    p.add("gp", lambda e: e.dma_start(out=ap3(AsT, CT),
                                      in_=lorasAT.rearrange("ct p r -> p ct r")),
          (), ("sW", 16))
    p.add("gp", lambda e: e.dma_start(out=BsTb[0:32, :], in_=lorasBT), (), ("sW", 16))
    vz1 = p.inc("sDV")
    p.add("ve", lambda e: e.memset(BsTb[32:64, :], 0.0), (), ("sDV", 1))
    lfb = p.inc("sW", 16)
    p.add("gp", lambda e: e.dma_start(out=BsTb[32:33, :], in_=ffnbias_d),
          [("sDV", vz1)], ("sW", 16))
    pre_dv = p.val("sDV")

    pe_prev = None
    epi_done = None
    for hf in range(2):
        w = [("sDV", pre_dv)]
        if epi_done is not None:
            w.append(epi_done)
        for ctb in range(CT):
            p.inc("sX", 16)
            p.add("sy", lambda e, hf=hf, ctb=ctb: e.dma_start_transpose(
                out=x1Fv[:, ctb, :],
                in_=x1_sp[hf * THALF:(hf + 1) * THALF, ctb * 128:(ctb + 1) * 128]),
                w if ctb == 0 else (), ("sX", 16))
        ldx = p.val("sX")
        aax = {}
        for blk in range(2):
            w = [("sX", ldx)]
            if blk == 0:
                w += [("sDV", vAct[CT - 1]), ("sW", lsh)]
            else:
                w.append(("sAC", aax[0]))

            def mkax(blk=blk):
                def f(e):
                    for ct in range(CT):
                        e.matmul(bank[2][0:32, 0:512], AeT[:, ct * 32:(ct + 1) * 32],
                                 x1Fv[:, ct, blk * 512:(blk + 1) * 512],
                                 start=(ct == 0), stop=(ct == CT - 1))
                    ins = None
                    for ct in range(CT):
                        ins = e.matmul(bank[3][0:32, 0:512], AsT[:, ct * 32:(ct + 1) * 32],
                                       x1Fv[:, ct, blk * 512:(blk + 1) * 512],
                                       start=(ct == 0), stop=(ct == CT - 1))
                    return ins
                return f
            tax = p.inc("sPE")
            p.add("te", mkax(), w, ("sPE", 1))
            aax[blk] = p.inc("sAC", 2)
            p.add("ac", lambda e, blk=blk: e.copy(AxTb[:, blk * 512:(blk + 1) * 512],
                                                  bank[2][0:32, 0:512]), [("sPE", tax)])
            p.add("ac", lambda e, blk=blk: e.copy(AxsTb[0:32, blk * 512:(blk + 1) * 512],
                                                  bank[3][0:32, 0:512]), (), ("sAC", 2))
        p.inc("sDV")
        p.add("ve", lambda e: e.memset(AxsTb[32:64, :], 0.0), [("sAC", aax[1])], ("sDV", 1))
        vones = p.inc("sDV")
        p.add("ve", lambda e: e.memset(AxsTb[32:33, :], 1.0), (), ("sDV", 1))
        wl = {}
        pe_t = {}
        ev_t = {}
        for s in range(16):
            par = s % 2
            w = []
            if s >= 2:
                w.append(("sPE", pe_t[(s - 2, 7)]))
            elif pe_prev is not None:
                w.append(("sPE", pe_prev[(14 + s, 7)]))
            wl[s] = p.inc("sW", 16)
            p.add("gp", lambda e, s=s, par=par: e.dma_start(
                out=ap3(nstrip[par], CT), in_=kerntiles[s]), w, ("sW", 16))
            for tt in range(8):
                bk = (s * 8 + tt) % 2
                w = [("sW", wl[s]), ("sX", ldx), ("sDV", vones), ("sDV", vbe)]
                prev = (s, tt - 2) if tt >= 2 else ((s - 1, tt + 6) if s >= 1 else None)
                if prev in ev_t:
                    w.append(("sAC", ev_t[prev]))

                def mkp(par=par, tt=tt, bk=bk, s=s):
                    def f(e):
                        sv = ap3(nstrip[par], CT)
                        for ct in range(CT):
                            e.matmul(bank[bk][:, 0:256], x1Fv[:, ct, tt * 128:(tt + 1) * 128],
                                     sv[:, ct, :], start=(ct == 0), stop=False)
                        e.matmul(bank[bk][:, 0:256], AxTb[0:32, tt * 128:(tt + 1) * 128],
                                 BeTb[0:32, s * 256:(s + 1) * 256], start=False, stop=False)
                        return e.matmul(bank[bk][:, 0:256],
                                        AxsTb[0:64, tt * 128:(tt + 1) * 128],
                                        BsTb[0:64, s * 256:(s + 1) * 256],
                                        start=False, stop=True)
                    return f
                pe_t[(s, tt)] = p.inc("sPE")
                p.add("te", mkp(), w, ("sPE", 1))
                ev_t[(s, tt)] = p.inc("sAC")
                p.add("ac", lambda e, s=s, tt=tt, bk=bk: e.copy(
                    stgv[:, tt, s * 256:(s + 1) * 256], bank[bk][:, 0:256]),
                    [("sPE", pe_t[(s, tt)])], ("sAC", 1))
        pe_prev = pe_t
        strips_done = ("sAC", p.val("sAC"))
        for tt in range(8):
            tpar = tt % 2
            vo = ln_epilogue(
                hf, tt, tpar, strips_done,
                residual_load=lambda e, hf=hf, tt=tt, tpar=tpar: e.dma_start(
                    out=x1ch[tpar][:, :],
                    in_=x1_sp[hf * THALF + tt * 128:hf * THALF + (tt + 1) * 128, :]),
                out_emit=lambda e: e.tensor_scalar(
                    sqb[:, :], zbuf[:, :], lnsc[:, 2:3], lnsc[:, 7:8],
                    ALU.subtract, ALU.mult),
                out_inc="sOUT")
            p.add("gp", lambda e, hf=hf, tt=tt: e.dma_start(
                out=y_out[hf * THALF + tt * 128:hf * THALF + (tt + 1) * 128, :],
                in_=sqb[:, :]), [("sDV", vo)], ("sOUT", 16))
        epi_done = ("sOUT", p.val("sOUT"))

    return _finish(nc, p, dbg, scr, dumps)


def _finish(nc, p, dbg, scr, dumps):
    for name in dumps:
        wv = [("sSP", p.val("sSP")), ("sDV", p.val("sDV")), ("sAC", p.val("sAC")),
              ("sPE", p.val("sPE")), ("sOUT", p.val("sOUT"))]
        p.inc("sOUT", 16)
        p.add("gp", lambda e, name=name: e.dma_start(out=dbg[name], in_=scr[name]),
              wv, ("sOUT", 16))
    tgt = p.val("sOUT")
    if tgt > 0:
        def fin(e, tgt=tgt):
            e.wait_ge(p.sems["sOUT"], tgt)
        fin.meta = ("gp", (("sOUT", tgt),), None)
        p.q["gp"].append(fin)

    with nc.Block() as block:
        block.tensor(lambda e: [f(e) for f in p.q["te"]])
        block.vector(lambda e: [f(e) for f in p.q["ve"]])
        block.scalar(lambda e: [f(e) for f in p.q["ac"]])
        block.gpsimd(lambda e: [f(e) for f in p.q["gp"]])
        block.sync(lambda e: [f(e) for f in p.q["sy"]])
    return nc



def _host_inputs(inputs, core):
    x = np.asarray(inputs["L0_attn_hidden_states"], dtype=np.float32).reshape(16384, 4096)
    sl = slice(core * T, (core + 1) * T)
    xTc = np.ascontiguousarray(x[sl])
    xFc = np.ascontiguousarray(xTc.T)

    Wq = np.asarray(inputs["L0_attn_Wq"], np.float32)
    Wk = np.asarray(inputs["L0_attn_Wk"], np.float32)
    Wv = np.asarray(inputs["L0_attn_Wv"], np.float32)
    Wg = np.asarray(inputs["L0_attn_Wg"], np.float32)
    Win = np.asarray(inputs["L0_attn_Win"], np.float32)
    Wcat = np.concatenate([Wq, Wk, Wv, Wg, Win], axis=0)
    WcatT = np.zeros((4096, NSTRIP1 * 512), np.float32)
    WcatT[:, :10272] = Wcat.T
    wtiles = np.ascontiguousarray(
        WcatT.reshape(CT, 128, NSTRIP1, 512).transpose(2, 1, 0, 3))

    WoT = np.ascontiguousarray(np.asarray(inputs["L0_attn_Wo"], np.float32).T)
    wotiles = np.ascontiguousarray(WoT.reshape(CT, 128, 16, 256).transpose(2, 1, 0, 3))
    kern = np.asarray(inputs["L0_ffn_kernel"], np.float32)
    kerntiles = np.ascontiguousarray(kern.reshape(CT, 128, 16, 256).transpose(2, 1, 0, 3))

    loraAT = np.ascontiguousarray(
        np.asarray(inputs["L0_ffn_lora_A"], np.float32).transpose(0, 2, 1)
        .reshape(16, CT, 128, 32))
    loraBT = np.ascontiguousarray(
        np.asarray(inputs["L0_ffn_lora_B"], np.float32).transpose(0, 2, 1))
    lorasAT = np.ascontiguousarray(
        np.asarray(inputs["L0_ffn_lora_A_shared"], np.float32).T.reshape(CT, 128, 32))
    lorasBT = np.ascontiguousarray(np.asarray(inputs["L0_ffn_lora_B_shared"], np.float32).T)

    pos = (core % 4) * T + np.arange(T, dtype=np.float64)
    invf = (1.0 / 10000.0) ** (np.arange(0, 128, 2, dtype=np.float64) / 128.0)
    ang = pos[:, None] * invf[None, :]
    cos128 = np.concatenate([np.cos(ang), np.cos(ang)], axis=1).astype(np.float32)
    sin64 = np.sin(ang).astype(np.float32)
    sinS = np.concatenate([-sin64, sin64], axis=1)
    cosrep = np.ascontiguousarray(np.tile(cos128, (1, 4))).astype(ml_dtypes.bfloat16)
    sinSrep = np.ascontiguousarray(np.tile(sinS, (1, 4))).astype(ml_dtypes.bfloat16)

    identf = np.eye(128, dtype=np.float32)
    mnegf = np.tril(np.full((128, 128), MNEG_VAL, np.float32), -1)
    utf = np.triu(np.ones((128, 128), np.float32))

    dtbias_row = np.zeros((1, 512), np.float32)
    dtbias_row[0, :32] = np.asarray(inputs["L0_attn_dt_bias"], np.float32)
    alog_row = np.asarray(inputs["L0_attn_A_log_bias"], np.float32).reshape(1, 32)
    ffnbias_row = np.asarray(inputs["L0_ffn_bias"], np.float32).reshape(1, 4096)
    gate_row = np.asarray(inputs["L0_ffn_gate"], np.float32)[core].reshape(1, 16)
    gnorm_col = np.asarray(inputs["L0_attn_gnorm_w"], np.float32).reshape(128, 1)

    posn = core % 4
    am = np.zeros((128, 3), np.float32)
    bm = np.zeros((128, 96), np.float32)
    for j in range(3):
        a = 1.0 if j < posn else 0.0
        am[:, j] = a
        bm[:, j * 32:(j + 1) * 32] = 1.0 - a

    return dict(xF=xFc, xT=xTc, wtiles=wtiles, wotiles=wotiles, kerntiles=kerntiles,
                loraAT=loraAT, loraBT=loraBT, lorasAT=lorasAT, lorasBT=lorasBT,
                cosrep=cosrep, sinSrep=sinSrep, identf=identf, mnegf=mnegf, utf=utf,
                dtbias_row=dtbias_row, alog_row=alog_row, ffnbias_row=ffnbias_row,
                gate_row=gate_row, gnorm_col=gnorm_col, amask=am, bmask_bc=bm)


_RESULTS = None


def kernel(**inputs):
    global _RESULTS
    nc = build_nc(dumps=_DUMPS, phases=_PHASES)
    in_maps = [_host_inputs(inputs, c) for c in range(8)]
    res = run_bass_kernel_spmd(nc, in_maps, core_ids=list(range(8)))
    _RESULTS = res
    out = np.concatenate([res.results[c]["y_out"] for c in range(8)], axis=0)
    return out.reshape(2, 8192, 4096)


# version 05
# speedup vs baseline: 1.1376x; 1.1273x over previous
import numpy as np
import ml_dtypes
from contextlib import ExitStack

import concourse.bass as bass
import concourse.mybir as mybir
from concourse.bass_utils import run_bass_kernel_spmd

F32 = mybir.dt.float32
BF16 = mybir.dt.bfloat16
AF = mybir.ActivationFunctionType
ALU = mybir.AluOpType

T = 2048
CT = 32
H = 32
NCH = 16
THALF = 1024
NSTRIP1 = 21
MNEG_VAL = -1e5
KB = 1024

_DUMPS = ()
_PHASES = 99


class Prog:
    def __init__(self):
        self.q = {k: [] for k in ("te", "ve", "ac", "gp", "sy")}
        self.c = {}
        self.nbar = 0
        self.sems = None

    def inc(self, sem, amt=1):
        self.c[sem] = self.c.get(sem, 0) + amt
        return self.c[sem]

    def val(self, sem):
        return self.c.get(sem, 0)

    def add(self, eng, emit, waits=(), inc=None):
        waits = tuple((s, v) for (s, v) in waits if v > 0)

        def fn(e, emit=emit, waits=waits, inc=inc):
            for (s, v) in waits:
                e.wait_ge(self.sems[s], v)
            ins = emit(e)
            if inc is not None:
                s, amt = inc
                if amt is None:
                    ins.then_inc(self.sems[s])
                else:
                    ins.then_inc(self.sems[s], amt)

        fn.meta = (eng, waits, inc)
        self.q[eng].append(fn)

    def barrier(self):
        self.nbar += 1
        tgt = 5 * self.nbar
        for eng in ("te", "ve", "ac", "gp", "sy"):
            def fn(e, tgt=tgt):
                e.sem_inc(self.sems["sB"], 1)
                e.wait_ge(self.sems["sB"], tgt)
            fn.meta = (eng, (("sB", tgt),), ("sB", 1))
            fn.is_barrier = True
            self.q[eng].append(fn)


def build_nc(dumps=(), phases=99):
    nc = bass.Bass("TRN2", target_bir_lowering=False)
    p = Prog()

    xF = nc.dram_tensor("xF", [4096, T], F32, kind="ExternalInput").ap()
    xT = nc.dram_tensor("xT", [T, 4096], F32, kind="ExternalInput").ap()
    wtiles = nc.dram_tensor("wtiles", [NSTRIP1, 128, CT, 512], F32, kind="ExternalInput").ap()
    wotiles = nc.dram_tensor("wotiles", [16, 128, CT, 256], F32, kind="ExternalInput").ap()
    kerntiles = nc.dram_tensor("kerntiles", [16, 128, CT, 256], F32, kind="ExternalInput").ap()
    loraAT = nc.dram_tensor("loraAT", [16, 128, CT * 32], F32, kind="ExternalInput").ap()
    loraBT = nc.dram_tensor("loraBT", [16, 32, 4096], F32, kind="ExternalInput").ap()
    lorasAT = nc.dram_tensor("lorasAT", [CT, 128, 32], F32, kind="ExternalInput").ap()
    lorasBT = nc.dram_tensor("lorasBT", [32, 4096], F32, kind="ExternalInput").ap()
    cosrep_d = nc.dram_tensor("cosrep", [T, 512], BF16, kind="ExternalInput").ap()
    sinSrep_d = nc.dram_tensor("sinSrep", [T, 512], BF16, kind="ExternalInput").ap()
    identf_d = nc.dram_tensor("identf", [128, 128], F32, kind="ExternalInput").ap()
    mnegf_d = nc.dram_tensor("mnegf", [128, 128], F32, kind="ExternalInput").ap()
    utf_d = nc.dram_tensor("utf", [128, 128], F32, kind="ExternalInput").ap()
    dtbias_d = nc.dram_tensor("dtbias_row", [1, 512], F32, kind="ExternalInput").ap()
    alog_d = nc.dram_tensor("alog_row", [1, 32], F32, kind="ExternalInput").ap()
    ffnbias_d = nc.dram_tensor("ffnbias_row", [1, 4096], F32, kind="ExternalInput").ap()
    gate_d = nc.dram_tensor("gate_row", [1, 16], F32, kind="ExternalInput").ap()
    gnorm_d = nc.dram_tensor("gnorm_col", [128, 1], F32, kind="ExternalInput").ap()
    amask_d = nc.dram_tensor("amask", [128, 3], F32, kind="ExternalInput").ap()
    bmask_d = nc.dram_tensor("bmask_bc", [128, 96], F32, kind="ExternalInput").ap()

    y_out = nc.dram_tensor("y_out", [T, 4096], F32, kind="ExternalOutput").ap()

    q_sp = nc.dram_tensor("q_sp", [T, 4096], BF16, kind="Internal").ap()
    k_sp = nc.dram_tensor("k_sp", [T, 1024], BF16, kind="Internal").ap()
    v_sp = nc.dram_tensor("v_sp", [T, 1024], BF16, kind="Internal").ap()
    g_sp = nc.dram_tensor("g_sp", [T, 4096], BF16, kind="Internal").ap()
    y_sp = nc.dram_tensor("y_sp", [T, 4096], BF16, kind="Internal").ap()
    x1_sp = nc.dram_tensor("x1_sp", [T, 4096], BF16, kind="Internal").ap()
    gtab_sp = nc.dram_tensor("gtab_sp", [8, 128, 512], F32, kind="Internal").ap()
    lndbg = nc.dram_tensor("lndbg", [16, 128, 8], F32, kind="Internal").ap()
    zdbg = nc.dram_tensor("zdbg", [128, 4096], F32, kind="Internal").ap()
    prevs_sp = nc.dram_tensor("prevs_sp", [H, T, 128], BF16, kind="Internal").ap()
    agin = nc.dram_tensor("agin", [4224, 128], F32, kind="Internal").ap()
    agout = nc.dram_tensor("agout", [16896, 128], F32, kind="Internal").ap()
    scr = dict(q_sp=q_sp, k_sp=k_sp, v_sp=v_sp, g_sp=g_sp, y_sp=y_sp,
               x1_sp=x1_sp, gtab_sp=gtab_sp, agin=agin, agout=agout,
               lndbg=lndbg, zdbg=zdbg)
    dbg = {}
    for name in dumps:
        t = scr[name]
        dbg[name] = nc.dram_tensor("dbg_" + name, list(t.shape), t.dtype,
                                   kind="ExternalOutput").ap()

    identf = nc.alloc_sbuf_tensor("identf_b", [128, 128], F32)
    mnegf = nc.alloc_sbuf_tensor("mnegf_b", [128, 128], F32)
    utf = nc.alloc_sbuf_tensor("utf_b", [128, 128], F32)
    onesf = nc.alloc_sbuf_tensor("onesf", [128, 128], F32)
    onesrow = nc.alloc_sbuf_tensor("onesrow", [1, 128], F32)
    onesrow_b = nc.alloc_sbuf_tensor("onesrow_b", [1, 128], BF16)
    dtbias = nc.alloc_sbuf_tensor("dtbias", [1, 512], F32)
    dtbias_bf = nc.alloc_sbuf_tensor("dtbias_bf", [1, 512], BF16)
    alog = nc.alloc_sbuf_tensor("alog", [1, 32], F32)
    eA = nc.alloc_sbuf_tensor("eA", [1, 32], F32)
    gater = nc.alloc_sbuf_tensor("gater", [1, 16], F32)
    gnorm = nc.alloc_sbuf_tensor("gnorm", [128, 1], F32)
    amask = nc.alloc_sbuf_tensor("amask_b", [128, 3], F32)
    bmask = nc.alloc_sbuf_tensor("bmask_b", [128, 96], F32)
    negA = nc.alloc_sbuf_tensor("negA", [128, 32], F32)
    gateb = nc.alloc_sbuf_tensor("gateb", [128, 16], F32)
    dt_t = nc.alloc_sbuf_tensor("dt_t", [128, 512], F32)
    Dtot = nc.alloc_sbuf_tensor("Dtot", [128, 32], F32)
    Dg = nc.alloc_sbuf_tensor("Dg", [128, 32], F32)
    Dp = nc.alloc_sbuf_tensor("Dp", [128, 32], F32)
    ssqh = nc.alloc_sbuf_tensor("ssqh", [128, NCH], F32)
    rrmsh = nc.alloc_sbuf_tensor("rrmsh", [128, NCH], F32)
    lnsc = nc.alloc_sbuf_tensor("lnsc", [128, 8], F32)
    epsb = nc.alloc_sbuf_tensor("epsb", [128, 1], F32)
    rt1 = [nc.alloc_sbuf_tensor(f"rt1_{i}", [128, 512], BF16) for i in range(2)]
    rt2 = [nc.alloc_sbuf_tensor(f"rt2_{i}", [128, 512], BF16) for i in range(2)]
    rt3 = [nc.alloc_sbuf_tensor(f"rt3_{i}", [128, 512], BF16) for i in range(2)]
    dtm = nc.alloc_sbuf_tensor("dtm", [128, 32], F32)
    dtl = nc.alloc_sbuf_tensor("dtl", [128, 32], F32)
    Fh = [nc.alloc_sbuf_tensor(f"Fh{i}", [128, 128], F32) for i in range(2)]


    sqscr = nc.alloc_sbuf_tensor("sqscr", [128, 128], F32)

    tscr = nc.alloc_sbuf_tensor("tscr", [128, 128], F32)

    ARENA = 188 * KB
    arena = nc.alloc_sbuf_tensor("arena", [128, ARENA], mybir.dt.uint8)
    abase = nc.lookup_mloc(arena).addr

    def at(name, shape, dtype, off_kb):
        return nc.alloc_sbuf_tensor_at(name, list(shape), dtype,
                                       offset=abase + int(off_kb * KB))

    XFH = at("XFH", [128, CT * THALF], BF16, 0)
    wstrip = [at(f"wstrip{i}", [128, CT * 512], BF16, 64 + 32 * i) for i in range(2)]
    hstage = [at(f"hstage{i}", [128, 8 * 512], BF16, 128 + 8 * i) for i in range(2)]
    cosb = at("cosb", [128, NCH * 512], BF16, 144)
    sinb = at("sinb", [128, NCH * 512], BF16, 160)
    g_t = at("g_t", [128, 512], F32, 0)
    G_t = at("G_t", [128, 512], F32, 2)
    negG_t = at("negG_t", [128, 512], F32, 4)
    expG_t = at("expG_t", [128, 512], F32, 6)
    Gl_t = at("Gl_t", [128, 512], F32, 8)
    Saf_t = at("Saf_t", [128, 512], F32, 10)
    ed_t = at("ed_t", [128, 512], F32, 12)
    ecd_t = at("ecd_t", [128, 512], F32, 14)
    kdg_t = at("kdg_t", [128, 512], F32, 16)
    tmp_t = at("tmp_t", [128, 512], F32, 18)
    Dcum_t = at("Dcum_t", [128, 512], F32, 16)
    Sin = at("Sin", [128, H * 128], F32, 20)
    Fj = at("Fj", [128, H * 128], F32, 36)
    Yacc = at("Yacc", [128, NCH * 128], F32, 52)
    ypacc = at("ypacc", [128, NCH * 128], BF16, 60)
    qTh = [at(f"qTh{i}", [128, T], BF16, 64 + 4 * i) for i in range(2)]
    kTh = [at(f"kTh{i}", [128, T], BF16, 72 + 4 * i) for i in range(2)]
    ktok = [at(f"ktok{i}", [128, T], BF16, 80 + 4 * i) for i in range(2)]
    vtok = [at(f"vtok{i}", [128, T], BF16, 88 + 4 * i) for i in range(2)]
    gth = [at(f"gth{i}", [128, T], BF16, 96 + 4 * i) for i in range(2)]
    prevacc = [at(f"prevacc{i}", [128, NCH * 128], BF16, 104 + 4 * i) for i in range(8)]
    prevldr = [at(f"prevldr{i}", [128, NCH * 128], BF16, 136 + 4 * i) for i in range(2)]
    Sloc = [at(f"Sloc{i}", [128, 128], F32, 144 + 0.5 * i) for i in range(8)]
    kd1 = [at(f"kd1_{i}", [128, 128], BF16, 148 + 0.25 * i) for i in range(8)]
    pmw = [at(f"pmw{i}", [128, 512], BF16, 154 + 1 * i) for i in range(2)]
    decw = [at(f"decw{i}", [128, 512], F32, 156 + 2 * i) for i in range(2)]
    sdtw = [at(f"sdtw{i}", [128, 512], BF16, 160 + 1 * i) for i in range(2)]
    zbw = [at(f"zbw{i}", [128, 512], F32, 162 + 2 * i) for i in range(2)]
    diagw = [at(f"diagw{i}", [128, 512], F32, 166 + 2 * i) for i in range(2)]
    dnegw = [at(f"dnegw{i}", [128, 512], F32, 170 + 2 * i) for i in range(2)]
    mneg4 = at("mneg4", [128, 512], F32, 174)
    diagp = [at(f"diagp{i}", [128, 128], F32, 150 + 0.5 * i) for i in range(2)]
    decb = [at(f"decb{i}", [128, 128], F32, 151 + 0.5 * i) for i in range(2)]
    sdtb = [at(f"sdtb{i}", [128, 128], BF16, 152 + 0.25 * i) for i in range(2)]
    zbp = [at(f"zbp{i}", [128, 128], F32, 152.5 + 0.5 * i) for i in range(2)]
    sgb = [at(f"sgb{i}", [128, 128], BF16, 153.5 + 0.25 * i) for i in range(2)]
    yT = at("yT", [128, H * THALF], BF16, 0)
    stg = at("stg", [128, 8 * 4096], BF16, 64)
    nstrip = [at(f"nstrip{i}", [128, CT * 256], BF16, 128 + 16 * i) for i in range(2)]
    zbuf = at("zbuf", [128, 4096], F32, 0)
    sqb = at("sqb", [128, 4096], F32, 16)
    xch = [at(f"xch{i}", [128, 4096], F32, 32 + 16 * i) for i in range(2)]
    x1ch = [at(f"x1ch{i}", [128, 4096], BF16, 32 + 8 * i) for i in range(2)]
    AeT32 = at("AeT32", [128, CT * 32], F32, 64)
    BeT = at("BeT", [32, 4096], F32, 68)
    bstg = at("bstg", [32, 4096], F32, 84)
    AxTb = at("AxTb", [32, THALF], BF16, 160)
    AxsTb = at("AxsTb", [64, THALF], BF16, 162)
    BeTb = at("BeTb", [32, 4096], BF16, 164)
    BsTb = at("BsTb", [64, 4096], BF16, 172)
    lstg = [at("lstg0", [128, CT * 32], F32, 180), at("lstg0b", [128, CT * 32], F32, 180)]
    AeT = at("AeT", [128, CT * 32], BF16, 184)
    AsT = at("AsT", [128, CT * 32], BF16, 186)

    bank = [nc.alloc_psum_tensor(f"bank{i}", [128, 512], F32) for i in range(8)]

    es = ExitStack()
    semnames = ["sW", "sX", "sSP", "sPE", "sAC", "sDV", "sCC", "sB", "sOUT"]
    p.sems = {n: es.enter_context(nc.semaphore(n)) for n in semnames}

    def ap3(t, a):
        return t.ap().rearrange("p (a b) -> p a b", a=a)

    for (dst, src) in [(identf, identf_d), (mnegf, mnegf_d), (utf, utf_d),
                       (dtbias, dtbias_d), (alog, alog_d), (gater, gate_d),
                       (gnorm, gnorm_d), (amask, amask_d), (bmask, bmask_d)]:
        p.inc("sX", 16)
        p.add("sy", lambda e, d=dst, s=src: e.dma_start(out=d[:, :], in_=s), (), ("sX", 16))
    p.inc("sX", 16)
    p.add("sy", lambda e: e.dma_start(out=ap3(cosb, NCH),
                                      in_=cosrep_d.rearrange("(n p) c -> p n c", p=128)),
          (), ("sX", 16))
    p.inc("sX", 16)
    p.add("sy", lambda e: e.dma_start(out=ap3(sinb, NCH),
                                      in_=sinSrep_d.rearrange("(n p) c -> p n c", p=128)),
          (), ("sX", 16))
    x0 = p.val("sX")
    p.inc("sDV")
    p.add("ve", lambda e: e.memset(onesf[:, :], 1.0), (), ("sDV", 1))
    p.inc("sDV")
    p.add("ve", lambda e: e.memset(epsb[:, :], 1e-5), (), ("sDV", 1))
    v0 = p.inc("sDV")
    p.add("ve", lambda e: e.memset(onesrow[:, :], 1.0), (), ("sDV", 1))
    a0 = p.inc("sAC")
    p.add("ac", lambda e: e.activation(eA[:, :], alog[:, :], AF.Exp), [("sX", x0)], ("sAC", 1))
    p.add("ac", lambda e: e.copy(onesrow_b[:, :], onesrow[:, :]), [("sDV", v0)])
    a1 = p.inc("sAC")
    p.add("ac", lambda e: e.copy(dtbias_bf[:, :], dtbias[:, :]), [("sX", x0)], ("sAC", 1))
    t0 = p.inc("sPE")
    p.add("te", lambda e: e.matmul(bank[2][:, 0:32], onesrow[0:1, :], eA[0:1, :],
                                   start=True, stop=True),
          [("sAC", a0), ("sDV", v0)], ("sPE", 1))
    t1 = p.inc("sPE")
    p.add("te", lambda e: e.matmul(bank[3][:, 0:16], onesrow[0:1, :], gater[0:1, :],
                                   start=True, stop=True),
          [("sX", x0), ("sDV", v0)], ("sPE", 1))
    p.add("ve", lambda e: e.tensor_scalar_mul(negA[:, :], bank[2][:, 0:32], -1.0),
          [("sPE", t0)])
    p.add("ac", lambda e: e.copy(gateb[:, :], bank[3][:, 0:16]), [("sPE", t1)])
    p.barrier()

    def strip_dest(s):
        if s < 8:
            return ("q", q_sp, s * 512, True)
        if s < 10:
            return ("k", k_sp, (s - 8) * 512, True)
        if s < 12:
            return ("v", v_sp, (s - 10) * 512, False)
        if s < 20:
            return ("g", g_sp, (s - 12) * 512, False)
        return ("dt", None, 0, False)

    XFHv = ap3(XFH, CT)
    pe_tile = {}
    last_rope_dv = [0, 0]
    last_dt_dv = [0]
    evac_done = {}
    hs_written = {}
    spill_done = {}
    pe_half = {}
    wload = {}
    xload = {}
    kglob = 0
    for half in range(2):
        w = [("sPE", pe_half.get(half - 1, 0))]
        xload[half] = p.inc("sW", 16)
        p.add("gp", lambda e, hf=half: e.dma_start(
            out=XFHv,
            in_=xF[:, hf * THALF:(hf + 1) * THALF].rearrange("(ct p) t -> p ct t", p=128)),
            w, ("sW", 16))
        for s in range(NSTRIP1):
            k = kglob
            kglob += 1
            par = k % 2
            w = []
            if k >= 2:
                pk = k - 2
                ph, ps = divmod(pk, NSTRIP1)
                w.append(("sPE", pe_tile[(ph, ps, 7)]))
            wload[k] = p.inc("sW", 16)
            p.add("gp", lambda e, s=s, par=par: e.dma_start(
                out=ap3(wstrip[par], CT), in_=wtiles[s]), w, ("sW", 16))
            kind, dest, dcol, isrope = strip_dest(s)
            for tt in range(8):
                n = half * 8 + tt
                bk = (2 + tt % 2) if kind == "dt" else (k * 8 + tt) % 2
                w = [("sW", wload[k]), ("sW", xload[half])]
                if kind == "dt":
                    prevtile = (half, s, tt - 2) if tt >= 2 else \
                        ((half - 1, s, tt + 6) if half >= 1 else None)
                else:
                    gtix = k * 8 + tt
                    if gtix >= 2:
                        pg = gtix - 2
                        pk2, ptt = divmod(pg, 8)
                        ph2, ps2 = divmod(pk2, NSTRIP1)
                        prevtile = (ph2, ps2, ptt)
                        while prevtile is not None and strip_dest(prevtile[1])[0] == "dt":
                            pg -= 1
                            if pg < 0:
                                prevtile = None
                                break
                            pk2, ptt = divmod(pg, 8)
                            ph2, ps2 = divmod(pk2, NSTRIP1)
                            prevtile = (ph2, ps2, ptt)
                    else:
                        prevtile = None
                if prevtile is not None and prevtile in evac_done:
                    w += evac_done[prevtile]

                def mk_pe(par=par, tt=tt, bk=bk, isdt=(kind == "dt")):
                    def f(e):
                        wv = ap3(wstrip[par], CT)
                        ins = None
                        for ct in range(CT):
                            ins = e.matmul(bank[bk][:, 0:512],
                                           XFHv[:, ct, tt * 128:(tt + 1) * 128],
                                           wv[:, ct, :],
                                           start=(ct == 0),
                                           stop=(ct == CT - 1 and not isdt))
                        if isdt:
                            ins = e.matmul(bank[bk][:, 0:512], onesrow_b[0:1, :],
                                           dtbias_bf[0:1, :], start=False, stop=True)
                        return ins
                    return f
                pe_tile[(half, s, tt)] = p.inc("sPE")
                p.add("te", mk_pe(), w, ("sPE", 1))

                if kind == "dt":
                    vv = p.inc("sDV")
                    p.add("ve", lambda e, bk=bk: e.tensor_scalar_max(
                        dtm[:, :], bank[bk][:, 0:32], 0.0),
                        [("sPE", pe_tile[(half, s, tt)]), ("sDV", last_dt_dv[0])],
                        ("sDV", 1))
                    dt_max_dv = vv
                    aa = p.inc("sAC")
                    p.add("ac", lambda e, bk=bk: e.activation(dtl[:, :], bank[bk][:, 0:32], AF.Abs),
                          [("sPE", pe_tile[(half, s, tt)]), ("sDV", last_dt_dv[0])],
                          ("sAC", 1))
                    evac_done[(half, s, tt)] = [("sDV", vv), ("sAC", aa)]
                    p.add("ac", lambda e: e.activation(dtl[:, :], dtl[:, :], AF.Exp, scale=-1.0))
                    aa2 = p.inc("sAC")
                    p.add("ac", lambda e: e.activation(dtl[:, :], dtl[:, :], AF.Ln, bias=1.0),
                          (), ("sAC", 1))
                    last_dt_dv[0] = p.inc("sDV")
                    p.add("ve", lambda e, n=n: e.tensor_add(
                        dt_t[:, n * 32:(n + 1) * 32], dtm[:, :], dtl[:, :]),
                        [("sAC", aa2), ("sDV", dt_max_dv)], ("sDV", 1))
                    continue

                wA = [("sPE", pe_tile[(half, s, tt)])]
                if (k - 2) in spill_done:
                    wA.append(("sSP", spill_done[k - 2]))
                if isrope:
                    rp_ = (k * 8 + tt) % 2
                    if last_rope_dv[rp_] > 0:
                        wA.append(("sDV", last_rope_dv[rp_]))
                    aa = p.inc("sAC")
                    p.add("ac", lambda e, bk=bk, rp_=rp_: e.copy(rt1[rp_][:, :], bank[bk][:, 0:512]),
                          wA, ("sAC", 1))
                    evac_done[(half, s, tt)] = [("sAC", aa)]
                    r1 = rt1[rp_].ap().rearrange("p (h s d) -> p h s d", h=4, s=2)
                    r2 = rt2[rp_].ap().rearrange("p (h s d) -> p h s d", h=4, s=2)
                    p.add("ve", lambda e, n=n, rp_=rp_: e.tensor_mul(
                        rt3[rp_][:, :], rt1[rp_][:, :], ap3(cosb, NCH)[:, n, :]),
                        [("sAC", aa), ("sX", x0)])
                    p.add("ve", lambda e, n=n, r1=r1, r2=r2: e.tensor_mul(
                        r2[:, :, 0, :], r1[:, :, 1, :],
                        ap3(sinb, NCH)[:, n, :].rearrange("p (h s d) -> p h s d", h=4, s=2)[:, :, 0, :]))
                    vhi = p.inc("sDV")
                    p.add("ve", lambda e, n=n, r1=r1, r2=r2: e.tensor_mul(
                        r2[:, :, 1, :], r1[:, :, 0, :],
                        ap3(sinb, NCH)[:, n, :].rearrange("p (h s d) -> p h s d", h=4, s=2)[:, :, 1, :]),
                        (), ("sDV", 1))
                    vv = p.inc("sDV")
                    p.add("ve", lambda e, par=par, tt=tt, rp_=rp_: e.tensor_add(
                        hstage[par][:, tt * 512:(tt + 1) * 512], rt3[rp_][:, :], rt2[rp_][:, :]),
                        [("sDV", vhi)], ("sDV", 1))
                    last_rope_dv[rp_] = vv
                    hs_written[(half, s)] = ("sDV", vv)
                else:
                    aa = p.inc("sAC")
                    p.add("ac", lambda e, bk=bk, par=par, tt=tt: e.copy(
                        hstage[par][:, tt * 512:(tt + 1) * 512], bank[bk][:, 0:512]),
                        wA, ("sAC", 1))
                    evac_done[(half, s, tt)] = [("sAC", aa)]
                    hs_written[(half, s)] = ("sAC", aa)
            if kind != "dt":
                spill_done[k] = p.inc("sSP", 16)
                p.add("sy", lambda e, par=par, dest=dest, dcol=dcol, hf=half: e.dma_start(
                    out=dest[hf * THALF:(hf + 1) * THALF, dcol:dcol + 512]
                        .rearrange("(tt p) o -> p tt o", p=128),
                    in_=ap3(hstage[par], 8)),
                    [hs_written[(half, s)]], ("sSP", 16))
        pe_half[half] = p.val("sPE")
    p.barrier()
    if phases <= 1:
        return _finish(nc, p, dbg, scr, dumps)

    vg = None
    for n in range(NCH):
        inc = ("sDV", 1) if n == NCH - 1 else None
        p.add("ve", lambda e, n=n: e.tensor_mul(g_t[:, n * 32:(n + 1) * 32],
                                                dt_t[:, n * 32:(n + 1) * 32], negA[:, :]),
              (), inc)
    vg = p.inc("sDV")
    for n in range(NCH):
        p.add("te", lambda e, n=n: e.matmul(bank[4][:, n * 32:(n + 1) * 32], utf[:, :],
                                            g_t[:, n * 32:(n + 1) * 32], start=True, stop=True),
              [("sDV", vg)] if n == 0 else ())
        p.add("te", lambda e, n=n: e.matmul(bank[5][:, n * 32:(n + 1) * 32], onesf[:, :],
                                            g_t[:, n * 32:(n + 1) * 32], start=True, stop=True),
              (), ("sPE", 1) if n == NCH - 1 else None)
        if n == NCH - 1:
            pass
    tGall = p.inc("sPE")
    aG = p.inc("sAC")
    p.add("ac", lambda e: e.copy(G_t[:, :], bank[4][:, :]), [("sPE", tGall)], ("sAC", 1))
    aGl = p.inc("sAC")
    p.add("ac", lambda e: e.copy(Gl_t[:, :], bank[5][:, :]), (), ("sAC", 1))
    vsf = p.inc("sDV")
    p.add("ve", lambda e: e.memset(Saf_t[:, 15 * 32:16 * 32], 0.0), [("sAC", aGl)], ("sDV", 1))
    for n in range(14, -1, -1):
        wv = [("sDV", vsf)]
        vsf = p.inc("sDV")
        p.add("ve", lambda e, n=n: e.tensor_add(Saf_t[:, n * 32:(n + 1) * 32],
                                                Saf_t[:, (n + 1) * 32:(n + 2) * 32],
                                                Gl_t[:, (n + 1) * 32:(n + 2) * 32]),
              wv, ("sDV", 1))
    p.add("ve", lambda e: e.tensor_sub(tmp_t[:, :], Gl_t[:, :], G_t[:, :]), [("sAC", aG)])
    vS = p.inc("sDV")
    p.add("ve", lambda e: e.tensor_scalar_mul(negG_t[:, :], G_t[:, :], -1.0), (), ("sDV", 1))
    p.add("ac", lambda e: e.activation(ed_t[:, :], tmp_t[:, :], AF.Exp), [("sDV", vS)])
    p.add("ac", lambda e: e.activation(ecd_t[:, :], Gl_t[:, :], AF.Exp))
    aE = p.inc("sAC")
    p.add("ac", lambda e: e.activation(expG_t[:, :], G_t[:, :], AF.Exp), (), ("sAC", 1))

    vpf = p.inc("sDV")
    p.add("ve", lambda e: e.memset(Dcum_t[:, 0:32], 0.0), (), ("sDV", 1))
    for n in range(1, NCH):
        wv = [("sDV", vpf)]
        vpf = p.inc("sDV")
        p.add("ve", lambda e, n=n: e.tensor_add(Dcum_t[:, n * 32:(n + 1) * 32],
                                                Dcum_t[:, (n - 1) * 32:n * 32],
                                                Gl_t[:, (n - 1) * 32:n * 32]),
              wv, ("sDV", 1))
    aDc = p.inc("sAC")
    p.add("ac", lambda e: e.activation(Dcum_t[:, :], Dcum_t[:, :], AF.Exp),
          [("sDV", vpf)], ("sAC", 1))
    vD = p.inc("sDV")
    p.add("ve", lambda e: e.tensor_add(Dg[:, :], Saf_t[:, 0:32], Gl_t[:, 0:32]),
          [("sDV", vsf)], ("sDV", 1))
    aD = p.inc("sAC")
    p.add("ac", lambda e: e.activation(Dtot[:, :], Dg[:, :], AF.Exp), [("sDV", vD)], ("sAC", 1))
    aLn = p.inc("sAC")
    p.add("ac", lambda e: e.activation(g_t[:, :], dt_t[:, :], AF.Ln),
          [("sDV", p.val("sDV"))], ("sAC", 1))
    vNG = p.inc("sDV")
    p.add("ve", lambda e: e.scalar_tensor_tensor(negG_t[:, :], g_t[:, :], 1.0,
                                                 negG_t[:, :], ALU.mult, ALU.add),
          [("sAC", aLn)], ("sDV", 1))
    vED = p.inc("sDV")
    p.add("ve", lambda e: e.tensor_mul(tmp_t[:, :], ed_t[:, :], dt_t[:, :]),
          [("sDV", vNG)], ("sDV", 1))
    tabs = [dt_t, g_t, G_t, Gl_t, Saf_t, ed_t, ecd_t, Dcum_t]
    wtab = [("sAC", p.val("sAC")), ("sDV", p.val("sDV"))]
    for i, tb in enumerate(tabs):
        p.inc("sSP", 16)
        p.add("sy", lambda e, i=i, tb=tb: e.dma_start(out=gtab_sp[i], in_=tb[:, :]),
              wtab, ("sSP", 16))
    p.barrier()
    if phases <= 2:
        return _finish(nc, p, dbg, scr, dumps)

    edt_t = tmp_t
    evF = {}
    dmaF = {}
    spP = {}
    lastC_per = {}
    tCm = {}
    for grp in range(4):
        w = []
        if grp >= 1:
            w.append(("sPE", lastC_per[(grp - 1, "all")]))
        kvl = p.inc("sX", 64)
        for kk in range(2):
            kv = 2 * grp + kk
            p.add("sy", lambda e, kv=kv, kk=kk: e.dma_start(
                out=ap3(ktok[kk], NCH),
                in_=k_sp[:, kv * 128:(kv + 1) * 128].rearrange("(n p) d -> p n d", p=128)),
                w if kk == 0 else (), ("sX", 16))
            p.add("sy", lambda e, kv=kv, kk=kk: e.dma_start(
                out=ap3(vtok[kk], NCH),
                in_=v_sp[:, kv * 128:(kv + 1) * 128].rearrange("(n p) d -> p n d", p=128)),
                (), ("sX", 16))
        vz0 = {}
        for hm in range(8):
            h = 8 * grp + hm
            w = []
            if (grp - 1, hm) in spP:
                w.append(("sSP", spP[(grp - 1, hm)]))
            if grp >= 1:
                w.append(("sAC", evF[8 * (grp - 1) + hm]))
            p.add("ve", lambda e, hm=hm: e.memset(Sloc[hm][:, :], 0.0), w)
            vz0[hm] = p.inc("sDV")
            p.add("ve", lambda e, hm=hm: e.memset(prevacc[hm][:, 0:128], 0.0), (), ("sDV", 1))
        vst1 = {}
        vcst = {}
        for n in range(NCH):
            for hm in range(8):
                h = 8 * grp + hm
                kk = hm // 4
                col = n * 32 + h
                w = [("sX", kvl), ("sDV", vz0[hm])]
                if (grp, hm, n - 2) in tCm:
                    w.append(("sPE", tCm[(grp, hm, n - 2)]))
                elif grp >= 1 and n < 2:
                    w.append(("sPE", tCm[(grp - 1, hm, NCH - 2 + n)]))
                vk = p.inc("sDV")
                p.add("ve", lambda e, n=n, hm=hm, kk=kk, col=col: e.tensor_scalar_mul(
                    kd1[hm][:, :], ap3(ktok[kk], NCH)[:, n, :], edt_t[:, col:col + 1]),
                    w, ("sDV", 1))
                w = [("sDV", vk)]
                if (n - 1, hm) in vst1:
                    w.append(("sDV", vst1[(n - 1, hm)]))
                tCm[(grp, hm, n)] = p.inc("sPE")
                p.add("te", lambda e, hm=hm, kk=kk, n=n: e.matmul(
                    bank[hm][:, 0:128], kd1[hm][:, :], ap3(vtok[kk], NCH)[:, n, :],
                    start=True, stop=True), w, ("sPE", 1))
                vst1[(n, hm)] = p.inc("sDV")
                p.add("ve", lambda e, hm=hm, col=col: e.scalar_tensor_tensor(
                    Sloc[hm][:, :], Sloc[hm][:, :], ecd_t[:, col:col + 1],
                    bank[hm][:, 0:128], ALU.mult, ALU.add),
                    [("sPE", tCm[(grp, hm, n)])], ("sDV", 1))
                if n < NCH - 1:
                    vcst[(n, hm)] = p.inc("sDV")
                    p.add("ve", lambda e, hm=hm, n=n: e.tensor_copy(
                        prevacc[hm][:, (n + 1) * 128:(n + 2) * 128], Sloc[hm][:, :]),
                        [("sDV", vst1[(n, hm)])], ("sDV", 1))
        lastC_per[(grp, "all")] = p.val("sPE")
        for hm in range(8):
            h = 8 * grp + hm
            w = [("sDV", vst1[(NCH - 1, hm)])]
            if h >= 2:
                w.append(("sSP", dmaF.get(h - 2, 0)))
                w.append(("sAC", evF.get(h - 2, 0)))
            evF[h] = p.inc("sAC")
            p.add("ac", lambda e, hm=hm: e.copy(Fh[hm % 2][:, :], Sloc[hm][:, :]),
                  w, ("sAC", 1))
            dmaF[h] = p.inc("sSP", 16)
            p.add("sy", lambda e, h=h, hm=hm: e.dma_start(
                out=agin[h * 128:(h + 1) * 128, :], in_=Fh[hm % 2][:, :]),
                [("sAC", evF[h])], ("sSP", 16))
            spP[(grp, hm)] = p.inc("sSP", 16)
            p.add("sy", lambda e, h=h, hm=hm: e.dma_start(
                out=prevs_sp[h].rearrange("(n p) d -> p n d", p=128),
                in_=ap3(prevacc[hm], NCH)),
                [("sDV", vcst[(NCH - 2, hm)])], ("sSP", 16))
    p.inc("sSP", 16)
    p.add("sy", lambda e: e.dma_start(out=agin[4096:4224, 0:32], in_=Dtot[:, :]),
          [("sAC", aD)], ("sSP", 16))
    p.add("gp", lambda e: e.collective_compute(
        "AllGather", ALU.bypass, replica_groups=[[0, 1, 2, 3], [4, 5, 6, 7]],
        ins=[agin.opt()], outs=[agout.opt()]),
        [("sSP", p.val("sSP"))], ("sCC", None))
    if phases <= 2.9:
        p.add("gp", lambda e: e.wait_ge(p.sems["sCC"], 1))
        p.barrier()
        return _finish(nc, p, dbg, scr, dumps)
    p.inc("sDV")
    p.add("ve", lambda e: e.memset(Sin[:, :], 0.0), (), ("sDV", 1))
    for j in range(3):
        w = [("sCC", 1), ("sDV", p.val("sDV"))]
        xj = p.inc("sX", 32)
        p.add("sy", lambda e, j=j: e.dma_start(
            out=ap3(Fj, H),
            in_=agout[j * 4224:j * 4224 + 4096, :].rearrange("(h p) e -> p h e", p=128)),
            w, ("sX", 16))
        p.add("sy", lambda e, j=j: e.dma_start(
            out=Dg[:, :], in_=agout[j * 4224 + 4096:j * 4224 + 4224, 0:32]), (), ("sX", 16))
        p.add("ve", lambda e, j=j: e.scalar_tensor_tensor(
            Dp[:, :], Dg[:, :], amask[:, j:j + 1], bmask[:, j * 32:(j + 1) * 32],
            ALU.mult, ALU.add), [("sX", xj)])
        for h in range(H):
            vts = p.inc("sDV")
            p.add("ve", lambda e, h=h: e.tensor_scalar_mul(
                tscr[:, :], Sin[:, h * 128:(h + 1) * 128], Dp[:, h:h + 1]), (), ("sDV", 1))
            p.add("ve", lambda e, h=h, j=j: e.scalar_tensor_tensor(
                Sin[:, h * 128:(h + 1) * 128], ap3(Fj, H)[:, h, :], amask[:, j:j + 1],
                tscr[:, :], ALU.mult, ALU.add), [("sDV", vts)],
                ("sDV", 1) if h == H - 1 else None)
        p.inc("sDV")
    p.barrier()
    if phases <= 3:
        return _finish(nc, p, dbg, scr, dumps)

    bS = [3, 0]; bR = [4, 2]; bY = [5, 1]; bZ = [6, 7]
    am4 = p.inc("sAC", 4)
    for j4 in range(4):
        p.add("ac", lambda e, j4=j4: e.copy(mneg4[:, j4 * 128:(j4 + 1) * 128], mnegf[:, :]),
              (), ("sAC", 1) if True else None)
    ld = {}
    lastuse = {}
    spillY = {}
    prev_vyp15 = [0]
    last_SDt = {0: 0, 1: 0}
    last_exp = {0: 0, 1: 0}
    last_Y = {0: 0, 1: 0}
    last_yc = {0: 0, 1: 0}
    last_zb = {0: 0, 1: 0}
    last_rowb = {0: 0, 1: 0}
    last_ZMM = {0: 0, 1: 0}
    last_yp = {0: 0, 1: 0}
    for h in range(H):
        kv = h // 4
        par = h % 2
        kpar = kv % 2
        w = [lastuse[("q", h - 2)]] if ("q", h - 2) in lastuse else []
        ld[("q", h)] = p.inc("sX", 16)
        p.add("sy", lambda e, h=h, par=par: e.dma_start_transpose(
            out=qTh[par][:, :], in_=q_sp[:, h * 128:(h + 1) * 128]), w, ("sX", 16))
        w = [lastuse[("g", h - 2)]] if ("g", h - 2) in lastuse else []
        ld[("g", h)] = p.inc("sX", 16)
        p.add("sy", lambda e, h=h, par=par: e.dma_start(
            out=ap3(gth[par], NCH),
            in_=g_sp[:, h * 128:(h + 1) * 128].rearrange("(n p) d -> p n d", p=128)),
            w, ("sX", 16))
        w = [lastuse[("pv", h - 2)]] if ("pv", h - 2) in lastuse else []
        ld[("pv", h)] = p.inc("sX", 16)
        p.add("sy", lambda e, h=h, par=par: e.dma_start(
            out=ap3(prevldr[par], NCH),
            in_=prevs_sp[h].rearrange("(n p) d -> p n d", p=128)), w, ("sX", 16))
        if h % 4 == 0:
            w = [lastuse[("kv", kv - 2)]] if ("kv", kv - 2) in lastuse else []
            ld[("kT", kv)] = p.inc("sX", 16)
            p.add("sy", lambda e, kv=kv, kpar=kpar: e.dma_start_transpose(
                out=kTh[kpar][:, :], in_=k_sp[:, kv * 128:(kv + 1) * 128]), w, ("sX", 16))
            ld[("vtok", kv)] = p.inc("sX", 16)
            p.add("sy", lambda e, kv=kv, kpar=kpar: e.dma_start(
                out=ap3(vtok[kpar], NCH),
                in_=v_sp[:, kv * 128:(kv + 1) * 128].rearrange("(n p) d -> p n d", p=128)),
                (), ("sX", 16))
        vSDt = {}; vyc = {}
        tS = {}; tY = {}; tZ = {}
        for nb in range(4):
            p2 = nb % 2
            cols = [(nb * 4 + j) * 32 + h for j in range(4)]
            ns = [nb * 4 + j for j in range(4)]
            w = []
            if last_rowb[p2] > 0:
                w.append(("sPE", last_rowb[p2]))
            for j in range(4):
                p.add("ve", lambda e, j=j, p2=p2, c=cols[j]: e.tensor_scalar_mul(
                    diagw[p2][:, j * 128:(j + 1) * 128], identf[:, :], G_t[:, c:c + 1]),
                    w if j == 0 else ())
            vdg = 0
            for j in range(4):
                if j == 3:
                    vdg = p.inc("sDV")
                p.add("ve", lambda e, j=j, p2=p2, c=cols[j]: e.tensor_scalar_mul(
                    dnegw[p2][:, j * 128:(j + 1) * 128], identf[:, :], negG_t[:, c:c + 1]),
                    (), ("sDV", 1) if j == 3 else None)
            w = [("sDV", vdg), ("sAC", am4)]
            if last_exp[p2] > 0:
                w.append(("sAC", last_exp[p2]))

            def mk_rowb(p2=p2):
                def f(e):
                    e.matmul(bank[bR[p2]][:, 0:512], onesf[:, :], diagw[p2][:, :],
                             start=True, stop=False)
                    e.matmul(bank[bR[p2]][:, 0:512], identf[:, :], mneg4[:, :],
                             start=False, stop=False)
                    ins = None
                    for j in range(4):
                        ins = e.matmul(bank[bR[p2]][:, j * 128:(j + 1) * 128],
                                       dnegw[p2][:, j * 128:(j + 1) * 128], onesf[:, :],
                                       start=False, stop=(j == 3))
                    return ins
                return f
            tRb = last_rowb[p2] = p.inc("sPE")
            p.add("te", mk_rowb(), w, ("sPE", 1))
            w = [("sPE", tRb)]
            if last_SDt[p2] > 0:
                w.append(("sDV", last_SDt[p2]))
            aex = last_exp[p2] = p.inc("sAC")
            p.add("ac", lambda e, p2=p2: e.activation(decw[p2][:, :], bank[bR[p2]][:, 0:512],
                                                      AF.Exp), w, ("sAC", 1))
            w = [("sX", ld[("q", h)]), ("sX", ld[("kT", kv)])]
            if last_SDt[p2] > 0:
                w.append(("sDV", last_SDt[p2]))

            def mk_S(p2=p2, ns=ns, par=par, kpar=kpar):
                def f(e):
                    ins = None
                    for j, n in enumerate(ns):
                        ins = e.matmul(bank[bS[p2]][:, j * 128:(j + 1) * 128],
                                       kTh[kpar][:, n * 128:(n + 1) * 128],
                                       qTh[par][:, n * 128:(n + 1) * 128],
                                       start=True, stop=True)
                    return ins
                return f
            tS[nb] = p.inc("sPE")
            p.add("te", mk_S(), w, ("sPE", 1))
            w = [("sPE", tS[nb]), ("sAC", aex)]
            if last_Y[p2] > 0:
                w.append(("sPE", last_Y[p2]))
            vSDt[nb] = last_SDt[p2] = p.inc("sDV")
            p.add("ve", lambda e, p2=p2: e.tensor_mul(sdtw[p2][:, :], bank[bS[p2]][:, 0:512],
                                                      decw[p2][:, :]), w, ("sDV", 1))
            w = [("sDV", vSDt[nb]), ("sX", ld[("vtok", kv)])]
            if last_yc[p2] > 0:
                w.append(("sDV", last_yc[p2]))

            def mk_Y(p2=p2, ns=ns, kpar=kpar):
                def f(e):
                    ins = None
                    for j, n in enumerate(ns):
                        ins = e.matmul(bank[bY[p2]][:, j * 128:(j + 1) * 128],
                                       sdtw[p2][:, j * 128:(j + 1) * 128],
                                       ap3(vtok[kpar], NCH)[:, n, :],
                                       start=True, stop=True)
                    return ins
                return f
            tY[nb] = last_Y[p2] = p.inc("sPE")
            p.add("te", mk_Y(), w, ("sPE", 1))
            w = [("sX", ld[("pv", h)])]
            if last_ZMM[p2] > 0:
                w.append(("sPE", last_ZMM[p2]))
            vpmb = 0
            for j, n in enumerate(ns):
                if j == 3:
                    vpmb = p.inc("sDV")
                p.add("ve", lambda e, j=j, n=n, h=h, par=par, p2=p2, c=cols[j]: e.scalar_tensor_tensor(
                    pmw[p2][:, j * 128:(j + 1) * 128], Sin[:, h * 128:(h + 1) * 128],
                    Dcum_t[:, c:c + 1], ap3(prevldr[par], NCH)[:, n, :], ALU.mult, ALU.add),
                    w if j == 0 else (), ("sDV", 1) if j == 3 else None)
            w = [("sDV", vpmb), ("sX", ld[("q", h)])]
            if last_zb[p2] > 0:
                w.append(("sDV", last_zb[p2]))

            def mk_Z(p2=p2, ns=ns, par=par):
                def f(e):
                    ins = None
                    for j, n in enumerate(ns):
                        ins = e.matmul(bank[bZ[p2]][:, j * 128:(j + 1) * 128],
                                       qTh[par][:, n * 128:(n + 1) * 128],
                                       pmw[p2][:, j * 128:(j + 1) * 128],
                                       start=True, stop=True)
                    return ins
                return f
            tZ[nb] = last_ZMM[p2] = p.inc("sPE")
            p.add("te", mk_Z(), w, ("sPE", 1))
            vzbb = 0
            w = [("sPE", tZ[nb])]
            if last_yc[p2] > 0:
                w.append(("sDV", last_yc[p2]))
            for j in range(4):
                if j == 3:
                    vzbb = p.inc("sDV")
                p.add("ve", lambda e, j=j, p2=p2, c=cols[j]: e.tensor_scalar_mul(
                    zbw[p2][:, j * 128:(j + 1) * 128], bank[bZ[p2]][:, j * 128:(j + 1) * 128],
                    expG_t[:, c:c + 1]), w if j == 0 else (), ("sDV", 1) if j == 3 else None)
            vyc[nb] = last_yc[p2] = last_zb[p2] = p.inc("sDV")
            p.add("ve", lambda e, nb=nb, p2=p2: e.tensor_add(
                Yacc[:, nb * 512:(nb + 1) * 512], zbw[p2][:, :], bank[bY[p2]][:, 0:512]),
                [("sPE", tY[nb]), ("sDV", vzbb)], ("sDV", 1))
            for j, n in enumerate(ns):
                p.inc("sAC")
                p.add("ac", lambda e, n=n: e.activation(sqscr[:, :],
                                                        Yacc[:, n * 128:(n + 1) * 128],
                                                        AF.Square, accum_out=ssqh[:, n:n + 1]),
                      [("sDV", vyc[nb])], ("sAC", 1))
        tS[NCH - 1] = tS[3]
        tZ[NCH - 1] = tZ[3]
        tY[NCH - 1] = tY[3]
        vpm = {NCH - 1: p.val("sDV")}
        lastuse[("q", h)] = ("sPE", max(tS[NCH - 1], tZ[NCH - 1]))
        lastuse[("kv", kv)] = ("sPE", max(tS[NCH - 1], tY[NCH - 1]))
        lastuse[("pv", h)] = ("sDV", vpm[NCH - 1])
        lastuse[("g", h)] = None
        arr = p.inc("sAC", 2)
        wrr = [("sDV", prev_vyp15[0])] if prev_vyp15[0] > 0 else []
        p.add("ac", lambda e: e.activation(rrmsh[:, :], ssqh[:, :], AF.Ln,
                                           bias=epsb[:, 0:1], scale=1.0 / 128), wrr)
        p.add("ac", lambda e: e.activation(rrmsh[:, :], rrmsh[:, :], AF.Exp, scale=-0.5),
              (), ("sAC", 2))
        vyp = {}
        for n in range(NCH):
            npar = n % 2
            w = [("sX", ld[("g", h)]), ("sAC", arr)]
            if n == 0 and (h - 1) in spillY:
                w.append(("sSP", spillY[h - 1]))
            if last_yp[npar] > 0:
                w.append(("sDV", last_yp[npar]))
            asg = p.inc("sAC")
            p.add("ac", lambda e, n=n, par=par, npar=npar: e.activation(
                sgb[npar][:, :], ap3(gth[par], NCH)[:, n, :], AF.Silu), w, ("sAC", 1))
            p.add("ve", lambda e, n=n: e.tensor_scalar_mul(
                tscr[:, :], Yacc[:, n * 128:(n + 1) * 128], rrmsh[:, n:n + 1]),
                [("sAC", asg)])
            vyp[n] = last_yp[npar] = p.inc("sDV")
            p.add("ve", lambda e, n=n, npar=npar: e.tensor_mul(
                ypacc[:, n * 128:(n + 1) * 128], tscr[:, :], sgb[npar][:, :]), (), ("sDV", 1))
        prev_vyp15[0] = vyp[NCH - 1]
        lastuse[("g", h)] = ("sAC", p.val("sAC"))
        spillY[h] = p.inc("sSP", 16)
        p.add("sy", lambda e, h=h: e.dma_start(
            out=y_sp[:, h * 128:(h + 1) * 128].rearrange("(n p) d -> p n d", p=128),
            in_=ap3(ypacc, NCH)), [("sDV", vyp[NCH - 1])], ("sSP", 16))
    p.barrier()
    if phases <= 4:
        return _finish(nc, p, dbg, scr, dumps)

    yTv = ap3(yT, H)
    stgv = ap3(stg, 8)

    epi_state = {"aa": 0, "outdma": 0}

    def ln_epilogue(hf, tt, tpar, strips_done, residual_load, out_emit, out_inc):
        lx = p.inc("sX", 16)
        wl = [strips_done]
        if tt >= 2:
            wl.append(("sDV", p.val("sDV")))
        p.add("sy", residual_load, wl, ("sX", 16))
        wz = [("sX", lx), strips_done]
        if epi_state["aa"] > 0:
            wz.append(("sAC", epi_state["aa"]))
        vz = p.inc("sDV")
        p.add("ve", lambda e, tt=tt, tpar=tpar: e.tensor_add(
            zbuf[:, :], (xch if out_inc == "sSP" else x1ch)[tpar][:, :], stgv[:, tt, :]),
            wz, ("sDV", 1))
        aa = p.inc("sAC", 2)
        epi_state["aa"] = aa
        wacc = [("sDV", vz)]
        if out_inc == "sOUT" and epi_state["outdma"] > 0:
            wacc.append(("sOUT", epi_state["outdma"]))
        p.add("ac", lambda e: e.activation(sqb[:, :], zbuf[:, :], AF.Copy,
                                           accum_out=lnsc[:, 0:1]), wacc)
        p.add("ac", lambda e: e.activation(sqb[:, :], zbuf[:, :], AF.Square,
                                           accum_out=lnsc[:, 1:2]), (), ("sAC", 2))
        vmean = p.inc("sDV")
        p.add("ve", lambda e: e.tensor_scalar_mul(lnsc[:, 2:3], lnsc[:, 0:1], 1.0 / 4096),
              [("sAC", aa)], ("sDV", 1))
        am = p.inc("sAC", 2)
        p.add("ac", lambda e: e.activation(lnsc[:, 3:4], lnsc[:, 0:1], AF.Square,
                                           scale=1.0 / 4096))
        p.add("ac", lambda e: e.activation(lnsc[:, 4:5], lnsc[:, 1:2], AF.Copy,
                                           scale=1.0 / 4096), (), ("sAC", 2))
        vv = p.inc("sDV")
        p.add("ve", lambda e: e.tensor_sub(lnsc[:, 5:6], lnsc[:, 4:5], lnsc[:, 3:4]),
              [("sAC", am)], ("sDV", 1))
        ar = p.inc("sAC", 2)
        p.add("ac", lambda e: e.activation(lnsc[:, 6:7], lnsc[:, 5:6], AF.Ln, bias=epsb[:, 0:1]),
              [("sDV", vv)])
        p.add("ac", lambda e: e.activation(lnsc[:, 7:8], lnsc[:, 6:7], AF.Exp, scale=-0.5),
              (), ("sAC", 2))
        vo = p.inc("sDV")
        p.add("ve", out_emit, [("sAC", ar), ("sDV", vmean)], ("sDV", 1))
        nv = p.inc(out_inc, 16)
        if out_inc == "sOUT":
            epi_state["outdma"] = nv
        return vo

    pe_prev = None
    epi_done = None
    for hf in range(2):
        w = []
        if epi_done is not None:
            w.append(epi_done)
        for ht in range(H):
            p.inc("sX", 16)
            p.add("sy", lambda e, hf=hf, ht=ht: e.dma_start_transpose(
                out=yTv[:, ht, :],
                in_=y_sp[hf * THALF:(hf + 1) * THALF, ht * 128:(ht + 1) * 128]),
                w if ht == 0 else (), ("sX", 16))
        ldy = p.val("sX")
        wl = {}
        scl = {}
        pe_t = {}
        ev_t = {}
        for s in range(16):
            par = s % 2
            w = []
            if s >= 2:
                w.append(("sPE", pe_t[(s - 2, 7)]))
            elif pe_prev is not None:
                w.append(("sPE", pe_prev[(14 + s, 7)]))
            wl[s] = p.inc("sW", 16)
            p.add("gp", lambda e, s=s, par=par: e.dma_start(
                out=ap3(nstrip[par], CT), in_=wotiles[s]), w, ("sW", 16))
            scl[s] = p.inc("sDV")
            p.add("ve", lambda e, par=par: e.tensor_scalar_mul(
                nstrip[par][:, :], nstrip[par][:, :], gnorm[:, 0:1]),
                [("sW", wl[s])], ("sDV", 1))
            for tt in range(8):
                bk = (s * 8 + tt) % 2
                w = [("sDV", scl[s]), ("sX", ldy)]
                prev = (s, tt - 2) if tt >= 2 else ((s - 1, tt + 6) if s >= 1 else None)
                if prev in ev_t:
                    w.append(("sAC", ev_t[prev]))

                def mkp(par=par, tt=tt, bk=bk):
                    def f(e):
                        sv = ap3(nstrip[par], CT)
                        ins = None
                        for ht in range(H):
                            ins = e.matmul(bank[bk][:, 0:256],
                                           yTv[:, ht, tt * 128:(tt + 1) * 128],
                                           sv[:, ht, :], start=(ht == 0), stop=(ht == H - 1))
                        return ins
                    return f
                pe_t[(s, tt)] = p.inc("sPE")
                p.add("te", mkp(), w, ("sPE", 1))
                ev_t[(s, tt)] = p.inc("sAC")
                p.add("ac", lambda e, s=s, tt=tt, bk=bk: e.copy(
                    stgv[:, tt, s * 256:(s + 1) * 256], bank[bk][:, 0:256]),
                    [("sPE", pe_t[(s, tt)])], ("sAC", 1))
        pe_prev = pe_t
        strips_done = ("sAC", p.val("sAC"))
        for tt in range(8):
            tpar = tt % 2
            vo = ln_epilogue(
                hf, tt, tpar, strips_done,
                residual_load=lambda e, hf=hf, tt=tt, tpar=tpar: e.dma_start(
                    out=xch[tpar][:, :],
                    in_=xT[hf * THALF + tt * 128:hf * THALF + (tt + 1) * 128, :]),
                out_emit=lambda e, tt=tt: e.tensor_scalar(
                    stgv[:, tt, :], zbuf[:, :], lnsc[:, 2:3], lnsc[:, 7:8],
                    ALU.subtract, ALU.mult),
                out_inc="sSP")
            p.add("sy", lambda e, hf=hf, tt=tt: e.dma_start(
                out=x1_sp[hf * THALF + tt * 128:hf * THALF + (tt + 1) * 128, :],
                in_=stgv[:, tt, :]), [("sDV", vo)], ("sSP", 16))
        epi_done = ("sSP", p.val("sSP"))
    p.barrier()
    if phases <= 5:
        return _finish(nc, p, dbg, scr, dumps)

    x1Fv = ap3(yT, CT)
    vs = 0
    for nn in range(16):
        par = nn % 2
        w = [("sDV", vs)] if nn >= 1 else []
        lx = p.inc("sX", 16)
        p.add("sy", lambda e, nn=nn, par=par: e.dma_start(
            out=lstg[par][:, :], in_=loraAT[nn]), w, ("sX", 16))
        wv = [("sX", lx)]
        if nn >= 1:
            wv.append(("sDV", vs))
        vs = p.inc("sDV")
        if nn == 0:
            p.add("ve", lambda e, par=par: e.tensor_scalar_mul(
                AeT32[:, :], lstg[par][:, :], gateb[:, 0:1]), wv, ("sDV", 1))
        else:
            p.add("ve", lambda e, nn=nn, par=par: e.scalar_tensor_tensor(
                AeT32[:, :], lstg[par][:, :], gateb[:, nn:nn + 1], AeT32[:, :],
                ALU.mult, ALU.add), wv, ("sDV", 1))
    vAe = p.inc("sDV")
    p.add("ve", lambda e: e.tensor_copy(AeT[:, :], AeT32[:, :]), [("sDV", vs)], ("sDV", 1))
    vAct = {CT - 1: vAe}
    vb = 0
    for nn in range(16):
        lx = p.inc("sX", 16)
        w = [("sDV", vb)] if vb else []
        p.add("sy", lambda e, nn=nn: e.dma_start(out=bstg[:, :], in_=loraBT[nn]),
              w, ("sX", 16))
        vbprev = vb
        vb = p.inc("sDV")
        if nn == 0:
            p.add("ve", lambda e: e.tensor_scalar_mul(BeT[:, :], bstg[:, :], gateb[0:32, 0:1]),
                  [("sX", lx)], ("sDV", 1))
        else:
            p.add("ve", lambda e, nn=nn: e.scalar_tensor_tensor(
                BeT[:, :], bstg[:, :], gateb[0:32, nn:nn + 1], BeT[:, :], ALU.mult, ALU.add),
                [("sX", lx), ("sDV", vbprev)], ("sDV", 1))
    vbe = p.inc("sDV")
    p.add("ve", lambda e: e.tensor_copy(BeTb[:, :], BeT[:, :]), [("sDV", vb)], ("sDV", 1))
    lsh = p.inc("sW", 32)
    p.add("gp", lambda e: e.dma_start(out=ap3(AsT, CT),
                                      in_=lorasAT.rearrange("ct p r -> p ct r")),
          (), ("sW", 16))
    p.add("gp", lambda e: e.dma_start(out=BsTb[0:32, :], in_=lorasBT), (), ("sW", 16))
    vz1 = p.inc("sDV")
    p.add("ve", lambda e: e.memset(BsTb[32:64, :], 0.0), (), ("sDV", 1))
    lfb = p.inc("sW", 16)
    p.add("gp", lambda e: e.dma_start(out=BsTb[32:33, :], in_=ffnbias_d),
          [("sDV", vz1)], ("sW", 16))
    pre_dv = p.val("sDV")

    pe_prev = None
    epi_done = None
    for hf in range(2):
        w = [("sDV", pre_dv)]
        if epi_done is not None:
            w.append(epi_done)
        for ctb in range(CT):
            p.inc("sX", 16)
            p.add("sy", lambda e, hf=hf, ctb=ctb: e.dma_start_transpose(
                out=x1Fv[:, ctb, :],
                in_=x1_sp[hf * THALF:(hf + 1) * THALF, ctb * 128:(ctb + 1) * 128]),
                w if ctb == 0 else (), ("sX", 16))
        ldx = p.val("sX")
        aax = {}
        for blk in range(2):
            w = [("sX", ldx)]
            if blk == 0:
                w += [("sDV", vAct[CT - 1]), ("sW", lsh)]
            else:
                w.append(("sAC", aax[0]))

            def mkax(blk=blk):
                def f(e):
                    for ct in range(CT):
                        e.matmul(bank[2][0:32, 0:512], AeT[:, ct * 32:(ct + 1) * 32],
                                 x1Fv[:, ct, blk * 512:(blk + 1) * 512],
                                 start=(ct == 0), stop=(ct == CT - 1))
                    ins = None
                    for ct in range(CT):
                        ins = e.matmul(bank[3][0:32, 0:512], AsT[:, ct * 32:(ct + 1) * 32],
                                       x1Fv[:, ct, blk * 512:(blk + 1) * 512],
                                       start=(ct == 0), stop=(ct == CT - 1))
                    return ins
                return f
            tax = p.inc("sPE")
            p.add("te", mkax(), w, ("sPE", 1))
            aax[blk] = p.inc("sAC", 2)
            p.add("ac", lambda e, blk=blk: e.copy(AxTb[:, blk * 512:(blk + 1) * 512],
                                                  bank[2][0:32, 0:512]), [("sPE", tax)])
            p.add("ac", lambda e, blk=blk: e.copy(AxsTb[0:32, blk * 512:(blk + 1) * 512],
                                                  bank[3][0:32, 0:512]), (), ("sAC", 2))
        p.inc("sDV")
        p.add("ve", lambda e: e.memset(AxsTb[32:64, :], 0.0), [("sAC", aax[1])], ("sDV", 1))
        vones = p.inc("sDV")
        p.add("ve", lambda e: e.memset(AxsTb[32:33, :], 1.0), (), ("sDV", 1))
        wl = {}
        pe_t = {}
        ev_t = {}
        for s in range(16):
            par = s % 2
            w = []
            if s >= 2:
                w.append(("sPE", pe_t[(s - 2, 7)]))
            elif pe_prev is not None:
                w.append(("sPE", pe_prev[(14 + s, 7)]))
            wl[s] = p.inc("sW", 16)
            p.add("gp", lambda e, s=s, par=par: e.dma_start(
                out=ap3(nstrip[par], CT), in_=kerntiles[s]), w, ("sW", 16))
            for tt in range(8):
                bk = (s * 8 + tt) % 2
                w = [("sW", wl[s]), ("sX", ldx), ("sDV", vones), ("sDV", vbe)]
                prev = (s, tt - 2) if tt >= 2 else ((s - 1, tt + 6) if s >= 1 else None)
                if prev in ev_t:
                    w.append(("sAC", ev_t[prev]))

                def mkp(par=par, tt=tt, bk=bk, s=s):
                    def f(e):
                        sv = ap3(nstrip[par], CT)
                        for ct in range(CT):
                            e.matmul(bank[bk][:, 0:256], x1Fv[:, ct, tt * 128:(tt + 1) * 128],
                                     sv[:, ct, :], start=(ct == 0), stop=False)
                        e.matmul(bank[bk][:, 0:256], AxTb[0:32, tt * 128:(tt + 1) * 128],
                                 BeTb[0:32, s * 256:(s + 1) * 256], start=False, stop=False)
                        return e.matmul(bank[bk][:, 0:256],
                                        AxsTb[0:64, tt * 128:(tt + 1) * 128],
                                        BsTb[0:64, s * 256:(s + 1) * 256],
                                        start=False, stop=True)
                    return f
                pe_t[(s, tt)] = p.inc("sPE")
                p.add("te", mkp(), w, ("sPE", 1))
                ev_t[(s, tt)] = p.inc("sAC")
                p.add("ac", lambda e, s=s, tt=tt, bk=bk: e.copy(
                    stgv[:, tt, s * 256:(s + 1) * 256], bank[bk][:, 0:256]),
                    [("sPE", pe_t[(s, tt)])], ("sAC", 1))
        pe_prev = pe_t
        strips_done = ("sAC", p.val("sAC"))
        for tt in range(8):
            tpar = tt % 2
            vo = ln_epilogue(
                hf, tt, tpar, strips_done,
                residual_load=lambda e, hf=hf, tt=tt, tpar=tpar: e.dma_start(
                    out=x1ch[tpar][:, :],
                    in_=x1_sp[hf * THALF + tt * 128:hf * THALF + (tt + 1) * 128, :]),
                out_emit=lambda e: e.tensor_scalar(
                    sqb[:, :], zbuf[:, :], lnsc[:, 2:3], lnsc[:, 7:8],
                    ALU.subtract, ALU.mult),
                out_inc="sOUT")
            p.add("gp", lambda e, hf=hf, tt=tt: e.dma_start(
                out=y_out[hf * THALF + tt * 128:hf * THALF + (tt + 1) * 128, :],
                in_=sqb[:, :]), [("sDV", vo)], ("sOUT", 16))
        epi_done = ("sOUT", p.val("sOUT"))

    return _finish(nc, p, dbg, scr, dumps)


def _finish(nc, p, dbg, scr, dumps):
    for name in dumps:
        wv = [("sSP", p.val("sSP")), ("sDV", p.val("sDV")), ("sAC", p.val("sAC")),
              ("sPE", p.val("sPE")), ("sOUT", p.val("sOUT"))]
        p.inc("sOUT", 16)
        p.add("gp", lambda e, name=name: e.dma_start(out=dbg[name], in_=scr[name]),
              wv, ("sOUT", 16))
    tgt = p.val("sOUT")
    if tgt > 0:
        def fin(e, tgt=tgt):
            e.wait_ge(p.sems["sOUT"], tgt)
        fin.meta = ("gp", (("sOUT", tgt),), None)
        p.q["gp"].append(fin)

    with nc.Block() as block:
        block.tensor(lambda e: [f(e) for f in p.q["te"]])
        block.vector(lambda e: [f(e) for f in p.q["ve"]])
        block.scalar(lambda e: [f(e) for f in p.q["ac"]])
        block.gpsimd(lambda e: [f(e) for f in p.q["gp"]])
        block.sync(lambda e: [f(e) for f in p.q["sy"]])
    return nc



def _host_inputs(inputs, core):
    x = np.asarray(inputs["L0_attn_hidden_states"], dtype=np.float32).reshape(16384, 4096)
    sl = slice(core * T, (core + 1) * T)
    xTc = np.ascontiguousarray(x[sl])
    xFc = np.ascontiguousarray(xTc.T)

    Wq = np.asarray(inputs["L0_attn_Wq"], np.float32)
    Wk = np.asarray(inputs["L0_attn_Wk"], np.float32)
    Wv = np.asarray(inputs["L0_attn_Wv"], np.float32)
    Wg = np.asarray(inputs["L0_attn_Wg"], np.float32)
    Win = np.asarray(inputs["L0_attn_Win"], np.float32)
    Wcat = np.concatenate([Wq, Wk, Wv, Wg, Win], axis=0)
    WcatT = np.zeros((4096, NSTRIP1 * 512), np.float32)
    WcatT[:, :10272] = Wcat.T
    wtiles = np.ascontiguousarray(
        WcatT.reshape(CT, 128, NSTRIP1, 512).transpose(2, 1, 0, 3))

    WoT = np.ascontiguousarray(np.asarray(inputs["L0_attn_Wo"], np.float32).T)
    wotiles = np.ascontiguousarray(WoT.reshape(CT, 128, 16, 256).transpose(2, 1, 0, 3))
    kern = np.asarray(inputs["L0_ffn_kernel"], np.float32)
    kerntiles = np.ascontiguousarray(kern.reshape(CT, 128, 16, 256).transpose(2, 1, 0, 3))

    loraAT = np.ascontiguousarray(
        np.asarray(inputs["L0_ffn_lora_A"], np.float32).transpose(0, 2, 1)
        .reshape(16, CT, 128, 32).transpose(0, 2, 1, 3).reshape(16, 128, CT * 32))
    loraBT = np.ascontiguousarray(
        np.asarray(inputs["L0_ffn_lora_B"], np.float32).transpose(0, 2, 1))
    lorasAT = np.ascontiguousarray(
        np.asarray(inputs["L0_ffn_lora_A_shared"], np.float32).T.reshape(CT, 128, 32))
    lorasBT = np.ascontiguousarray(np.asarray(inputs["L0_ffn_lora_B_shared"], np.float32).T)

    pos = (core % 4) * T + np.arange(T, dtype=np.float64)
    invf = (1.0 / 10000.0) ** (np.arange(0, 128, 2, dtype=np.float64) / 128.0)
    ang = pos[:, None] * invf[None, :]
    cos128 = np.concatenate([np.cos(ang), np.cos(ang)], axis=1).astype(np.float32)
    sin64 = np.sin(ang).astype(np.float32)
    sinS = np.concatenate([-sin64, sin64], axis=1)
    cosrep = np.ascontiguousarray(np.tile(cos128, (1, 4))).astype(ml_dtypes.bfloat16)
    sinSrep = np.ascontiguousarray(np.tile(sinS, (1, 4))).astype(ml_dtypes.bfloat16)

    identf = np.eye(128, dtype=np.float32)
    mnegf = np.tril(np.full((128, 128), MNEG_VAL, np.float32), -1)
    utf = np.triu(np.ones((128, 128), np.float32))

    dtbias_row = np.zeros((1, 512), np.float32)
    dtbias_row[0, :32] = np.asarray(inputs["L0_attn_dt_bias"], np.float32)
    alog_row = np.asarray(inputs["L0_attn_A_log_bias"], np.float32).reshape(1, 32)
    ffnbias_row = np.asarray(inputs["L0_ffn_bias"], np.float32).reshape(1, 4096)
    gate_row = np.asarray(inputs["L0_ffn_gate"], np.float32)[core].reshape(1, 16)
    gnorm_col = np.asarray(inputs["L0_attn_gnorm_w"], np.float32).reshape(128, 1)

    posn = core % 4
    am = np.zeros((128, 3), np.float32)
    bm = np.zeros((128, 96), np.float32)
    for j in range(3):
        a = 1.0 if j < posn else 0.0
        am[:, j] = a
        bm[:, j * 32:(j + 1) * 32] = 1.0 - a

    return dict(xF=xFc, xT=xTc, wtiles=wtiles, wotiles=wotiles, kerntiles=kerntiles,
                loraAT=loraAT, loraBT=loraBT, lorasAT=lorasAT, lorasBT=lorasBT,
                cosrep=cosrep, sinSrep=sinSrep, identf=identf, mnegf=mnegf, utf=utf,
                dtbias_row=dtbias_row, alog_row=alog_row, ffnbias_row=ffnbias_row,
                gate_row=gate_row, gnorm_col=gnorm_col, amask=am, bmask_bc=bm)


_RESULTS = None


def kernel(**inputs):
    global _RESULTS
    nc = build_nc(dumps=_DUMPS, phases=_PHASES)
    in_maps = [_host_inputs(inputs, c) for c in range(8)]
    res = run_bass_kernel_spmd(nc, in_maps, core_ids=list(range(8)))
    _RESULTS = res
    out = np.concatenate([res.results[c]["y_out"] for c in range(8)], axis=0)
    return out.reshape(2, 8192, 4096)


# version 06 (final)
# speedup vs baseline: 1.1461x; 1.0075x over previous
import numpy as np
import ml_dtypes
from contextlib import ExitStack

import concourse.bass as bass
import concourse.mybir as mybir
from concourse.bass_utils import run_bass_kernel_spmd

F32 = mybir.dt.float32
BF16 = mybir.dt.bfloat16
AF = mybir.ActivationFunctionType
ALU = mybir.AluOpType

T = 2048
CT = 32
H = 32
NCH = 16
THALF = 1024
NSTRIP1 = 21
MNEG_VAL = -1e5
KB = 1024

_DUMPS = ()
_PHASES = 99


class Prog:
    def __init__(self):
        self.q = {k: [] for k in ("te", "ve", "ac", "gp", "sy")}
        self.c = {}
        self.nbar = 0
        self.sems = None

    def inc(self, sem, amt=1):
        self.c[sem] = self.c.get(sem, 0) + amt
        return self.c[sem]

    def val(self, sem):
        return self.c.get(sem, 0)

    def add(self, eng, emit, waits=(), inc=None):
        wm = {}
        for (s, v) in waits:
            if v > 0 and v > wm.get(s, 0):
                wm[s] = v
        waits = tuple(wm.items())

        def fn(e, emit=emit, waits=waits, inc=inc):
            for (s, v) in waits:
                e.wait_ge(self.sems[s], v)
            ins = emit(e)
            if inc is not None:
                s, amt = inc
                if amt is None:
                    ins.then_inc(self.sems[s])
                else:
                    ins.then_inc(self.sems[s], amt)

        fn.meta = (eng, waits, inc)
        self.q[eng].append(fn)

    def barrier(self):
        self.nbar += 1
        tgt = 5 * self.nbar
        for eng in ("te", "ve", "ac", "gp", "sy"):
            def fn(e, tgt=tgt):
                e.sem_inc(self.sems["sB"], 1)
                e.wait_ge(self.sems["sB"], tgt)
            fn.meta = (eng, (("sB", tgt),), ("sB", 1))
            fn.is_barrier = True
            self.q[eng].append(fn)


def build_nc(dumps=(), phases=99):
    nc = bass.Bass("TRN2", target_bir_lowering=False)
    p = Prog()

    xF = nc.dram_tensor("xF", [4096, T], F32, kind="ExternalInput").ap()
    xT = nc.dram_tensor("xT", [T, 4096], F32, kind="ExternalInput").ap()
    wtiles = nc.dram_tensor("wtiles", [NSTRIP1, 128, CT, 512], F32, kind="ExternalInput").ap()
    wotiles = nc.dram_tensor("wotiles", [16, 128, CT, 256], F32, kind="ExternalInput").ap()
    kerntiles = nc.dram_tensor("kerntiles", [16, 128, CT, 256], F32, kind="ExternalInput").ap()
    loraAT = nc.dram_tensor("loraAT", [16, 128, CT * 32], F32, kind="ExternalInput").ap()
    loraBT = nc.dram_tensor("loraBT", [16, 32, 4096], F32, kind="ExternalInput").ap()
    lorasAT = nc.dram_tensor("lorasAT", [CT, 128, 32], F32, kind="ExternalInput").ap()
    lorasBT = nc.dram_tensor("lorasBT", [32, 4096], F32, kind="ExternalInput").ap()
    cosrep_d = nc.dram_tensor("cosrep", [T, 512], BF16, kind="ExternalInput").ap()
    sinSrep_d = nc.dram_tensor("sinSrep", [T, 512], BF16, kind="ExternalInput").ap()
    identf_d = nc.dram_tensor("identf", [128, 128], F32, kind="ExternalInput").ap()
    mnegf_d = nc.dram_tensor("mnegf", [128, 128], F32, kind="ExternalInput").ap()
    utf_d = nc.dram_tensor("utf", [128, 128], F32, kind="ExternalInput").ap()
    dtbias_d = nc.dram_tensor("dtbias_row", [1, 512], F32, kind="ExternalInput").ap()
    alog_d = nc.dram_tensor("alog_row", [1, 32], F32, kind="ExternalInput").ap()
    ffnbias_d = nc.dram_tensor("ffnbias_row", [1, 4096], F32, kind="ExternalInput").ap()
    gate_d = nc.dram_tensor("gate_row", [1, 16], F32, kind="ExternalInput").ap()
    gnorm_d = nc.dram_tensor("gnorm_col", [128, 1], F32, kind="ExternalInput").ap()
    amask_d = nc.dram_tensor("amask", [128, 3], F32, kind="ExternalInput").ap()
    bmask_d = nc.dram_tensor("bmask_bc", [128, 96], F32, kind="ExternalInput").ap()

    y_out = nc.dram_tensor("y_out", [T, 4096], F32, kind="ExternalOutput").ap()

    q_sp = nc.dram_tensor("q_sp", [T, 4096], BF16, kind="Internal").ap()
    k_sp = nc.dram_tensor("k_sp", [T, 1024], BF16, kind="Internal").ap()
    v_sp = nc.dram_tensor("v_sp", [T, 1024], BF16, kind="Internal").ap()
    g_sp = nc.dram_tensor("g_sp", [T, 4096], BF16, kind="Internal").ap()
    y_sp = nc.dram_tensor("y_sp", [T, 4096], BF16, kind="Internal").ap()
    x1_sp = nc.dram_tensor("x1_sp", [T, 4096], BF16, kind="Internal").ap()
    gtab_sp = nc.dram_tensor("gtab_sp", [8, 128, 512], F32, kind="Internal").ap()
    lndbg = nc.dram_tensor("lndbg", [16, 128, 8], F32, kind="Internal").ap()
    zdbg = nc.dram_tensor("zdbg", [128, 4096], F32, kind="Internal").ap()
    prevs_sp = nc.dram_tensor("prevs_sp", [H, T, 128], BF16, kind="Internal").ap()
    agin = nc.dram_tensor("agin", [4224, 128], F32, kind="Internal").ap()
    agout = nc.dram_tensor("agout", [16896, 128], F32, kind="Internal").ap()
    scr = dict(q_sp=q_sp, k_sp=k_sp, v_sp=v_sp, g_sp=g_sp, y_sp=y_sp,
               x1_sp=x1_sp, gtab_sp=gtab_sp, agin=agin, agout=agout,
               lndbg=lndbg, zdbg=zdbg)
    dbg = {}
    for name in dumps:
        t = scr[name]
        dbg[name] = nc.dram_tensor("dbg_" + name, list(t.shape), t.dtype,
                                   kind="ExternalOutput").ap()

    identf = nc.alloc_sbuf_tensor("identf_b", [128, 128], F32)
    mnegf = nc.alloc_sbuf_tensor("mnegf_b", [128, 128], F32)
    utf = nc.alloc_sbuf_tensor("utf_b", [128, 128], F32)
    onesf = nc.alloc_sbuf_tensor("onesf", [128, 128], F32)
    onesrow = nc.alloc_sbuf_tensor("onesrow", [1, 128], F32)
    onesrow_b = nc.alloc_sbuf_tensor("onesrow_b", [1, 128], BF16)
    dtbias = nc.alloc_sbuf_tensor("dtbias", [1, 512], F32)
    dtbias_bf = nc.alloc_sbuf_tensor("dtbias_bf", [1, 512], BF16)
    alog = nc.alloc_sbuf_tensor("alog", [1, 32], F32)
    eA = nc.alloc_sbuf_tensor("eA", [1, 32], F32)
    gater = nc.alloc_sbuf_tensor("gater", [1, 16], F32)
    gnorm = nc.alloc_sbuf_tensor("gnorm", [128, 1], F32)
    amask = nc.alloc_sbuf_tensor("amask_b", [128, 3], F32)
    bmask = nc.alloc_sbuf_tensor("bmask_b", [128, 96], F32)
    negA = nc.alloc_sbuf_tensor("negA", [128, 32], F32)
    gateb = nc.alloc_sbuf_tensor("gateb", [128, 16], F32)
    dt_t = nc.alloc_sbuf_tensor("dt_t", [128, 512], F32)
    Dtot = nc.alloc_sbuf_tensor("Dtot", [128, 32], F32)
    Dg = nc.alloc_sbuf_tensor("Dg", [128, 32], F32)
    Dp = nc.alloc_sbuf_tensor("Dp", [128, 32], F32)
    ssqh = nc.alloc_sbuf_tensor("ssqh", [128, NCH], F32)
    rrmsh = nc.alloc_sbuf_tensor("rrmsh", [128, NCH], F32)
    lnsc = nc.alloc_sbuf_tensor("lnsc", [128, 8], F32)
    epsb = nc.alloc_sbuf_tensor("epsb", [128, 1], F32)
    rt1 = [nc.alloc_sbuf_tensor(f"rt1_{i}", [128, 512], BF16) for i in range(2)]
    rt2 = [nc.alloc_sbuf_tensor(f"rt2_{i}", [128, 512], BF16) for i in range(2)]
    rt3 = [nc.alloc_sbuf_tensor(f"rt3_{i}", [128, 512], BF16) for i in range(2)]
    dtm = nc.alloc_sbuf_tensor("dtm", [128, 32], F32)
    dtl = nc.alloc_sbuf_tensor("dtl", [128, 32], F32)
    Fh = [nc.alloc_sbuf_tensor(f"Fh{i}", [128, 128], F32) for i in range(2)]


    sqscr = nc.alloc_sbuf_tensor("sqscr", [128, 128], F32)

    tscr = nc.alloc_sbuf_tensor("tscr", [128, 128], F32)

    ARENA = 188 * KB
    arena = nc.alloc_sbuf_tensor("arena", [128, ARENA], mybir.dt.uint8)
    abase = nc.lookup_mloc(arena).addr

    def at(name, shape, dtype, off_kb):
        return nc.alloc_sbuf_tensor_at(name, list(shape), dtype,
                                       offset=abase + int(off_kb * KB))

    XFH = at("XFH", [128, CT * THALF], BF16, 0)
    wstrip = [at(f"wstrip{i}", [128, CT * 512], BF16, 64 + 32 * i) for i in range(2)]
    hstage = [at(f"hstage{i}", [128, 8 * 512], BF16, 128 + 8 * i) for i in range(2)]
    cosb = at("cosb", [128, NCH * 512], BF16, 144)
    sinb = at("sinb", [128, NCH * 512], BF16, 160)
    g_t = at("g_t", [128, 512], F32, 0)
    G_t = at("G_t", [128, 512], F32, 2)
    negG_t = at("negG_t", [128, 512], F32, 4)
    expG_t = at("expG_t", [128, 512], F32, 6)
    Gl_t = at("Gl_t", [128, 512], F32, 8)
    Saf_t = at("Saf_t", [128, 512], F32, 10)
    ed_t = at("ed_t", [128, 512], F32, 12)
    ecd_t = at("ecd_t", [128, 512], F32, 14)
    kdg_t = at("kdg_t", [128, 512], F32, 16)
    tmp_t = at("tmp_t", [128, 512], F32, 18)
    Dcum_t = at("Dcum_t", [128, 512], F32, 16)
    Sin = at("Sin", [128, H * 128], F32, 20)
    Fj = at("Fj", [128, H * 128], F32, 36)
    Yacc = at("Yacc", [128, NCH * 128], F32, 52)
    ypacc = at("ypacc", [128, NCH * 128], BF16, 60)
    qTh = [at(f"qTh{i}", [128, T], BF16, 64 + 4 * i) for i in range(2)]
    kTh = [at(f"kTh{i}", [128, T], BF16, 72 + 4 * i) for i in range(2)]
    ktok = [at(f"ktok{i}", [128, T], BF16, 80 + 4 * i) for i in range(2)]
    vtok = [at(f"vtok{i}", [128, T], BF16, 88 + 4 * i) for i in range(2)]
    gth = [at(f"gth{i}", [128, T], BF16, 96 + 4 * i) for i in range(2)]
    prevacc = [at(f"prevacc{i}", [128, NCH * 128], BF16, 104 + 4 * i) for i in range(8)]
    prevldr = [at(f"prevldr{i}", [128, NCH * 128], BF16, 136 + 4 * i) for i in range(2)]
    Sloc = [at(f"Sloc{i}", [128, 128], F32, 144 + 0.5 * i) for i in range(8)]
    kd1 = [at(f"kd1_{i}", [128, 128], BF16, 148 + 0.25 * i) for i in range(8)]
    pmw = [at(f"pmw{i}", [128, 512], BF16, 154 + 1 * i) for i in range(2)]
    decw = [at(f"decw{i}", [128, 512], F32, 156 + 2 * i) for i in range(2)]
    sdtw = [at(f"sdtw{i}", [128, 512], BF16, 160 + 1 * i) for i in range(2)]
    zbw = [at(f"zbw{i}", [128, 512], F32, 162 + 2 * i) for i in range(2)]
    diagw = [at(f"diagw{i}", [128, 512], F32, 166 + 2 * i) for i in range(2)]
    dnegw = [at(f"dnegw{i}", [128, 512], F32, 170 + 2 * i) for i in range(2)]
    mneg4 = at("mneg4", [128, 512], F32, 174)
    diagp = [at(f"diagp{i}", [128, 128], F32, 150 + 0.5 * i) for i in range(2)]
    decb = [at(f"decb{i}", [128, 128], F32, 151 + 0.5 * i) for i in range(2)]
    sdtb = [at(f"sdtb{i}", [128, 128], BF16, 152 + 0.25 * i) for i in range(2)]
    zbp = [at(f"zbp{i}", [128, 128], F32, 152.5 + 0.5 * i) for i in range(2)]
    sgb = [at(f"sgb{i}", [128, 128], BF16, 153.5 + 0.25 * i) for i in range(2)]
    yT = at("yT", [128, H * THALF], BF16, 0)
    stg = at("stg", [128, 8 * 4096], BF16, 64)
    nstrip = [at(f"nstrip{i}", [128, CT * 256], BF16, 128 + 16 * i) for i in range(2)]
    zbuf = at("zbuf", [128, 4096], F32, 0)
    sqb = at("sqb", [128, 4096], F32, 16)
    xch = [at(f"xch{i}", [128, 4096], F32, 32 + 16 * i) for i in range(2)]
    x1ch = [at(f"x1ch{i}", [128, 4096], BF16, 32 + 8 * i) for i in range(2)]
    AeT32 = at("AeT32", [128, CT * 32], F32, 64)
    BeT = at("BeT", [32, 4096], F32, 68)
    bstg = at("bstg", [32, 4096], F32, 84)
    AxTb = at("AxTb", [32, THALF], BF16, 160)
    AxsTb = at("AxsTb", [64, THALF], BF16, 162)
    BeTb = at("BeTb", [32, 4096], BF16, 164)
    BsTb = at("BsTb", [64, 4096], BF16, 172)
    lstg = [at("lstg0", [128, CT * 32], F32, 180), at("lstg0b", [128, CT * 32], F32, 180)]
    AeT = at("AeT", [128, CT * 32], BF16, 184)
    AsT = at("AsT", [128, CT * 32], BF16, 186)

    bank = [nc.alloc_psum_tensor(f"bank{i}", [128, 512], F32) for i in range(8)]

    es = ExitStack()
    semnames = ["sW", "sX", "sSP", "sPE", "sAC", "sDV", "sCC", "sB", "sOUT"]
    p.sems = {n: es.enter_context(nc.semaphore(n)) for n in semnames}

    def ap3(t, a):
        return t.ap().rearrange("p (a b) -> p a b", a=a)

    for (dst, src) in [(identf, identf_d), (mnegf, mnegf_d), (utf, utf_d),
                       (dtbias, dtbias_d), (alog, alog_d), (gater, gate_d),
                       (gnorm, gnorm_d), (amask, amask_d), (bmask, bmask_d)]:
        p.inc("sX", 16)
        p.add("sy", lambda e, d=dst, s=src: e.dma_start(out=d[:, :], in_=s), (), ("sX", 16))
    p.inc("sX", 16)
    p.add("sy", lambda e: e.dma_start(out=ap3(cosb, NCH),
                                      in_=cosrep_d.rearrange("(n p) c -> p n c", p=128)),
          (), ("sX", 16))
    p.inc("sX", 16)
    p.add("sy", lambda e: e.dma_start(out=ap3(sinb, NCH),
                                      in_=sinSrep_d.rearrange("(n p) c -> p n c", p=128)),
          (), ("sX", 16))
    x0 = p.val("sX")
    p.inc("sDV")
    p.add("ve", lambda e: e.memset(onesf[:, :], 1.0), (), ("sDV", 1))
    p.inc("sDV")
    p.add("ve", lambda e: e.memset(epsb[:, :], 1e-5), (), ("sDV", 1))
    v0 = p.inc("sDV")
    p.add("ve", lambda e: e.memset(onesrow[:, :], 1.0), (), ("sDV", 1))
    a0 = p.inc("sAC")
    p.add("ac", lambda e: e.activation(eA[:, :], alog[:, :], AF.Exp), [("sX", x0)], ("sAC", 1))
    p.add("ac", lambda e: e.copy(onesrow_b[:, :], onesrow[:, :]), [("sDV", v0)])
    a1 = p.inc("sAC")
    p.add("ac", lambda e: e.copy(dtbias_bf[:, :], dtbias[:, :]), [("sX", x0)], ("sAC", 1))
    t0 = p.inc("sPE")
    p.add("te", lambda e: e.matmul(bank[2][:, 0:32], onesrow[0:1, :], eA[0:1, :],
                                   start=True, stop=True),
          [("sAC", a0), ("sDV", v0)], ("sPE", 1))
    t1 = p.inc("sPE")
    p.add("te", lambda e: e.matmul(bank[3][:, 0:16], onesrow[0:1, :], gater[0:1, :],
                                   start=True, stop=True),
          [("sX", x0), ("sDV", v0)], ("sPE", 1))
    p.add("ve", lambda e: e.tensor_scalar_mul(negA[:, :], bank[2][:, 0:32], -1.0),
          [("sPE", t0)])
    p.add("ac", lambda e: e.copy(gateb[:, :], bank[3][:, 0:16]), [("sPE", t1)])
    p.barrier()

    def strip_dest(s):
        if s < 8:
            return ("q", q_sp, s * 512, True)
        if s < 10:
            return ("k", k_sp, (s - 8) * 512, True)
        if s < 12:
            return ("v", v_sp, (s - 10) * 512, False)
        if s < 20:
            return ("g", g_sp, (s - 12) * 512, False)
        return ("dt", None, 0, False)

    XFHv = ap3(XFH, CT)
    pe_tile = {}
    last_rope_dv = [0, 0]
    last_dt_dv = [0]
    evac_done = {}
    hs_written = {}
    spill_done = {}
    pe_half = {}
    wload = {}
    xload = {}
    kglob = 0
    for half in range(2):
        w = [("sPE", pe_half.get(half - 1, 0))]
        xload[half] = p.inc("sW", 16)
        p.add("gp", lambda e, hf=half: e.dma_start(
            out=XFHv,
            in_=xF[:, hf * THALF:(hf + 1) * THALF].rearrange("(ct p) t -> p ct t", p=128)),
            w, ("sW", 16))
        for s in range(NSTRIP1):
            k = kglob
            kglob += 1
            par = k % 2
            w = []
            if k >= 2:
                pk = k - 2
                ph, ps = divmod(pk, NSTRIP1)
                w.append(("sPE", pe_tile[(ph, ps, 7)]))
            wload[k] = p.inc("sW", 16)
            p.add("gp", lambda e, s=s, par=par: e.dma_start(
                out=ap3(wstrip[par], CT), in_=wtiles[s]), w, ("sW", 16))
            kind, dest, dcol, isrope = strip_dest(s)
            for tt in range(8):
                n = half * 8 + tt
                bk = (2 + tt % 2) if kind == "dt" else (k * 8 + tt) % 2
                w = [("sW", wload[k]), ("sW", xload[half])]
                if kind == "dt":
                    prevtile = (half, s, tt - 2) if tt >= 2 else \
                        ((half - 1, s, tt + 6) if half >= 1 else None)
                else:
                    gtix = k * 8 + tt
                    if gtix >= 2:
                        pg = gtix - 2
                        pk2, ptt = divmod(pg, 8)
                        ph2, ps2 = divmod(pk2, NSTRIP1)
                        prevtile = (ph2, ps2, ptt)
                        while prevtile is not None and strip_dest(prevtile[1])[0] == "dt":
                            pg -= 1
                            if pg < 0:
                                prevtile = None
                                break
                            pk2, ptt = divmod(pg, 8)
                            ph2, ps2 = divmod(pk2, NSTRIP1)
                            prevtile = (ph2, ps2, ptt)
                    else:
                        prevtile = None
                if prevtile is not None and prevtile in evac_done:
                    w += evac_done[prevtile]

                def mk_pe(par=par, tt=tt, bk=bk, isdt=(kind == "dt")):
                    def f(e):
                        wv = ap3(wstrip[par], CT)
                        ins = None
                        for ct in range(CT):
                            ins = e.matmul(bank[bk][:, 0:512],
                                           XFHv[:, ct, tt * 128:(tt + 1) * 128],
                                           wv[:, ct, :],
                                           start=(ct == 0),
                                           stop=(ct == CT - 1 and not isdt))
                        if isdt:
                            ins = e.matmul(bank[bk][:, 0:512], onesrow_b[0:1, :],
                                           dtbias_bf[0:1, :], start=False, stop=True)
                        return ins
                    return f
                pe_tile[(half, s, tt)] = p.inc("sPE")
                p.add("te", mk_pe(), w, ("sPE", 1))

                if kind == "dt":
                    vv = p.inc("sDV")
                    p.add("ve", lambda e, bk=bk: e.tensor_scalar_max(
                        dtm[:, :], bank[bk][:, 0:32], 0.0),
                        [("sPE", pe_tile[(half, s, tt)]), ("sDV", last_dt_dv[0])],
                        ("sDV", 1))
                    dt_max_dv = vv
                    aa = p.inc("sAC")
                    p.add("ac", lambda e, bk=bk: e.activation(dtl[:, :], bank[bk][:, 0:32], AF.Abs),
                          [("sPE", pe_tile[(half, s, tt)]), ("sDV", last_dt_dv[0])],
                          ("sAC", 1))
                    evac_done[(half, s, tt)] = [("sDV", vv), ("sAC", aa)]
                    p.add("ac", lambda e: e.activation(dtl[:, :], dtl[:, :], AF.Exp, scale=-1.0))
                    aa2 = p.inc("sAC")
                    p.add("ac", lambda e: e.activation(dtl[:, :], dtl[:, :], AF.Ln, bias=1.0),
                          (), ("sAC", 1))
                    last_dt_dv[0] = p.inc("sDV")
                    p.add("ve", lambda e, n=n: e.tensor_add(
                        dt_t[:, n * 32:(n + 1) * 32], dtm[:, :], dtl[:, :]),
                        [("sAC", aa2), ("sDV", dt_max_dv)], ("sDV", 1))
                    continue

                wA = [("sPE", pe_tile[(half, s, tt)])]
                if (k - 2) in spill_done:
                    wA.append(("sSP", spill_done[k - 2]))
                if isrope:
                    rp_ = (k * 8 + tt) % 2
                    if last_rope_dv[rp_] > 0:
                        wA.append(("sDV", last_rope_dv[rp_]))
                    aa = p.inc("sAC")
                    p.add("ac", lambda e, bk=bk, rp_=rp_: e.copy(rt1[rp_][:, :], bank[bk][:, 0:512]),
                          wA, ("sAC", 1))
                    evac_done[(half, s, tt)] = [("sAC", aa)]
                    r1 = rt1[rp_].ap().rearrange("p (h s d) -> p h s d", h=4, s=2)
                    r2 = rt2[rp_].ap().rearrange("p (h s d) -> p h s d", h=4, s=2)
                    p.add("ve", lambda e, n=n, rp_=rp_: e.tensor_mul(
                        rt3[rp_][:, :], rt1[rp_][:, :], ap3(cosb, NCH)[:, n, :]),
                        [("sAC", aa), ("sX", x0)])
                    p.add("ve", lambda e, n=n, r1=r1, r2=r2: e.tensor_mul(
                        r2[:, :, 0, :], r1[:, :, 1, :],
                        ap3(sinb, NCH)[:, n, :].rearrange("p (h s d) -> p h s d", h=4, s=2)[:, :, 0, :]))
                    vhi = p.inc("sDV")
                    p.add("ve", lambda e, n=n, r1=r1, r2=r2: e.tensor_mul(
                        r2[:, :, 1, :], r1[:, :, 0, :],
                        ap3(sinb, NCH)[:, n, :].rearrange("p (h s d) -> p h s d", h=4, s=2)[:, :, 1, :]),
                        (), ("sDV", 1))
                    vv = p.inc("sDV")
                    p.add("ve", lambda e, par=par, tt=tt, rp_=rp_: e.tensor_add(
                        hstage[par][:, tt * 512:(tt + 1) * 512], rt3[rp_][:, :], rt2[rp_][:, :]),
                        [("sDV", vhi)], ("sDV", 1))
                    last_rope_dv[rp_] = vv
                    hs_written[(half, s)] = ("sDV", vv)
                else:
                    aa = p.inc("sAC")
                    p.add("ac", lambda e, bk=bk, par=par, tt=tt: e.copy(
                        hstage[par][:, tt * 512:(tt + 1) * 512], bank[bk][:, 0:512]),
                        wA, ("sAC", 1))
                    evac_done[(half, s, tt)] = [("sAC", aa)]
                    hs_written[(half, s)] = ("sAC", aa)
            if kind != "dt":
                spill_done[k] = p.inc("sSP", 16)
                p.add("sy", lambda e, par=par, dest=dest, dcol=dcol, hf=half: e.dma_start(
                    out=dest[hf * THALF:(hf + 1) * THALF, dcol:dcol + 512]
                        .rearrange("(tt p) o -> p tt o", p=128),
                    in_=ap3(hstage[par], 8)),
                    [hs_written[(half, s)]], ("sSP", 16))
        pe_half[half] = p.val("sPE")
    p.barrier()
    if phases <= 1:
        return _finish(nc, p, dbg, scr, dumps)

    vg = None
    for n in range(NCH):
        inc = ("sDV", 1) if n == NCH - 1 else None
        p.add("ve", lambda e, n=n: e.tensor_mul(g_t[:, n * 32:(n + 1) * 32],
                                                dt_t[:, n * 32:(n + 1) * 32], negA[:, :]),
              (), inc)
    vg = p.inc("sDV")
    for n in range(NCH):
        p.add("te", lambda e, n=n: e.matmul(bank[4][:, n * 32:(n + 1) * 32], utf[:, :],
                                            g_t[:, n * 32:(n + 1) * 32], start=True, stop=True),
              [("sDV", vg)] if n == 0 else ())
        p.add("te", lambda e, n=n: e.matmul(bank[5][:, n * 32:(n + 1) * 32], onesf[:, :],
                                            g_t[:, n * 32:(n + 1) * 32], start=True, stop=True),
              (), ("sPE", 1) if n == NCH - 1 else None)
        if n == NCH - 1:
            pass
    tGall = p.inc("sPE")
    aG = p.inc("sAC")
    p.add("ac", lambda e: e.copy(G_t[:, :], bank[4][:, :]), [("sPE", tGall)], ("sAC", 1))
    aGl = p.inc("sAC")
    p.add("ac", lambda e: e.copy(Gl_t[:, :], bank[5][:, :]), (), ("sAC", 1))
    vsf = p.inc("sDV")
    p.add("ve", lambda e: e.memset(Saf_t[:, 15 * 32:16 * 32], 0.0), [("sAC", aGl)], ("sDV", 1))
    for n in range(14, -1, -1):
        wv = [("sDV", vsf)]
        vsf = p.inc("sDV")
        p.add("ve", lambda e, n=n: e.tensor_add(Saf_t[:, n * 32:(n + 1) * 32],
                                                Saf_t[:, (n + 1) * 32:(n + 2) * 32],
                                                Gl_t[:, (n + 1) * 32:(n + 2) * 32]),
              wv, ("sDV", 1))
    p.add("ve", lambda e: e.tensor_sub(tmp_t[:, :], Gl_t[:, :], G_t[:, :]), [("sAC", aG)])
    vS = p.inc("sDV")
    p.add("ve", lambda e: e.tensor_scalar_mul(negG_t[:, :], G_t[:, :], -1.0), (), ("sDV", 1))
    p.add("ac", lambda e: e.activation(ed_t[:, :], tmp_t[:, :], AF.Exp), [("sDV", vS)])
    p.add("ac", lambda e: e.activation(ecd_t[:, :], Gl_t[:, :], AF.Exp))
    aE = p.inc("sAC")
    p.add("ac", lambda e: e.activation(expG_t[:, :], G_t[:, :], AF.Exp), (), ("sAC", 1))

    vpf = p.inc("sDV")
    p.add("ve", lambda e: e.memset(Dcum_t[:, 0:32], 0.0), (), ("sDV", 1))
    for n in range(1, NCH):
        wv = [("sDV", vpf)]
        vpf = p.inc("sDV")
        p.add("ve", lambda e, n=n: e.tensor_add(Dcum_t[:, n * 32:(n + 1) * 32],
                                                Dcum_t[:, (n - 1) * 32:n * 32],
                                                Gl_t[:, (n - 1) * 32:n * 32]),
              wv, ("sDV", 1))
    aDc = p.inc("sAC")
    p.add("ac", lambda e: e.activation(Dcum_t[:, :], Dcum_t[:, :], AF.Exp),
          [("sDV", vpf)], ("sAC", 1))
    vD = p.inc("sDV")
    p.add("ve", lambda e: e.tensor_add(Dg[:, :], Saf_t[:, 0:32], Gl_t[:, 0:32]),
          [("sDV", vsf)], ("sDV", 1))
    aD = p.inc("sAC")
    p.add("ac", lambda e: e.activation(Dtot[:, :], Dg[:, :], AF.Exp), [("sDV", vD)], ("sAC", 1))
    aLn = p.inc("sAC")
    p.add("ac", lambda e: e.activation(g_t[:, :], dt_t[:, :], AF.Ln),
          [("sDV", p.val("sDV"))], ("sAC", 1))
    vNG = p.inc("sDV")
    p.add("ve", lambda e: e.scalar_tensor_tensor(negG_t[:, :], g_t[:, :], 1.0,
                                                 negG_t[:, :], ALU.mult, ALU.add),
          [("sAC", aLn)], ("sDV", 1))
    vED = p.inc("sDV")
    p.add("ve", lambda e: e.tensor_mul(tmp_t[:, :], ed_t[:, :], dt_t[:, :]),
          [("sDV", vNG)], ("sDV", 1))
    tabs = [dt_t, g_t, G_t, Gl_t, Saf_t, ed_t, ecd_t, Dcum_t]
    wtab = [("sAC", p.val("sAC")), ("sDV", p.val("sDV"))]
    for i, tb in enumerate(tabs):
        p.inc("sSP", 16)
        p.add("sy", lambda e, i=i, tb=tb: e.dma_start(out=gtab_sp[i], in_=tb[:, :]),
              wtab, ("sSP", 16))
    p.barrier()
    if phases <= 2:
        return _finish(nc, p, dbg, scr, dumps)

    edt_t = tmp_t
    evF = {}
    dmaF = {}
    spP = {}
    lastC_per = {}
    tCm = {}
    for grp in range(4):
        w = []
        if grp >= 1:
            w.append(("sPE", lastC_per[(grp - 1, "all")]))
        kvl = p.inc("sX", 64)
        for kk in range(2):
            kv = 2 * grp + kk
            p.add("sy", lambda e, kv=kv, kk=kk: e.dma_start(
                out=ap3(ktok[kk], NCH),
                in_=k_sp[:, kv * 128:(kv + 1) * 128].rearrange("(n p) d -> p n d", p=128)),
                w if kk == 0 else (), ("sX", 16))
            p.add("sy", lambda e, kv=kv, kk=kk: e.dma_start(
                out=ap3(vtok[kk], NCH),
                in_=v_sp[:, kv * 128:(kv + 1) * 128].rearrange("(n p) d -> p n d", p=128)),
                (), ("sX", 16))
        vz0 = {}
        for hm in range(8):
            h = 8 * grp + hm
            w = []
            if (grp - 1, hm) in spP:
                w.append(("sSP", spP[(grp - 1, hm)]))
            if grp >= 1:
                w.append(("sAC", evF[8 * (grp - 1) + hm]))
            p.add("ve", lambda e, hm=hm: e.memset(Sloc[hm][:, :], 0.0), w)
            vz0[hm] = p.inc("sDV")
            p.add("ve", lambda e, hm=hm: e.memset(prevacc[hm][:, 0:128], 0.0), (), ("sDV", 1))
        vst1 = {}
        vcst = {}
        for n in range(NCH):
            for hm in range(8):
                h = 8 * grp + hm
                kk = hm // 4
                col = n * 32 + h
                w = [("sX", kvl), ("sDV", vz0[hm])]
                if (grp, hm, n - 2) in tCm:
                    w.append(("sPE", tCm[(grp, hm, n - 2)]))
                elif grp >= 1 and n < 2:
                    w.append(("sPE", tCm[(grp - 1, hm, NCH - 2 + n)]))
                vk = p.inc("sDV")
                p.add("ve", lambda e, n=n, hm=hm, kk=kk, col=col: e.tensor_scalar_mul(
                    kd1[hm][:, :], ap3(ktok[kk], NCH)[:, n, :], edt_t[:, col:col + 1]),
                    w, ("sDV", 1))
                w = [("sDV", vk)]
                if (n - 1, hm) in vst1:
                    w.append(("sDV", vst1[(n - 1, hm)]))
                tCm[(grp, hm, n)] = p.inc("sPE")
                p.add("te", lambda e, hm=hm, kk=kk, n=n: e.matmul(
                    bank[hm][:, 0:128], kd1[hm][:, :], ap3(vtok[kk], NCH)[:, n, :],
                    start=True, stop=True), w, ("sPE", 1))
                vst1[(n, hm)] = p.inc("sDV")
                p.add("ve", lambda e, hm=hm, col=col: e.scalar_tensor_tensor(
                    Sloc[hm][:, :], Sloc[hm][:, :], ecd_t[:, col:col + 1],
                    bank[hm][:, 0:128], ALU.mult, ALU.add),
                    [("sPE", tCm[(grp, hm, n)])], ("sDV", 1))
                if n < NCH - 1:
                    vcst[(n, hm)] = p.inc("sDV")
                    p.add("ve", lambda e, hm=hm, n=n: e.tensor_copy(
                        prevacc[hm][:, (n + 1) * 128:(n + 2) * 128], Sloc[hm][:, :]),
                        [("sDV", vst1[(n, hm)])], ("sDV", 1))
        lastC_per[(grp, "all")] = p.val("sPE")
        for hm in range(8):
            h = 8 * grp + hm
            w = [("sDV", vst1[(NCH - 1, hm)])]
            if h >= 2:
                w.append(("sSP", dmaF.get(h - 2, 0)))
                w.append(("sAC", evF.get(h - 2, 0)))
            evF[h] = p.inc("sAC")
            p.add("ac", lambda e, hm=hm: e.copy(Fh[hm % 2][:, :], Sloc[hm][:, :]),
                  w, ("sAC", 1))
            dmaF[h] = p.inc("sSP", 16)
            p.add("sy", lambda e, h=h, hm=hm: e.dma_start(
                out=agin[h * 128:(h + 1) * 128, :], in_=Fh[hm % 2][:, :]),
                [("sAC", evF[h])], ("sSP", 16))
            spP[(grp, hm)] = p.inc("sSP", 16)
            p.add("sy", lambda e, h=h, hm=hm: e.dma_start(
                out=prevs_sp[h].rearrange("(n p) d -> p n d", p=128),
                in_=ap3(prevacc[hm], NCH)),
                [("sDV", vcst[(NCH - 2, hm)])], ("sSP", 16))
    p.inc("sSP", 16)
    p.add("sy", lambda e: e.dma_start(out=agin[4096:4224, 0:32], in_=Dtot[:, :]),
          [("sAC", aD)], ("sSP", 16))
    p.add("gp", lambda e: e.collective_compute(
        "AllGather", ALU.bypass, replica_groups=[[0, 1, 2, 3], [4, 5, 6, 7]],
        ins=[agin.opt()], outs=[agout.opt()]),
        [("sSP", p.val("sSP"))], ("sCC", None))
    if phases <= 2.9:
        p.add("gp", lambda e: e.wait_ge(p.sems["sCC"], 1))
        p.barrier()
        return _finish(nc, p, dbg, scr, dumps)
    p.inc("sDV")
    p.add("ve", lambda e: e.memset(Sin[:, :], 0.0), (), ("sDV", 1))
    for j in range(3):
        w = [("sCC", 1), ("sDV", p.val("sDV"))]
        xj = p.inc("sX", 32)
        p.add("sy", lambda e, j=j: e.dma_start(
            out=ap3(Fj, H),
            in_=agout[j * 4224:j * 4224 + 4096, :].rearrange("(h p) e -> p h e", p=128)),
            w, ("sX", 16))
        p.add("sy", lambda e, j=j: e.dma_start(
            out=Dg[:, :], in_=agout[j * 4224 + 4096:j * 4224 + 4224, 0:32]), (), ("sX", 16))
        p.add("ve", lambda e, j=j: e.scalar_tensor_tensor(
            Dp[:, :], Dg[:, :], amask[:, j:j + 1], bmask[:, j * 32:(j + 1) * 32],
            ALU.mult, ALU.add), [("sX", xj)])
        for h in range(H):
            vts = p.inc("sDV")
            p.add("ve", lambda e, h=h: e.tensor_scalar_mul(
                tscr[:, :], Sin[:, h * 128:(h + 1) * 128], Dp[:, h:h + 1]), (), ("sDV", 1))
            p.add("ve", lambda e, h=h, j=j: e.scalar_tensor_tensor(
                Sin[:, h * 128:(h + 1) * 128], ap3(Fj, H)[:, h, :], amask[:, j:j + 1],
                tscr[:, :], ALU.mult, ALU.add), [("sDV", vts)],
                ("sDV", 1) if h == H - 1 else None)
        p.inc("sDV")
    p.barrier()
    if phases <= 3:
        return _finish(nc, p, dbg, scr, dumps)

    bS = [3, 0]; bR = [4, 2]; bY = [5, 1]; bZ = [6, 7]
    am4 = p.inc("sAC", 4)
    for j4 in range(4):
        p.add("ac", lambda e, j4=j4: e.copy(mneg4[:, j4 * 128:(j4 + 1) * 128], mnegf[:, :]),
              (), ("sAC", 1) if True else None)
    ld = {}
    lastuse = {}
    spillY = {}
    prev_vyp15 = [0]
    last_SDt = {0: 0, 1: 0}
    last_exp = {0: 0, 1: 0}
    last_Y = {0: 0, 1: 0}
    last_yc = {0: 0, 1: 0}
    last_zb = {0: 0, 1: 0}
    last_rowb = {0: 0, 1: 0}
    last_ZMM = {0: 0, 1: 0}
    last_yp = {0: 0, 1: 0}
    for h in range(H):
        kv = h // 4
        par = h % 2
        kpar = kv % 2
        w = [lastuse[("q", h - 2)]] if ("q", h - 2) in lastuse else []
        ld[("q", h)] = p.inc("sX", 16)
        p.add("sy", lambda e, h=h, par=par: e.dma_start_transpose(
            out=qTh[par][:, :], in_=q_sp[:, h * 128:(h + 1) * 128]), w, ("sX", 16))
        w = [lastuse[("g", h - 2)]] if ("g", h - 2) in lastuse else []
        ld[("g", h)] = p.inc("sX", 16)
        p.add("sy", lambda e, h=h, par=par: e.dma_start(
            out=ap3(gth[par], NCH),
            in_=g_sp[:, h * 128:(h + 1) * 128].rearrange("(n p) d -> p n d", p=128)),
            w, ("sX", 16))
        w = [lastuse[("pv", h - 2)]] if ("pv", h - 2) in lastuse else []
        ld[("pv", h)] = p.inc("sX", 16)
        p.add("sy", lambda e, h=h, par=par: e.dma_start(
            out=ap3(prevldr[par], NCH),
            in_=prevs_sp[h].rearrange("(n p) d -> p n d", p=128)), w, ("sX", 16))
        if h % 4 == 0:
            w = [lastuse[("kv", kv - 2)]] if ("kv", kv - 2) in lastuse else []
            ld[("kT", kv)] = p.inc("sX", 16)
            p.add("sy", lambda e, kv=kv, kpar=kpar: e.dma_start_transpose(
                out=kTh[kpar][:, :], in_=k_sp[:, kv * 128:(kv + 1) * 128]), w, ("sX", 16))
            ld[("vtok", kv)] = p.inc("sX", 16)
            p.add("sy", lambda e, kv=kv, kpar=kpar: e.dma_start(
                out=ap3(vtok[kpar], NCH),
                in_=v_sp[:, kv * 128:(kv + 1) * 128].rearrange("(n p) d -> p n d", p=128)),
                (), ("sX", 16))
        vSDt = {}; vyc = {}
        tS = {}; tY = {}; tZ = {}
        for nb in range(4):
            p2 = nb % 2
            cols = [(nb * 4 + j) * 32 + h for j in range(4)]
            ns = [nb * 4 + j for j in range(4)]
            w = []
            if last_rowb[p2] > 0:
                w.append(("sPE", last_rowb[p2]))
            for j in range(4):
                p.add("ve", lambda e, j=j, p2=p2, c=cols[j]: e.tensor_scalar_mul(
                    diagw[p2][:, j * 128:(j + 1) * 128], identf[:, :], G_t[:, c:c + 1]),
                    w if j == 0 else ())
            vdg = 0
            for j in range(4):
                if j == 3:
                    vdg = p.inc("sDV")
                p.add("ve", lambda e, j=j, p2=p2, c=cols[j]: e.tensor_scalar_mul(
                    dnegw[p2][:, j * 128:(j + 1) * 128], identf[:, :], negG_t[:, c:c + 1]),
                    (), ("sDV", 1) if j == 3 else None)
            w = [("sDV", vdg), ("sAC", am4)]
            if last_exp[p2] > 0:
                w.append(("sAC", last_exp[p2]))

            def mk_rowb(p2=p2):
                def f(e):
                    e.matmul(bank[bR[p2]][:, 0:512], onesf[:, :], diagw[p2][:, :],
                             start=True, stop=False)
                    e.matmul(bank[bR[p2]][:, 0:512], identf[:, :], mneg4[:, :],
                             start=False, stop=False)
                    ins = None
                    for j in range(4):
                        ins = e.matmul(bank[bR[p2]][:, j * 128:(j + 1) * 128],
                                       dnegw[p2][:, j * 128:(j + 1) * 128], onesf[:, :],
                                       start=False, stop=(j == 3))
                    return ins
                return f
            tRb = last_rowb[p2] = p.inc("sPE")
            p.add("te", mk_rowb(), w, ("sPE", 1))
            w = [("sPE", tRb)]
            if last_SDt[p2] > 0:
                w.append(("sDV", last_SDt[p2]))
            aex = last_exp[p2] = p.inc("sAC")
            p.add("ac", lambda e, p2=p2: e.activation(decw[p2][:, :], bank[bR[p2]][:, 0:512],
                                                      AF.Exp), w, ("sAC", 1))
            w = [("sX", ld[("q", h)]), ("sX", ld[("kT", kv)])]
            if last_SDt[p2] > 0:
                w.append(("sDV", last_SDt[p2]))

            def mk_S(p2=p2, ns=ns, par=par, kpar=kpar):
                def f(e):
                    ins = None
                    for j, n in enumerate(ns):
                        ins = e.matmul(bank[bS[p2]][:, j * 128:(j + 1) * 128],
                                       kTh[kpar][:, n * 128:(n + 1) * 128],
                                       qTh[par][:, n * 128:(n + 1) * 128],
                                       start=True, stop=True)
                    return ins
                return f
            tS[nb] = p.inc("sPE")
            p.add("te", mk_S(), w, ("sPE", 1))
            w = [("sPE", tS[nb]), ("sAC", aex)]
            if last_Y[p2] > 0:
                w.append(("sPE", last_Y[p2]))
            vSDt[nb] = last_SDt[p2] = p.inc("sDV")
            p.add("ve", lambda e, p2=p2: e.tensor_mul(sdtw[p2][:, :], bank[bS[p2]][:, 0:512],
                                                      decw[p2][:, :]), w, ("sDV", 1))
            w = [("sDV", vSDt[nb]), ("sX", ld[("vtok", kv)])]
            if last_yc[p2] > 0:
                w.append(("sDV", last_yc[p2]))

            def mk_Y(p2=p2, ns=ns, kpar=kpar):
                def f(e):
                    ins = None
                    for j, n in enumerate(ns):
                        ins = e.matmul(bank[bY[p2]][:, j * 128:(j + 1) * 128],
                                       sdtw[p2][:, j * 128:(j + 1) * 128],
                                       ap3(vtok[kpar], NCH)[:, n, :],
                                       start=True, stop=True)
                    return ins
                return f
            tY[nb] = last_Y[p2] = p.inc("sPE")
            p.add("te", mk_Y(), w, ("sPE", 1))
            w = [("sX", ld[("pv", h)])]
            if last_ZMM[p2] > 0:
                w.append(("sPE", last_ZMM[p2]))
            vpmb = 0
            for j, n in enumerate(ns):
                if j == 3:
                    vpmb = p.inc("sDV")
                p.add("ve", lambda e, j=j, n=n, h=h, par=par, p2=p2, c=cols[j]: e.scalar_tensor_tensor(
                    pmw[p2][:, j * 128:(j + 1) * 128], Sin[:, h * 128:(h + 1) * 128],
                    Dcum_t[:, c:c + 1], ap3(prevldr[par], NCH)[:, n, :], ALU.mult, ALU.add),
                    w if j == 0 else (), ("sDV", 1) if j == 3 else None)
            w = [("sDV", vpmb), ("sX", ld[("q", h)])]
            if last_zb[p2] > 0:
                w.append(("sDV", last_zb[p2]))

            def mk_Z(p2=p2, ns=ns, par=par):
                def f(e):
                    ins = None
                    for j, n in enumerate(ns):
                        ins = e.matmul(bank[bZ[p2]][:, j * 128:(j + 1) * 128],
                                       qTh[par][:, n * 128:(n + 1) * 128],
                                       pmw[p2][:, j * 128:(j + 1) * 128],
                                       start=True, stop=True)
                    return ins
                return f
            tZ[nb] = last_ZMM[p2] = p.inc("sPE")
            p.add("te", mk_Z(), w, ("sPE", 1))
            vzbb = 0
            w = [("sPE", tZ[nb])]
            if last_yc[p2] > 0:
                w.append(("sDV", last_yc[p2]))
            for j in range(4):
                if j == 3:
                    vzbb = p.inc("sDV")
                p.add("ve", lambda e, j=j, p2=p2, c=cols[j]: e.tensor_scalar_mul(
                    zbw[p2][:, j * 128:(j + 1) * 128], bank[bZ[p2]][:, j * 128:(j + 1) * 128],
                    expG_t[:, c:c + 1]), w if j == 0 else (), ("sDV", 1) if j == 3 else None)
            vyc[nb] = last_yc[p2] = last_zb[p2] = p.inc("sDV")
            p.add("ve", lambda e, nb=nb, p2=p2: e.tensor_add(
                Yacc[:, nb * 512:(nb + 1) * 512], zbw[p2][:, :], bank[bY[p2]][:, 0:512]),
                [("sPE", tY[nb]), ("sDV", vzbb)], ("sDV", 1))
            for j, n in enumerate(ns):
                p.inc("sAC")
                p.add("ac", lambda e, n=n: e.activation(sqscr[:, :],
                                                        Yacc[:, n * 128:(n + 1) * 128],
                                                        AF.Square, accum_out=ssqh[:, n:n + 1]),
                      [("sDV", vyc[nb])], ("sAC", 1))
        tS[NCH - 1] = tS[3]
        tZ[NCH - 1] = tZ[3]
        tY[NCH - 1] = tY[3]
        vpm = {NCH - 1: p.val("sDV")}
        lastuse[("q", h)] = ("sPE", max(tS[NCH - 1], tZ[NCH - 1]))
        lastuse[("kv", kv)] = ("sPE", max(tS[NCH - 1], tY[NCH - 1]))
        lastuse[("pv", h)] = ("sDV", vpm[NCH - 1])
        lastuse[("g", h)] = None
        arr = p.inc("sAC", 2)
        wrr = [("sDV", prev_vyp15[0])] if prev_vyp15[0] > 0 else []
        p.add("ac", lambda e: e.activation(rrmsh[:, :], ssqh[:, :], AF.Ln,
                                           bias=epsb[:, 0:1], scale=1.0 / 128), wrr)
        p.add("ac", lambda e: e.activation(rrmsh[:, :], rrmsh[:, :], AF.Exp, scale=-0.5),
              (), ("sAC", 2))
        vyp = {}
        for n in range(NCH):
            npar = n % 2
            w = [("sX", ld[("g", h)]), ("sAC", arr)]
            if n == 0 and (h - 1) in spillY:
                w.append(("sSP", spillY[h - 1]))
            if last_yp[npar] > 0:
                w.append(("sDV", last_yp[npar]))
            asg = p.inc("sAC")
            p.add("ac", lambda e, n=n, par=par, npar=npar: e.activation(
                sgb[npar][:, :], ap3(gth[par], NCH)[:, n, :], AF.Silu), w, ("sAC", 1))
            p.add("ve", lambda e, n=n: e.tensor_scalar_mul(
                tscr[:, :], Yacc[:, n * 128:(n + 1) * 128], rrmsh[:, n:n + 1]),
                [("sAC", asg)])
            vyp[n] = last_yp[npar] = p.inc("sDV")
            p.add("ve", lambda e, n=n, npar=npar: e.tensor_mul(
                ypacc[:, n * 128:(n + 1) * 128], tscr[:, :], sgb[npar][:, :]), (), ("sDV", 1))
        prev_vyp15[0] = vyp[NCH - 1]
        lastuse[("g", h)] = ("sAC", p.val("sAC"))
        spillY[h] = p.inc("sSP", 16)
        p.add("sy", lambda e, h=h: e.dma_start(
            out=y_sp[:, h * 128:(h + 1) * 128].rearrange("(n p) d -> p n d", p=128),
            in_=ap3(ypacc, NCH)), [("sDV", vyp[NCH - 1])], ("sSP", 16))
    p.barrier()
    if phases <= 4:
        return _finish(nc, p, dbg, scr, dumps)

    yTv = ap3(yT, H)
    stgv = ap3(stg, 8)

    epi_state = {"aa": 0, "outdma": 0}

    def ln_epilogue(hf, tt, tpar, strips_done, residual_load, out_emit, out_inc):
        lx = p.inc("sX", 16)
        wl = [strips_done]
        if tt >= 2:
            wl.append(("sDV", p.val("sDV")))
        p.add("sy", residual_load, wl, ("sX", 16))
        wz = [("sX", lx), strips_done]
        if epi_state["aa"] > 0:
            wz.append(("sAC", epi_state["aa"]))
        vz = p.inc("sDV")
        p.add("ve", lambda e, tt=tt, tpar=tpar: e.tensor_add(
            zbuf[:, :], (xch if out_inc == "sSP" else x1ch)[tpar][:, :], stgv[:, tt, :]),
            wz, ("sDV", 1))
        aa = p.inc("sAC", 2)
        epi_state["aa"] = aa
        wacc = [("sDV", vz)]
        if out_inc == "sOUT" and epi_state["outdma"] > 0:
            wacc.append(("sOUT", epi_state["outdma"]))
        p.add("ac", lambda e: e.activation(sqb[:, :], zbuf[:, :], AF.Copy,
                                           accum_out=lnsc[:, 0:1]), wacc)
        p.add("ac", lambda e: e.activation(sqb[:, :], zbuf[:, :], AF.Square,
                                           accum_out=lnsc[:, 1:2]), (), ("sAC", 2))
        vmean = p.inc("sDV")
        p.add("ve", lambda e: e.tensor_scalar_mul(lnsc[:, 2:3], lnsc[:, 0:1], 1.0 / 4096),
              [("sAC", aa)], ("sDV", 1))
        am = p.inc("sAC", 2)
        p.add("ac", lambda e: e.activation(lnsc[:, 3:4], lnsc[:, 0:1], AF.Square,
                                           scale=1.0 / 4096))
        p.add("ac", lambda e: e.activation(lnsc[:, 4:5], lnsc[:, 1:2], AF.Copy,
                                           scale=1.0 / 4096), (), ("sAC", 2))
        vv = p.inc("sDV")
        p.add("ve", lambda e: e.tensor_sub(lnsc[:, 5:6], lnsc[:, 4:5], lnsc[:, 3:4]),
              [("sAC", am)], ("sDV", 1))
        ar = p.inc("sAC", 2)
        p.add("ac", lambda e: e.activation(lnsc[:, 6:7], lnsc[:, 5:6], AF.Ln, bias=epsb[:, 0:1]),
              [("sDV", vv)])
        p.add("ac", lambda e: e.activation(lnsc[:, 7:8], lnsc[:, 6:7], AF.Exp, scale=-0.5),
              (), ("sAC", 2))
        vo = p.inc("sDV")
        p.add("ve", out_emit, [("sAC", ar), ("sDV", vmean)], ("sDV", 1))
        nv = p.inc(out_inc, 16)
        if out_inc == "sOUT":
            epi_state["outdma"] = nv
        return vo

    pe_prev = None
    epi_done = None
    for hf in range(2):
        w = []
        if epi_done is not None:
            w.append(epi_done)
        for ht in range(H):
            p.inc("sX", 16)
            p.add("sy", lambda e, hf=hf, ht=ht: e.dma_start_transpose(
                out=yTv[:, ht, :],
                in_=y_sp[hf * THALF:(hf + 1) * THALF, ht * 128:(ht + 1) * 128]),
                w if ht == 0 else (), ("sX", 16))
        ldy = p.val("sX")
        wl = {}
        scl = {}
        pe_t = {}
        ev_t = {}
        for s in range(16):
            par = s % 2
            w = []
            if s >= 2:
                w.append(("sPE", pe_t[(s - 2, 7)]))
            elif pe_prev is not None:
                w.append(("sPE", pe_prev[(14 + s, 7)]))
            wl[s] = p.inc("sW", 16)
            p.add("gp", lambda e, s=s, par=par: e.dma_start(
                out=ap3(nstrip[par], CT), in_=wotiles[s]), w, ("sW", 16))
            scl[s] = p.inc("sDV")
            p.add("ve", lambda e, par=par: e.tensor_scalar_mul(
                nstrip[par][:, :], nstrip[par][:, :], gnorm[:, 0:1]),
                [("sW", wl[s])], ("sDV", 1))
            for tt in range(8):
                bk = (s * 8 + tt) % 2
                w = [("sDV", scl[s]), ("sX", ldy)]
                prev = (s, tt - 2) if tt >= 2 else ((s - 1, tt + 6) if s >= 1 else None)
                if prev in ev_t:
                    w.append(("sAC", ev_t[prev]))

                def mkp(par=par, tt=tt, bk=bk):
                    def f(e):
                        sv = ap3(nstrip[par], CT)
                        ins = None
                        for ht in range(H):
                            ins = e.matmul(bank[bk][:, 0:256],
                                           yTv[:, ht, tt * 128:(tt + 1) * 128],
                                           sv[:, ht, :], start=(ht == 0), stop=(ht == H - 1))
                        return ins
                    return f
                pe_t[(s, tt)] = p.inc("sPE")
                p.add("te", mkp(), w, ("sPE", 1))
                ev_t[(s, tt)] = p.inc("sAC")
                p.add("ac", lambda e, s=s, tt=tt, bk=bk: e.copy(
                    stgv[:, tt, s * 256:(s + 1) * 256], bank[bk][:, 0:256]),
                    [("sPE", pe_t[(s, tt)])], ("sAC", 1))
        pe_prev = pe_t
        strips_done = ("sAC", p.val("sAC"))
        for tt in range(8):
            tpar = tt % 2
            vo = ln_epilogue(
                hf, tt, tpar, strips_done,
                residual_load=lambda e, hf=hf, tt=tt, tpar=tpar: e.dma_start(
                    out=xch[tpar][:, :],
                    in_=xT[hf * THALF + tt * 128:hf * THALF + (tt + 1) * 128, :]),
                out_emit=lambda e, tt=tt: e.tensor_scalar(
                    stgv[:, tt, :], zbuf[:, :], lnsc[:, 2:3], lnsc[:, 7:8],
                    ALU.subtract, ALU.mult),
                out_inc="sSP")
            p.add("sy", lambda e, hf=hf, tt=tt: e.dma_start(
                out=x1_sp[hf * THALF + tt * 128:hf * THALF + (tt + 1) * 128, :],
                in_=stgv[:, tt, :]), [("sDV", vo)], ("sSP", 16))
        epi_done = ("sSP", p.val("sSP"))
    p.barrier()
    if phases <= 5:
        return _finish(nc, p, dbg, scr, dumps)

    x1Fv = ap3(yT, CT)
    vs = 0
    for nn in range(16):
        par = nn % 2
        w = [("sDV", vs)] if nn >= 1 else []
        lx = p.inc("sX", 16)
        p.add("sy", lambda e, nn=nn, par=par: e.dma_start(
            out=lstg[par][:, :], in_=loraAT[nn]), w, ("sX", 16))
        wv = [("sX", lx)]
        if nn >= 1:
            wv.append(("sDV", vs))
        vs = p.inc("sDV")
        if nn == 0:
            p.add("ve", lambda e, par=par: e.tensor_scalar_mul(
                AeT32[:, :], lstg[par][:, :], gateb[:, 0:1]), wv, ("sDV", 1))
        else:
            p.add("ve", lambda e, nn=nn, par=par: e.scalar_tensor_tensor(
                AeT32[:, :], lstg[par][:, :], gateb[:, nn:nn + 1], AeT32[:, :],
                ALU.mult, ALU.add), wv, ("sDV", 1))
    vAe = p.inc("sDV")
    p.add("ve", lambda e: e.tensor_copy(AeT[:, :], AeT32[:, :]), [("sDV", vs)], ("sDV", 1))
    vAct = {CT - 1: vAe}
    vb = 0
    for nn in range(16):
        lx = p.inc("sX", 16)
        w = [("sDV", vb)] if vb else []
        p.add("sy", lambda e, nn=nn: e.dma_start(out=bstg[:, :], in_=loraBT[nn]),
              w, ("sX", 16))
        vbprev = vb
        vb = p.inc("sDV")
        if nn == 0:
            p.add("ve", lambda e: e.tensor_scalar_mul(BeT[:, :], bstg[:, :], gateb[0:32, 0:1]),
                  [("sX", lx)], ("sDV", 1))
        else:
            p.add("ve", lambda e, nn=nn: e.scalar_tensor_tensor(
                BeT[:, :], bstg[:, :], gateb[0:32, nn:nn + 1], BeT[:, :], ALU.mult, ALU.add),
                [("sX", lx), ("sDV", vbprev)], ("sDV", 1))
    vbe = p.inc("sDV")
    p.add("ve", lambda e: e.tensor_copy(BeTb[:, :], BeT[:, :]), [("sDV", vb)], ("sDV", 1))
    lsh = p.inc("sW", 32)
    p.add("gp", lambda e: e.dma_start(out=ap3(AsT, CT),
                                      in_=lorasAT.rearrange("ct p r -> p ct r")),
          (), ("sW", 16))
    p.add("gp", lambda e: e.dma_start(out=BsTb[0:32, :], in_=lorasBT), (), ("sW", 16))
    vz1 = p.inc("sDV")
    p.add("ve", lambda e: e.memset(BsTb[32:64, :], 0.0), (), ("sDV", 1))
    lfb = p.inc("sW", 16)
    p.add("gp", lambda e: e.dma_start(out=BsTb[32:33, :], in_=ffnbias_d),
          [("sDV", vz1)], ("sW", 16))
    pre_dv = p.val("sDV")

    pe_prev = None
    epi_done = None
    for hf in range(2):
        w = [("sDV", pre_dv)]
        if epi_done is not None:
            w.append(epi_done)
        for ctb in range(CT):
            p.inc("sX", 16)
            p.add("sy", lambda e, hf=hf, ctb=ctb: e.dma_start_transpose(
                out=x1Fv[:, ctb, :],
                in_=x1_sp[hf * THALF:(hf + 1) * THALF, ctb * 128:(ctb + 1) * 128]),
                w if ctb == 0 else (), ("sX", 16))
        ldx = p.val("sX")
        aax = {}
        for blk in range(2):
            w = [("sX", ldx)]
            if blk == 0:
                w += [("sDV", vAct[CT - 1]), ("sW", lsh)]
            else:
                w.append(("sAC", aax[0]))

            def mkax(blk=blk):
                def f(e):
                    for ct in range(CT):
                        e.matmul(bank[2][0:32, 0:512], AeT[:, ct * 32:(ct + 1) * 32],
                                 x1Fv[:, ct, blk * 512:(blk + 1) * 512],
                                 start=(ct == 0), stop=(ct == CT - 1))
                    ins = None
                    for ct in range(CT):
                        ins = e.matmul(bank[3][0:32, 0:512], AsT[:, ct * 32:(ct + 1) * 32],
                                       x1Fv[:, ct, blk * 512:(blk + 1) * 512],
                                       start=(ct == 0), stop=(ct == CT - 1))
                    return ins
                return f
            tax = p.inc("sPE")
            p.add("te", mkax(), w, ("sPE", 1))
            aax[blk] = p.inc("sAC", 2)
            p.add("ac", lambda e, blk=blk: e.copy(AxTb[:, blk * 512:(blk + 1) * 512],
                                                  bank[2][0:32, 0:512]), [("sPE", tax)])
            p.add("ac", lambda e, blk=blk: e.copy(AxsTb[0:32, blk * 512:(blk + 1) * 512],
                                                  bank[3][0:32, 0:512]), (), ("sAC", 2))
        p.inc("sDV")
        p.add("ve", lambda e: e.memset(AxsTb[32:64, :], 0.0), [("sAC", aax[1])], ("sDV", 1))
        vones = p.inc("sDV")
        p.add("ve", lambda e: e.memset(AxsTb[32:33, :], 1.0), (), ("sDV", 1))
        wl = {}
        pe_t = {}
        ev_t = {}
        for s in range(16):
            par = s % 2
            w = []
            if s >= 2:
                w.append(("sPE", pe_t[(s - 2, 7)]))
            elif pe_prev is not None:
                w.append(("sPE", pe_prev[(14 + s, 7)]))
            wl[s] = p.inc("sW", 16)
            p.add("gp", lambda e, s=s, par=par: e.dma_start(
                out=ap3(nstrip[par], CT), in_=kerntiles[s]), w, ("sW", 16))
            for tt in range(8):
                bk = (s * 8 + tt) % 2
                w = [("sW", wl[s]), ("sX", ldx), ("sDV", vones), ("sDV", vbe)]
                prev = (s, tt - 2) if tt >= 2 else ((s - 1, tt + 6) if s >= 1 else None)
                if prev in ev_t:
                    w.append(("sAC", ev_t[prev]))

                def mkp(par=par, tt=tt, bk=bk, s=s):
                    def f(e):
                        sv = ap3(nstrip[par], CT)
                        for ct in range(CT):
                            e.matmul(bank[bk][:, 0:256], x1Fv[:, ct, tt * 128:(tt + 1) * 128],
                                     sv[:, ct, :], start=(ct == 0), stop=False)
                        e.matmul(bank[bk][:, 0:256], AxTb[0:32, tt * 128:(tt + 1) * 128],
                                 BeTb[0:32, s * 256:(s + 1) * 256], start=False, stop=False)
                        return e.matmul(bank[bk][:, 0:256],
                                        AxsTb[0:64, tt * 128:(tt + 1) * 128],
                                        BsTb[0:64, s * 256:(s + 1) * 256],
                                        start=False, stop=True)
                    return f
                pe_t[(s, tt)] = p.inc("sPE")
                p.add("te", mkp(), w, ("sPE", 1))
                ev_t[(s, tt)] = p.inc("sAC")
                p.add("ac", lambda e, s=s, tt=tt, bk=bk: e.copy(
                    stgv[:, tt, s * 256:(s + 1) * 256], bank[bk][:, 0:256]),
                    [("sPE", pe_t[(s, tt)])], ("sAC", 1))
        pe_prev = pe_t
        strips_done = ("sAC", p.val("sAC"))
        for tt in range(8):
            tpar = tt % 2
            vo = ln_epilogue(
                hf, tt, tpar, strips_done,
                residual_load=lambda e, hf=hf, tt=tt, tpar=tpar: e.dma_start(
                    out=x1ch[tpar][:, :],
                    in_=x1_sp[hf * THALF + tt * 128:hf * THALF + (tt + 1) * 128, :]),
                out_emit=lambda e: e.tensor_scalar(
                    sqb[:, :], zbuf[:, :], lnsc[:, 2:3], lnsc[:, 7:8],
                    ALU.subtract, ALU.mult),
                out_inc="sOUT")
            p.add("gp", lambda e, hf=hf, tt=tt: e.dma_start(
                out=y_out[hf * THALF + tt * 128:hf * THALF + (tt + 1) * 128, :],
                in_=sqb[:, :]), [("sDV", vo)], ("sOUT", 16))
        epi_done = ("sOUT", p.val("sOUT"))

    return _finish(nc, p, dbg, scr, dumps)


def _finish(nc, p, dbg, scr, dumps):
    for name in dumps:
        wv = [("sSP", p.val("sSP")), ("sDV", p.val("sDV")), ("sAC", p.val("sAC")),
              ("sPE", p.val("sPE")), ("sOUT", p.val("sOUT"))]
        p.inc("sOUT", 16)
        p.add("gp", lambda e, name=name: e.dma_start(out=dbg[name], in_=scr[name]),
              wv, ("sOUT", 16))
    tgt = p.val("sOUT")
    if tgt > 0:
        def fin(e, tgt=tgt):
            e.wait_ge(p.sems["sOUT"], tgt)
        fin.meta = ("gp", (("sOUT", tgt),), None)
        p.q["gp"].append(fin)

    with nc.Block() as block:
        block.tensor(lambda e: [f(e) for f in p.q["te"]])
        block.vector(lambda e: [f(e) for f in p.q["ve"]])
        block.scalar(lambda e: [f(e) for f in p.q["ac"]])
        block.gpsimd(lambda e: [f(e) for f in p.q["gp"]])
        block.sync(lambda e: [f(e) for f in p.q["sy"]])
    return nc



def _host_inputs(inputs, core):
    x = np.asarray(inputs["L0_attn_hidden_states"], dtype=np.float32).reshape(16384, 4096)
    sl = slice(core * T, (core + 1) * T)
    xTc = np.ascontiguousarray(x[sl])
    xFc = np.ascontiguousarray(xTc.T)

    Wq = np.asarray(inputs["L0_attn_Wq"], np.float32)
    Wk = np.asarray(inputs["L0_attn_Wk"], np.float32)
    Wv = np.asarray(inputs["L0_attn_Wv"], np.float32)
    Wg = np.asarray(inputs["L0_attn_Wg"], np.float32)
    Win = np.asarray(inputs["L0_attn_Win"], np.float32)
    Wcat = np.concatenate([Wq, Wk, Wv, Wg, Win], axis=0)
    WcatT = np.zeros((4096, NSTRIP1 * 512), np.float32)
    WcatT[:, :10272] = Wcat.T
    wtiles = np.ascontiguousarray(
        WcatT.reshape(CT, 128, NSTRIP1, 512).transpose(2, 1, 0, 3))

    WoT = np.ascontiguousarray(np.asarray(inputs["L0_attn_Wo"], np.float32).T)
    wotiles = np.ascontiguousarray(WoT.reshape(CT, 128, 16, 256).transpose(2, 1, 0, 3))
    kern = np.asarray(inputs["L0_ffn_kernel"], np.float32)
    kerntiles = np.ascontiguousarray(kern.reshape(CT, 128, 16, 256).transpose(2, 1, 0, 3))

    loraAT = np.ascontiguousarray(
        np.asarray(inputs["L0_ffn_lora_A"], np.float32).transpose(0, 2, 1)
        .reshape(16, CT, 128, 32).transpose(0, 2, 1, 3).reshape(16, 128, CT * 32))
    loraBT = np.ascontiguousarray(
        np.asarray(inputs["L0_ffn_lora_B"], np.float32).transpose(0, 2, 1))
    lorasAT = np.ascontiguousarray(
        np.asarray(inputs["L0_ffn_lora_A_shared"], np.float32).T.reshape(CT, 128, 32))
    lorasBT = np.ascontiguousarray(np.asarray(inputs["L0_ffn_lora_B_shared"], np.float32).T)

    pos = (core % 4) * T + np.arange(T, dtype=np.float64)
    invf = (1.0 / 10000.0) ** (np.arange(0, 128, 2, dtype=np.float64) / 128.0)
    ang = pos[:, None] * invf[None, :]
    cos128 = np.concatenate([np.cos(ang), np.cos(ang)], axis=1).astype(np.float32)
    sin64 = np.sin(ang).astype(np.float32)
    sinS = np.concatenate([-sin64, sin64], axis=1)
    cosrep = np.ascontiguousarray(np.tile(cos128, (1, 4))).astype(ml_dtypes.bfloat16)
    sinSrep = np.ascontiguousarray(np.tile(sinS, (1, 4))).astype(ml_dtypes.bfloat16)

    identf = np.eye(128, dtype=np.float32)
    mnegf = np.tril(np.full((128, 128), MNEG_VAL, np.float32), -1)
    utf = np.triu(np.ones((128, 128), np.float32))

    dtbias_row = np.zeros((1, 512), np.float32)
    dtbias_row[0, :32] = np.asarray(inputs["L0_attn_dt_bias"], np.float32)
    alog_row = np.asarray(inputs["L0_attn_A_log_bias"], np.float32).reshape(1, 32)
    ffnbias_row = np.asarray(inputs["L0_ffn_bias"], np.float32).reshape(1, 4096)
    gate_row = np.asarray(inputs["L0_ffn_gate"], np.float32)[core].reshape(1, 16)
    gnorm_col = np.asarray(inputs["L0_attn_gnorm_w"], np.float32).reshape(128, 1)

    posn = core % 4
    am = np.zeros((128, 3), np.float32)
    bm = np.zeros((128, 96), np.float32)
    for j in range(3):
        a = 1.0 if j < posn else 0.0
        am[:, j] = a
        bm[:, j * 32:(j + 1) * 32] = 1.0 - a

    return dict(xF=xFc, xT=xTc, wtiles=wtiles, wotiles=wotiles, kerntiles=kerntiles,
                loraAT=loraAT, loraBT=loraBT, lorasAT=lorasAT, lorasBT=lorasBT,
                cosrep=cosrep, sinSrep=sinSrep, identf=identf, mnegf=mnegf, utf=utf,
                dtbias_row=dtbias_row, alog_row=alog_row, ffnbias_row=ffnbias_row,
                gate_row=gate_row, gnorm_col=gnorm_col, amask=am, bmask_bc=bm)


_RESULTS = None


def kernel(**inputs):
    global _RESULTS
    nc = build_nc(dumps=_DUMPS, phases=_PHASES)
    in_maps = [_host_inputs(inputs, c) for c in range(8)]
    res = run_bass_kernel_spmd(nc, in_maps, core_ids=list(range(8)))
    _RESULTS = res
    out = np.concatenate([res.results[c]["y_out"] for c in range(8)], axis=0)
    return out.reshape(2, 8192, 4096)
